# version 01
# baseline (speedup 1.0000x reference)
import os
import numpy as np
import ml_dtypes
import concourse.bass as bass
import concourse.mybir as mybir
from concourse.bass_utils import run_bass_kernel_spmd
from contextlib import ExitStack

F32 = mybir.dt.float32
BF16 = mybir.dt.bfloat16
AF_ = mybir.ActivationFunctionType
ALU = mybir.AluOpType
AXX = mybir.AxisListType.X
BFNP = ml_dtypes.bfloat16

B, T, D, H, DH, DL, FFD, RLOW = 2, 8192, 2048, 16, 128, 64, 8192, 32
NCORE = 8
TPC = 2048
NQB = 1024
DC = 16
NKT = 64
SCALE = float(1.0 / np.sqrt(128.0))
EPS = 1e-5

DEBUG = False
ENGINES = ("sync", "tensor", "vector", "scalar", "gpsimd")
DMA, DMB, PE, SE, VE = "dma", "dmb", "pe", "se", "ve"


class Plan:
    def __init__(self):
        self.ops = {e: [] for e in ENGINES}
        self.counts = {PE: 0, SE: 0, VE: 0}
        self._barrier = {}

    def op(self, engine, fn, waits=(), inc=None, amt=1):
        w = list(waits)
        if self._barrier.get(engine):
            w = list(self._barrier[engine]) + w
            self._barrier[engine] = None
        if engine in ("vector", "scalar") and inc is not None:
            prev = self.counts.get(inc, 0)
            if prev:
                w.append((inc, prev))
        if inc is not None:
            self.counts[inc] = self.counts.get(inc, 0) + amt
        self.ops[engine].append((fn, tuple(w), inc, amt))
        return self.counts.get(inc) if inc is not None else None

    def barrier(self):
        snap = [(s, c) for s, c in self.counts.items() if c > 0]
        for e in ENGINES:
            self._barrier[e] = snap

    def emit(self, block, sems):
        for ename in ENGINES:
            ops = self.ops[ename]
            if not ops:
                continue
            dec = getattr(block, ename)

            def section(eng, _ops=ops):
                waited = {}
                for fn, waits, inc, amt in _ops:
                    for sname, val in waits:
                        if val and val > 0 and waited.get(sname, 0) < val:
                            eng.wait_ge(sems[sname], val)
                            waited[sname] = val
                    ins = fn(eng)
                    if inc is not None:
                        ins.then_inc(sems[inc], amt)
            dec(section)


def slabs_for(j):
    qs = 128 * (j // 8)
    out = []
    if qs < 512:
        out.append((qs, 512 - qs))
    if 1024 - max(qs, 512) > 0:
        out.append((max(qs, 512), 1024 - max(qs, 512)))
    return out


AB_H1 = 0
AB_PANEL = 32768
AB_EVAC = 40960
AB_WDN = 45056
AB_WUP = 45568
AB_LR2E = 43008
AB_KSB = 0
AB_PROBS = 32768
AB_PRES = 36864
AB_VR = 38912
AB_Q = 39936
AB_SB1 = 44032
AB_YS = 46080
AB_CTX = 0
AB_SLAB = 32768
AB_G = 0
AB_XN2 = 32768
AB_FSLAB = 40960
AB_SBC = 43008
AB_LR2T = 44032
AB_KVPAN = 32768
AB_KEXP = 40960
AB_RQT = 41984
AB_RING = 44032
AB_QME = 46080
AB_SQ2 = 47104
AB_MYE = 45056
AB_KLAT = 0
AB_VLT = 8192
AB_SQK = 16384
AB_VT = 24576
AB_QM = 28736
AB_YN = 30784
AB_VEXP = 31808

AF_IO = 0
AF_MUB = 4096
AF_RSB = 6144
AF_MU2B = 8192
AF_RS2B = 10240
AF_MISC = 12288


class Builder:
    def __init__(self, nc):
        self.nc = nc
        self.P = Plan()
        self.war = {}
        self.ccs = ["cc0", "cc1", "cc2"]

    def need(self, key):
        v = self.war.get(key)
        return list(v.items()) if v else []

    def done(self, key, sem, count):
        d = self.war.setdefault(key, {})
        if count and count > d.get(sem, 0):
            d[sem] = count

    def dma(self, q, dst, src, waits=(), stream="misc"):
        sem = "D" + stream
        c = self.P.op(q, lambda e, d=dst, s=src: e.dma_start(out=d, in_=s),
                      waits=waits, inc=sem, amt=16)
        return (sem, c)

    def build(self):
        nc = self.nc
        ap = lambda name, shape, dt, kind="ExternalInput": nc.dram_tensor(name, shape, dt, kind=kind).ap()
        self.x0 = ap("x0", [D, TPC], F32)
        self.out = ap("out", [D, TPC], F32, "ExternalOutput")
        self.w_in = {}
        for name, shape in [
            ("ffn_wqT", [D, D]), ("ffn_wkT", [D, D]), ("ffn_wvT", [D, D]), ("ffn_woT", [D, D]),
            ("ff1T", [D, FFD]), ("ff2T", [FFD, D]),
            ("wdownT", [D, RLOW]), ("wupT", [RLOW, D]),
            ("attn_wq", [D, D]), ("attn_proj", [D, D]), ("wkvlat", [D, 2 * DL]),
            ("kexpT", [H * DH, DL]), ("vexp", [H * DL, DH]),
        ]:
            self.w_in[name] = ap(name, shape, F32)
        self.wrouter_in = ap("wrouter_sb", [128, DC * 3], F32)
        self.vec_in = {n: ap(n, [128, DC], F32)
                       for n in ("bias_v", "ln1s_v", "ln1b_v", "ln2s_v", "ln2b_v")}
        self.ident_f_in = ap("ident_f", [128, 128], F32)
        self.ident_b_in = ap("ident_b", [128, 128], BF16)
        self.masks_in = ap("masks_i", [8, 128, 128], BF16)
        self.rq_cos_in = ap("rq_cos", [TPC, 256], F32)
        self.rq_sin_in = ap("rq_sin", [TPC, 256], F32)
        self.rk_cos_in = ap("rk_cos", [TPC, 32], F32)
        self.rk_sin_in = ap("rk_sin", [TPC, 32], F32)

        dt_ = lambda name, shape, dt, **kw: nc.dram_tensor(name, shape, dt, kind="Internal", **kw).ap()
        self.wbf = {n: dt_("bf_" + n, list(t.shape), BF16) for n, t in self.w_in.items()}
        self.x_a = dt_("x_a", [D, TPC], F32)
        self.x_b = dt_("x_b", [D, TPC], F32)
        self.xt_scr = dt_("xt_scr", [D, TPC], F32)
        self.lr2_scr = dt_("lr2_scr", [D, TPC], BF16)
        self.q_scr = dt_("q_scr", [B * H * 128, NQB], BF16)
        self.qm_scr = dt_("qm_scr", [B * H * DL, NQB], BF16)
        self.ctx_scr = dt_("ctx_scr", [D, TPC], BF16)
        self.CF_K, self.CF_KN, self.CF_V, self.CF_ROWS = 0, 2048, 2064, 4112
        self.contribF = dt_("contribF", [self.CF_ROWS, TPC], BF16)
        self.gathF = dt_("gathF", [NCORE * self.CF_ROWS, TPC], BF16, addr_space="Shared")
        self.contribM = dt_("contribM", [128, TPC], BF16)
        self.gathM = dt_("gathM", [NCORE * 128, TPC], BF16, addr_space="Shared")
        self.dbg = {}
        self.dbg_out = {}
        if DEBUG:
            for n, shape, dt2 in [("x1", [D, TPC], F32), ("xt", [D, TPC], F32),
                                  ("ctx", [D, TPC], BF16), ("q", [B * H * 128, NQB], BF16),
                                  ("contrib", [self.CF_ROWS, TPC], BF16),
                                  ("lr2", [D, TPC], BF16), ("smask", [1, 2 * TPC], BF16),
                                  ("x2", [D, TPC], F32), ("ctxm", [D, TPC], BF16),
                                  ("qm", [B * H * DL, NQB], BF16), ("contribm", [128, TPC], BF16)]:
                self.dbg[n] = dt_("dbg_" + n, shape, dt2)
                self.dbg_out[n] = ap("dout_" + n, shape, dt2, "ExternalOutput")

        with ExitStack() as st:
            E = st.enter_context
            sb = lambda name, shape, dt: E(nc.sbuf_tensor(name, shape, dt))
            self.masks = sb("masks", [128, 8 * 128], BF16)
            self.idf = sb("idf", [128, 128], F32)
            self.idb = sb("idb", [128, 128], BF16)
            self.ones_b = sb("ones_b", [128, 1], BF16)
            self.ones_b1 = sb("ones_b1", [1, 128], BF16)
            self.ones_f1 = sb("ones_f1", [1, 128], F32)
            self.ones_f128 = sb("ones_f128", [128, 1], F32)
            self.vec_sb = {n: sb("sb_" + n, [128, DC], F32) for n in self.vec_in}
            self.wrouter = sb("wrouter", [128, DC * 3], F32)
            self.statn = sb("statn", [1, 2 * TPC], F32)
            self.smask = sb("smask", [1, 2 * TPC], BF16)
            self.rq2 = sb("rq2", [1, 64], F32)
            self.rq2m = sb("rq2m", [1, 64], F32)
            self.rk2m = sb("rk2m", [1, 4], F32)
            self.mh = sb("mh", [128, 4], F32)
            self.tiny = sb("tiny", [1, 64], F32)
            self.epsv = sb("epsv", [1, 1], F32)
            self.AB = sb("AB", [128, 49152], BF16)
            self.AF = sb("AF", [128, 14336], F32)
            self.psum = E(nc.psum_tensor("psum", [128, 3584], F32))
            self.psb = E(nc.psum_tensor("psb", [128, 1024], BF16))

            self.emit_all()

            sems = {}
            for name in sorted(self.P.counts):
                sems[name] = E(nc.semaphore(name.replace(":", "_")))
            block = E(nc.Block())
            self.P.emit(block, sems)
        return nc

    def tap(self, name, src):
        if not DEBUG or name not in self.dbg:
            return
        w = [(s, c) for s, c in self.P.counts.items() if c > 0]
        d = self.dma("gpsimd", self.dbg[name], src, waits=w, stream="tap")
        self.dma("gpsimd", self.dbg_out[name], self.dbg[name], waits=[d], stream="tap")

    class _Stop(Exception):
        pass

    def ck(self):
        self._phase += 1
        if self._phase > int(os.environ.get("KSTOP", "99")):
            raise Builder._Stop()

    def emit_all(self):
        self._phase = 0
        try:
            if os.environ.get("KONLY") == "lat":
                P = self.P
                self.load_consts()
                self.prep_weights()
                P.barrier()
                self.stats_pass(self.x0)
                P.barrier()
                self.finalize_stats(AF_MUB, AF_RSB)
                P.barrier()
                self.h_pass(self.x0, router=False)
                P.barrier()
                self.mla_latents()
                raise Builder._Stop()
            self._emit_all()
        except Builder._Stop:
            pass
        fin = [(s, c) for s, c in self.P.counts.items() if c > 0]
        self.P.op("gpsimd", lambda g: g.engine_nop(), waits=fin)

    def _emit_all(self):
        P = self.P
        self.load_consts()
        self.ck()
        self.prep_weights()
        P.barrier()
        self.ck()
        self.stats_pass(self.x0)
        P.barrier()
        self.finalize_stats(AF_MUB, AF_RSB)
        P.barrier()
        self.ck()
        self.ffn_sublayer(self.x0, self.x_a, final=False)
        self.tap("x1", self.x_a)
        P.barrier()
        self.finalize_stats(AF_MUB, AF_RSB)
        P.barrier()
        self.ck()
        self.mla_sublayer(self.x_a, self.x_b)
        self.tap("x2", self.x_b)
        P.barrier()
        self.finalize_stats(AF_MUB, AF_RSB)
        P.barrier()
        self.ck()
        self.ffn_sublayer(self.x_b, self.out, final=True)

    def load_consts(self):
        for jo in range(8):
            self.dma("sync", self.masks[:, jo * 128:(jo + 1) * 128], self.masks_in[jo], stream="consts")
        self.dma("sync", self.idf[:, :], self.ident_f_in, stream="consts")
        self.dma("sync", self.idb[:, :], self.ident_b_in, stream="consts")
        for n in self.vec_in:
            self.dma("sync", self.vec_sb[n][:, :], self.vec_in[n], stream="consts")
        self.dma("sync", self.wrouter[:, :], self.wrouter_in, stream="consts")
        P = self.P
        P.op("vector", lambda v: v.memset(self.ones_b[:], 1.0), inc=VE)
        P.op("vector", lambda v: v.memset(self.ones_b1[:], 1.0), inc=VE)
        P.op("vector", lambda v: v.memset(self.ones_f1[:], 1.0), inc=VE)
        P.op("vector", lambda v: v.memset(self.ones_f128[:], 1.0), inc=VE)
        P.op("vector", lambda v: v.memset(self.epsv[:], float(EPS)), inc=VE)

    def prep_weights(self):
        P = self.P
        par = 0
        for name, src in self.w_in.items():
            dst = self.wbf[name]
            rows, cols = src.shape
            for ch in range((rows + 127) // 128):
                r0 = ch * 128
                pr = min(128, rows - r0)
                for c0 in range(0, cols, 4096):
                    w = min(4096, cols - c0)
                    p = par % 2
                    par += 1
                    fo = p * 4096
                    bo = AB_PANEL + p * 4096
                    d = self.dma("sync", self.AF[0:pr, fo:fo + w], src[r0:r0 + pr, c0:c0 + w],
                                 waits=self.need(("wp", p)), stream="wpi%d" % p)
                    v = P.op("vector", lambda e, pr_=pr, f_=fo, b_=bo, w_=w:
                             e.tensor_copy(out=self.AB[0:pr_, b_:b_ + w_], in_=self.AF[0:pr_, f_:f_ + w_]),
                             waits=[d], inc=VE)
                    d2 = self.dma("gpsimd", dst[r0:r0 + pr, c0:c0 + w], self.AB[0:pr, bo:bo + w],
                                  waits=[(VE, v)], stream="wpo%d" % p)
                    self.done(("wp", p), *d2)

    def stats_pass(self, xsrc):
        P = self.P
        for tg in range(4):
            m2 = None
            for kc in range(DC):
                p = kc % 2
                fo = AF_IO + p * 512
                d = self.dma("sync", self.AF[:, fo:fo + 512],
                             xsrc[kc * 128:(kc + 1) * 128, tg * 512:(tg + 1) * 512],
                             waits=self.need(("st", p)), stream="stx%d" % p)
                s = P.op("scalar", lambda e, f_=fo: e.activation(
                    out=self.AF[:, f_ + 1024:f_ + 1536], in_=self.AF[:, f_:f_ + 512], func=AF_.Square),
                    waits=[d], inc=SE)
                P.op("tensor", lambda t, f_=fo, first=(kc == 0), last=(kc == DC - 1): t.matmul(
                    self.psum[0:1, 0:512], self.ones_f128[:, :], self.AF[:, f_:f_ + 512],
                    start=first, stop=last), waits=[d], inc=PE)
                m2 = P.op("tensor", lambda t, f_=fo, first=(kc == 0), last=(kc == DC - 1): t.matmul(
                    self.psum[0:1, 512:1024], self.ones_f128[:, :], self.AF[:, f_ + 1024:f_ + 1536],
                    start=first, stop=last), waits=[(SE, s)], inc=PE)
                self.done(("st", p), PE, m2)
            v = P.op("vector", lambda e, tg_=tg: e.tensor_copy(
                out=self.statn[:, tg_ * 512:tg_ * 512 + 512], in_=self.psum[0:1, 0:512]),
                waits=[(PE, m2)], inc=VE)
            v = P.op("vector", lambda e, tg_=tg: e.tensor_copy(
                out=self.statn[:, TPC + tg_ * 512:TPC + tg_ * 512 + 512], in_=self.psum[0:1, 512:1024]),
                waits=[(PE, m2)], inc=VE)
            self.done(("st", 0), VE, v)
            self.done(("st", 1), VE, v)

    def finalize_stats(self, mub_col, rsb_col):
        P = self.P
        v = P.op("vector", lambda e: e.tensor_scalar(out=self.statn[:, :], in0=self.statn[:, :],
                                                     scalar1=float(1.0 / D), scalar2=None, op0=ALU.mult),
                 inc=VE)
        v = P.op("vector", lambda e: e.tensor_tensor(out=self.AF[0:1, AF_MISC:AF_MISC + TPC],
                                                     in0=self.statn[:, 0:TPC], in1=self.statn[:, 0:TPC],
                                                     op=ALU.mult), waits=[(VE, v)], inc=VE)
        v = P.op("vector", lambda e: e.tensor_tensor(out=self.statn[:, TPC:2 * TPC],
                                                     in0=self.statn[:, TPC:2 * TPC],
                                                     in1=self.AF[0:1, AF_MISC:AF_MISC + TPC],
                                                     op=ALU.subtract), waits=[(VE, v)], inc=VE)
        s = P.op("scalar", lambda e: e.activation(out=self.AF[0:1, AF_MISC:AF_MISC + TPC],
                                                  in_=self.statn[:, TPC:2 * TPC],
                                                  func=AF_.Sqrt, bias=self.epsv[:, :]),
                 waits=[(VE, v)], inc=SE)
        v = P.op("vector", lambda e: e.reciprocal(out=self.statn[:, TPC:2 * TPC],
                                                  in_=self.AF[0:1, AF_MISC:AF_MISC + TPC]),
                 waits=[(SE, s)], inc=VE)
        evs = {}
        for half, col in ((0, mub_col), (1, rsb_col)):
            for i in range(4):
                w = [(VE, v)] + ([(VE, evs[i])] if i in evs else [])
                m = P.op("tensor", lambda t, h_=half, i_=i: t.matmul(
                    self.psum[:, i_ * 512:i_ * 512 + 512], self.ones_f1[:, :],
                    self.statn[:, h_ * TPC + i_ * 512:h_ * TPC + (i_ + 1) * 512],
                    start=True, stop=True), waits=w, inc=PE)
                evs[i] = P.op("vector", lambda e, c_=col, i_=i: e.tensor_copy(
                    out=self.AF[:, c_ + i_ * 512:c_ + i_ * 512 + 512],
                    in_=self.psum[:, i_ * 512:i_ * 512 + 512]), waits=[(PE, m)], inc=VE)

    def ffn_sublayer(self, xsrc, xdst, final):
        P = self.P
        self.h_pass(xsrc, router=True)
        P.barrier()
        self.ck()
        self.ffn_lowrank()
        P.barrier()
        self.tap("lr2", self.lr2_scr)
        self.tap("smask", self.smask[:, :])
        self.ffn_inner_ln()
        P.barrier()
        self.ck()
        self.ffn_qk("ffn_wqT", False)
        P.barrier()
        self.ck()
        self.ffn_qk("ffn_wkT", True)
        P.barrier()
        self.ck()
        self.ffn_v()
        P.barrier()
        self.tap("q", self.q_scr)
        self.tap("contrib", self.contribF)
        self.ck()
        cc = self.ccs.pop(0)
        P.barrier()
        P.op("gpsimd", lambda g: g.collective_compute(
            "AllGather", ALU.bypass, replica_groups=[list(range(NCORE))],
            ins=[self.contribF.opt()], outs=[self.gathF.opt()]), inc=cc, amt=1)
        self.cc_cur = cc
        P.barrier()
        self.ck()
        self.ffn_attn()
        P.barrier()
        self.tap("ctx", self.ctx_scr)
        self.ck()
        self.dense_to_x(self.wbf["ffn_woT"], xsrc, self.xt_scr, kind="wo", fuse_stats=True)
        P.barrier()
        self.tap("xt", self.xt_scr)
        self.finalize_stats(AF_MU2B, AF_RS2B)
        P.barrier()
        self.ck()
        self.ffn_ff_select(xsrc, xdst, final)

    def h_pass(self, xsrc, router):
        P = self.P
        mlast = None
        selast = None
        for kc in range(DC):
            p = kc % 2
            fo = AF_IO + p * 2048
            d = self.dma("sync", self.AF[:, fo:fo + TPC], xsrc[kc * 128:(kc + 1) * 128, :],
                         waits=self.need(("h1", p)), stream="h1x%d" % p)
            v = P.op("vector", lambda e, f_=fo: e.tensor_tensor(
                out=self.AF[:, f_:f_ + TPC], in0=self.AF[:, f_:f_ + TPC],
                in1=self.AF[:, AF_MUB:AF_MUB + TPC], op=ALU.subtract), waits=[d], inc=VE)
            v = P.op("vector", lambda e, f_=fo: e.tensor_tensor(
                out=self.AF[:, f_:f_ + TPC], in0=self.AF[:, f_:f_ + TPC],
                in1=self.AF[:, AF_RSB:AF_RSB + TPC], op=ALU.mult), inc=VE)
            selast = P.op("scalar", lambda e, f_=fo, kc_=kc: e.copy(
                out=self.AB[:, AB_H1 + kc_ * TPC:AB_H1 + (kc_ + 1) * TPC], in_=self.AF[:, f_:f_ + TPC]),
                waits=[(VE, v)], inc=SE)
            if router:
                for tg in range(4):
                    mlast = P.op("tensor", lambda t, f_=fo, kc_=kc, tg_=tg: t.matmul(
                        self.psum[0:3, tg_ * 512:tg_ * 512 + 512],
                        self.wrouter[:, kc_ * 3:(kc_ + 1) * 3],
                        self.AF[:, f_ + tg_ * 512:f_ + tg_ * 512 + 512],
                        start=(kc_ == 0), stop=(kc_ == DC - 1)), waits=[(VE, v)], inc=PE)
                self.done(("h1", p), PE, mlast)
                self.done(("h1", p), SE, selast)
            else:
                self.done(("h1", p), SE, selast)
        self.h1_se = selast
        if not router:
            return
        v = P.op("vector", lambda e: e.tensor_copy(out=self.AF[0:3, AF_IO:AF_IO + TPC],
                                                   in_=self.psum[0:3, 0:TPC]),
                 waits=[(PE, mlast), (SE, selast)], inc=VE)
        d1 = self.dma("gpsimd", self.AF[0:1, AF_IO + TPC:AF_IO + 2 * TPC],
                      self.AF[1:2, AF_IO:AF_IO + TPC], waits=[(VE, v)], stream="rrow")
        d2 = self.dma("gpsimd", self.AF[0:1, AF_MU2B:AF_MU2B + TPC],
                      self.AF[2:3, AF_IO:AF_IO + TPC], waits=[(VE, v)], stream="rrow")
        L0 = self.AF[0:1, AF_IO:AF_IO + TPC]
        L1 = self.AF[0:1, AF_IO + TPC:AF_IO + 2 * TPC]
        L2 = self.AF[0:1, AF_MU2B:AF_MU2B + TPC]
        TA = self.AF[0:1, AF_RS2B:AF_RS2B + TPC]
        TB = self.AF[0:1, AF_MISC:AF_MISC + TPC]
        v = P.op("vector", lambda e: e.tensor_tensor(out=TA, in0=L0, in1=L1, op=ALU.is_ge),
                 waits=[d2], inc=VE)
        v = P.op("vector", lambda e: e.tensor_tensor(out=TB, in0=L0, in1=L2, op=ALU.is_ge), inc=VE)
        v = P.op("vector", lambda e: e.tensor_tensor(out=self.smask[:, 0:TPC], in0=TA, in1=TB,
                                                     op=ALU.mult), inc=VE)
        v = P.op("vector", lambda e: e.tensor_tensor(out=TA, in0=L1, in1=L0, op=ALU.is_gt), inc=VE)
        v = P.op("vector", lambda e: e.tensor_tensor(out=TB, in0=L1, in1=L2, op=ALU.is_ge), inc=VE)
        v = P.op("vector", lambda e: e.tensor_tensor(out=self.smask[:, TPC:2 * TPC], in0=TA, in1=TB,
                                                     op=ALU.mult), inc=VE)
        v = P.op("vector", lambda e: e.tensor_scalar(out=self.smask[:, :], in0=self.smask[:, :],
                                                     scalar1=0.5, scalar2=None, op0=ALU.mult), inc=VE)

    def ffn_lowrank(self):
        P = self.P
        for kc in range(DC):
            self.dma("sync", self.AB[:, AB_WDN + kc * 32:AB_WDN + (kc + 1) * 32],
                     self.wbf["wdownT"][kc * 128:(kc + 1) * 128, :], stream="lrw")
        du = self.dma("sync", self.AB[0:RLOW, AB_WUP:AB_WUP + TPC], self.wbf["wupT"][:, :],
                      stream="lrw")
        m = None
        for tg in range(4):
            for kc in range(DC):
                m = P.op("tensor", lambda t, kc_=kc, tg_=tg: t.matmul(
                    self.psum[0:RLOW, tg_ * 512:tg_ * 512 + 512],
                    self.AB[:, AB_WDN + kc_ * 32:AB_WDN + (kc_ + 1) * 32],
                    self.AB[:, AB_H1 + kc_ * TPC + tg_ * 512:AB_H1 + kc_ * TPC + tg_ * 512 + 512],
                    start=(kc_ == 0), stop=(kc_ == DC - 1)),
                    waits=[du, (SE, self.h1_se)], inc=PE)
        v = P.op("vector", lambda e: e.tensor_copy(out=self.AB[0:RLOW, AB_LR2E:AB_LR2E + TPC],
                                                   in_=self.psum[0:RLOW, 0:TPC]),
                 waits=[(PE, m)], inc=VE)
        for oc in range(DC):
            for tg in range(4):
                i = oc * 4 + tg
                pcol = 2048 + (i % 2) * 512
                ecol = AB_EVAC + (i % 2) * 512
                waits = [(VE, v), du] + self.need(("lrp", i % 2))
                m = P.op("tensor", lambda t, oc_=oc, tg_=tg, pc=pcol: t.matmul(
                    self.psum[:, pc:pc + 512],
                    self.AB[0:RLOW, AB_WUP + oc_ * 128:AB_WUP + (oc_ + 1) * 128],
                    self.AB[0:RLOW, AB_LR2E + tg_ * 512:AB_LR2E + tg_ * 512 + 512],
                    start=True, stop=True), waits=waits, inc=PE)
                s = P.op("scalar", lambda e, pc=pcol, ec=ecol: e.copy(
                    out=self.AB[:, ec:ec + 512], in_=self.psum[:, pc:pc + 512]),
                    waits=[(PE, m)] + self.need(("lre", i % 2)), inc=SE)
                d = self.dma("gpsimd", self.lr2_scr[oc * 128:(oc + 1) * 128, tg * 512:(tg + 1) * 512],
                             self.AB[:, ecol:ecol + 512], waits=[(SE, s)], stream="lre%d" % (i % 2))
                self.done(("lrp", i % 2), SE, s)
                self.done(("lre", i % 2), *d)

    def ffn_inner_ln(self):
        P = self.P
        P.barrier()
        sq = AB_EVAC
        for tg in range(4):
            m2 = None
            sprev = None
            for kc in range(DC):
                p = kc % 2
                sprev = P.op("scalar", lambda e, kc_=kc, tg_=tg, p_=p: e.activation(
                    out=self.AB[:, sq + p_ * 512:sq + p_ * 512 + 512],
                    in_=self.AB[:, AB_H1 + kc_ * TPC + tg_ * 512:AB_H1 + kc_ * TPC + tg_ * 512 + 512],
                    func=AF_.Square), waits=self.need(("iln", p)), inc=SE)
                P.op("tensor", lambda t, kc_=kc, tg_=tg, first=(kc == 0), last=(kc == DC - 1): t.matmul(
                    self.psum[0:1, 2048:2560], self.ones_b[:, :],
                    self.AB[:, AB_H1 + kc_ * TPC + tg_ * 512:AB_H1 + kc_ * TPC + tg_ * 512 + 512],
                    start=first, stop=last), inc=PE)
                m2 = P.op("tensor", lambda t, p_=p, first=(kc == 0), last=(kc == DC - 1): t.matmul(
                    self.psum[0:1, 2560:3072], self.ones_b[:, :],
                    self.AB[:, sq + p_ * 512:sq + p_ * 512 + 512],
                    start=first, stop=last), waits=[(SE, sprev)], inc=PE)
                self.done(("iln", p), PE, m2)
            v = P.op("vector", lambda e, tg_=tg: e.tensor_copy(
                out=self.statn[:, tg_ * 512:tg_ * 512 + 512], in_=self.psum[0:1, 2048:2560]),
                waits=[(PE, m2)], inc=VE)
            v = P.op("vector", lambda e, tg_=tg: e.tensor_copy(
                out=self.statn[:, TPC + tg_ * 512:TPC + tg_ * 512 + 512], in_=self.psum[0:1, 2560:3072]),
                waits=[(PE, m2)], inc=VE)
            self.done(("iln", 0), VE, v)
            self.done(("iln", 1), VE, v)
        P.barrier()
        self.finalize_stats(AF_MU2B, AF_RS2B)
        P.barrier()
        for kc in range(DC):
            p = kc % 2
            fo = AF_IO + p * 2048
            v = P.op("vector", lambda e, kc_=kc, f_=fo: e.tensor_tensor(
                out=self.AF[:, f_:f_ + TPC], in0=self.AB[:, AB_H1 + kc_ * TPC:AB_H1 + (kc_ + 1) * TPC],
                in1=self.AF[:, AF_MU2B:AF_MU2B + TPC], op=ALU.subtract),
                waits=self.need(("ilna", p)), inc=VE)
            v = P.op("vector", lambda e, f_=fo: e.tensor_tensor(
                out=self.AF[:, f_:f_ + TPC], in0=self.AF[:, f_:f_ + TPC],
                in1=self.AF[:, AF_RS2B:AF_RS2B + TPC], op=ALU.mult), inc=VE)
            v = P.op("vector", lambda e, kc_=kc, f_=fo: e.tensor_scalar(
                out=self.AB[:, AB_H1 + kc_ * TPC:AB_H1 + (kc_ + 1) * TPC], in0=self.AF[:, f_:f_ + TPC],
                scalar1=self.vec_sb["ln1s_v"][:, kc_:kc_ + 1], scalar2=self.vec_sb["ln1b_v"][:, kc_:kc_ + 1],
                op0=ALU.mult, op1=ALU.add), inc=VE)
            self.done(("ilna", p), VE, v)
            self.h1_se = max(self.h1_se, 0)
        self.iln_ve = v

    def ffn_qk(self, wname, is_k):
        P = self.P
        for og in range(4):
            dp = None
            for kc in range(DC):
                dp = self.dma("sync", self.AB[:, AB_PANEL + kc * 512:AB_PANEL + (kc + 1) * 512],
                              self.wbf[wname][kc * 128:(kc + 1) * 128, og * 512:(og + 1) * 512],
                              waits=self.need("pan"), stream="pan")
            for oi in range(4):
                h = og * 4 + oi
                m = None
                for kc in range(DC):
                    for tg in range(4):
                        waits = [dp, (SE, self.h1_se), (VE, self.iln_ve)]
                        if kc == 0 and tg == 0:
                            waits += self.need("qk_ps")
                        m = P.op("tensor", lambda t, kc_=kc, tg_=tg, oi_=oi: t.matmul(
                            self.psum[:, tg_ * 512:tg_ * 512 + 512],
                            self.AB[:, AB_PANEL + kc_ * 512 + oi_ * 128:AB_PANEL + kc_ * 512 + oi_ * 128 + 128],
                            self.AB[:, AB_H1 + kc_ * TPC + tg_ * 512:AB_H1 + kc_ * TPC + tg_ * 512 + 512],
                            start=(kc_ == 0), stop=(kc_ == DC - 1)), waits=waits, inc=PE)
                s = P.op("scalar", lambda e: e.copy(out=self.AB[:, AB_EVAC:AB_EVAC + TPC],
                                                    in_=self.psum[:, 0:TPC]),
                         waits=[(PE, m)] + self.need("qk_ev"), inc=SE)
                s2 = P.op("scalar", lambda e: e.activation(out=self.AB[:, AB_EVAC + TPC:AB_EVAC + 2 * TPC],
                                                           in_=self.psum[:, 0:TPC], func=AF_.Square),
                          waits=[(PE, m)], inc=SE)
                self.done("qk_ps", SE, s2)
                nv = None
                for tg in range(4):
                    waits = [(SE, s2)] + ([(VE, nv)] if nv else []) + (self.need("n_ps") if tg == 0 else [])
                    mn = P.op("tensor", lambda t, tg_=tg: t.matmul(
                        self.psum[0:1, 2048:2560], self.ones_b[:, :],
                        self.AB[:, AB_EVAC + TPC + tg_ * 512:AB_EVAC + TPC + tg_ * 512 + 512],
                        start=True, stop=True), waits=waits, inc=PE)
                    if not is_k:
                        nv = P.op("vector", lambda e, tg_=tg: e.reduce_max(
                            out=self.tiny[0:1, 32 + tg_ % 2:33 + tg_ % 2], in_=self.psum[0:1, 2048:2560],
                            axis=AXX), waits=[(PE, mn)], inc=VE)
                        if tg % 2 == 1:
                            nv = P.op("vector", lambda e, b_=tg // 2, h_=h: e.tensor_tensor(
                                out=self.rq2[0:1, b_ * 16 + h_:b_ * 16 + h_ + 1],
                                in0=self.tiny[0:1, 32:33], in1=self.tiny[0:1, 33:34], op=ALU.max), inc=VE)
                    else:
                        nv = P.op("vector", lambda e, tg_=tg: e.tensor_copy(
                            out=self.AB[0:1, AB_SB1 + tg_ * 512:AB_SB1 + tg_ * 512 + 512],
                            in_=self.psum[0:1, 2048:2560]), waits=[(PE, mn)], inc=VE)
                self.done("n_ps", VE, nv)
                if is_k:
                    dn = self.dma("gpsimd", self.contribF[self.CF_KN + h:self.CF_KN + h + 1, :],
                                  self.AB[0:1, AB_SB1:AB_SB1 + TPC], waits=[(VE, nv)], stream="qke")
                    d = self.dma("gpsimd", self.contribF[h * 128:(h + 1) * 128, :],
                                 self.AB[:, AB_EVAC:AB_EVAC + TPC], waits=[(SE, s)], stream="qke")
                else:
                    for bb in range(2):
                        d = self.dma("gpsimd", self.q_scr[(bb * H + h) * 128:(bb * H + h) * 128 + 128, :],
                                     self.AB[:, AB_EVAC + bb * NQB:AB_EVAC + (bb + 1) * NQB],
                                     waits=[(SE, s)], stream="qke")
                self.done("qk_ev", *d)
            self.done("pan", PE, P.counts[PE])

    def ffn_v(self):
        P = self.P
        for vg in range(4):
            dp = None
            for kc in range(DC):
                dp = self.dma("sync", self.AB[:, AB_PANEL + kc * 512:AB_PANEL + (kc + 1) * 512],
                              self.wbf["ffn_wvT"][kc * 128:(kc + 1) * 128, vg * 512:(vg + 1) * 512],
                              waits=self.need("pan"), stream="pan")
            for t128 in range(DC):
                par = t128 % 2
                pc = par * 512
                m = None
                for kc in range(DC):
                    waits = [dp, (SE, self.h1_se), (VE, self.iln_ve)]
                    if kc == 0:
                        waits += self.need(("v_ps", par))
                    m = P.op("tensor", lambda t, kc_=kc, t_=t128, pc_=pc: t.matmul(
                        self.psum[:, pc_:pc_ + 512],
                        self.AB[:, AB_H1 + kc_ * TPC + t_ * 128:AB_H1 + kc_ * TPC + t_ * 128 + 128],
                        self.AB[:, AB_PANEL + kc_ * 512:AB_PANEL + (kc_ + 1) * 512],
                        start=(kc_ == 0), stop=(kc_ == DC - 1)), waits=waits, inc=PE)
                s = P.op("scalar", lambda e, par_=par, pc_=pc: e.copy(
                    out=self.AB[:, AB_EVAC + par_ * 512:AB_EVAC + par_ * 512 + 512],
                    in_=self.psum[:, pc_:pc_ + 512]),
                    waits=[(PE, m)] + self.need(("v_ev", par)), inc=SE)
                d = self.dma("gpsimd",
                             self.contribF[self.CF_V + t128 * 128:self.CF_V + (t128 + 1) * 128,
                                           vg * 512:(vg + 1) * 512],
                             self.AB[:, AB_EVAC + par * 512:AB_EVAC + par * 512 + 512],
                             waits=[(SE, s)], stream="ve%d" % par)
                self.done(("v_ps", par), SE, s)
                self.done(("v_ev", par), *d)
            self.done("pan", PE, P.counts[PE])

    def ffn_attn(self):
        P = self.P
        cc = self.cc_cur
        lcol = 3072
        exp_at_slot = {}
        pv_at_probs = {}
        pres_ve = {}
        pres_pe = {}
        for it in range(B * 8):
            bb, hp = it // 8, it % 8
            buf = it % 2
            kbase = AB_KSB + buf * 16384
            qbase = AB_Q + buf * 2048
            for hl in range(2):
                h = hp * 2 + hl
                for rr in range(NCORE):
                    self.dma("sync",
                             self.AB[:, kbase + hl * 8192 + rr * 1024:kbase + hl * 8192 + (rr + 1) * 1024],
                             self.gathF[rr * self.CF_ROWS + h * 128:rr * self.CF_ROWS + (h + 1) * 128,
                                        bb * NQB:(bb + 1) * NQB],
                             waits=[(cc, 1)] + self.need(("ksb", buf)), stream="ksb%d" % buf)
                dq = self.dma("sync", self.AB[:, qbase + hl * 1024:qbase + (hl + 1) * 1024],
                              self.q_scr[(bb * H + h) * 128:(bb * H + h) * 128 + 128, :],
                              waits=self.need(("ksb", buf)), stream="ksb%d" % buf)
            vlast = None
            for hl in range(2):
                h = hp * 2 + hl
                for rr in range(NCORE):
                    sl = rr % 2
                    dn = self.dma("sync", self.AB[0:1, AB_SB1 + sl * 1024:AB_SB1 + (sl + 1) * 1024],
                                  self.gathF[rr * self.CF_ROWS + self.CF_KN + h:
                                             rr * self.CF_ROWS + self.CF_KN + h + 1,
                                             bb * NQB:(bb + 1) * NQB],
                                  waits=self.need(("knt", sl)), stream="knt%d" % sl)
                    if rr == 0:
                        v = P.op("vector", lambda e, sl_=sl: e.reduce_max(
                            out=self.tiny[0:1, 40:41],
                            in_=self.AB[0:1, AB_SB1 + sl_ * 1024:AB_SB1 + sl_ * 1024 + 1024],
                            axis=AXX), waits=[dn], inc=VE)
                    else:
                        v = P.op("vector", lambda e, sl_=sl: e.reduce_max(
                            out=self.tiny[0:1, 43:44],
                            in_=self.AB[0:1, AB_SB1 + sl_ * 1024:AB_SB1 + sl_ * 1024 + 1024],
                            axis=AXX), waits=[dn], inc=VE)
                        v = P.op("vector", lambda e: e.tensor_tensor(
                            out=self.tiny[0:1, 40:41], in0=self.tiny[0:1, 40:41],
                            in1=self.tiny[0:1, 43:44], op=ALU.max), inc=VE)
                    self.done(("knt", sl), VE, v)
                v = P.op("vector", lambda e, b_=bb, h_=h: e.tensor_tensor(
                    out=self.tiny[0:1, 41:42], in0=self.tiny[0:1, 40:41],
                    in1=self.rq2[0:1, b_ * 16 + h_:b_ * 16 + h_ + 1], op=ALU.add), inc=VE)
                v = P.op("vector", lambda e: e.tensor_scalar(
                    out=self.tiny[0:1, 42:43], in0=self.tiny[0:1, 41:42],
                    scalar1=float(-0.5 * SCALE), scalar2=None, op0=ALU.mult), inc=VE)
                mb = P.op("tensor", lambda t: t.matmul(
                    self.psum[:, 2048:2049], self.ones_f1[:, :], self.tiny[0:1, 42:43],
                    start=True, stop=True),
                    waits=[(VE, v)] + self.need(("sc", 2048)) + self.need(("mhc", buf * 2 + hl)), inc=PE)
                vlast = P.op("vector", lambda e, c_=buf * 2 + hl: e.tensor_copy(
                    out=self.mh[:, c_:c_ + 1], in_=self.psum[:, 2048:2049]),
                    waits=[(PE, mb)], inc=VE)
            mh_ve = vlast
            ring = 0
            l_first = True
            probs_info = {}
            acc_last = None
            for j in range(NKT):
                jo = j % 8
                rr, mm_ = j % 8, j // 8
                vslot = j % 4
                dv = self.dma("sync", self.AB[:, AB_VR + vslot * 256:AB_VR + (vslot + 1) * 256],
                              self.gathF[rr * self.CF_ROWS + self.CF_V + bb * NQB + mm_ * 128:
                                         rr * self.CF_ROWS + self.CF_V + bb * NQB + (mm_ + 1) * 128,
                                         hp * 256:(hp + 1) * 256],
                              waits=self.need(("vr", vslot)), stream="vr%d" % vslot)
                for hl in range(2):
                    for si, (qs, n) in enumerate(slabs_for(j)):
                        scol = 2048 + 512 * (ring % 2)
                        ring += 1
                        pcol = AB_PROBS + ((hl * 2 + si) * 2 + (j % 2)) * 512
                        waits = [dq, (VE, mh_ve), (SE, exp_at_slot.get(scol, 0))] + \
                            self.need(("sc", scol))
                        pe_c = P.op("tensor", lambda t, sc=scol, hl_=hl, jj=j, q0=qs, nn=n, kb=kbase, qb=qbase:
                                    t.matmul(self.psum[:, sc:sc + nn],
                                             self.AB[:, kb + hl_ * 8192 + (jj % 8) * 1024 + (jj // 8) * 128:
                                                     kb + hl_ * 8192 + (jj % 8) * 1024 + (jj // 8) * 128 + 128],
                                             self.AB[:, qb + hl_ * 1024 + q0:qb + hl_ * 1024 + q0 + nn],
                                             start=True, stop=True), waits=waits, inc=PE)
                        waits = [(PE, pe_c), (PE, pv_at_probs.get(pcol, 0)), (VE, pres_ve.get(pcol, 0))]
                        se_c = P.op("scalar", lambda s, sc=scol, pc=pcol, c_=buf * 2 + hl, nn=n:
                                    s.activation(out=self.AB[:, pc:pc + nn], in_=self.psum[:, sc:sc + nn],
                                                 func=AF_.Exp, bias=self.mh[:, c_:c_ + 1], scale=SCALE),
                                    waits=waits, inc=SE)
                        exp_at_slot[scol] = se_c
                        ve_c = 0
                        if qs == 128 * (j // 8):
                            ve_c = P.op("vector", lambda v, pc=pcol, jo_=jo: v.tensor_tensor(
                                out=self.AB[:, pc:pc + 128], in0=self.AB[:, pc:pc + 128],
                                in1=self.masks[:, jo_ * 128:(jo_ + 1) * 128], op=ALU.mult),
                                waits=[(SE, se_c)], inc=VE)
                        probs_info[(j, hl, si)] = (se_c, ve_c, pcol, qs, n)
                    for si, (qs, n) in enumerate(slabs_for(j)):
                        se_c, ve_c, pcol, _, _ = probs_info[(j, hl, si)]
                        waits = [(SE, se_c), dv]
                        if ve_c:
                            waits.append((VE, ve_c))
                        if j == 0:
                            waits += self.need(("acc", hl, 0)) + self.need(("acc", hl, 1))
                        if jo == 7 and qs == 128 * (j // 8):
                            pieces = [(qs, 128, True)]
                            if n > 128:
                                pieces.append((qs + 128, n - 128, j == NKT - 1))
                        else:
                            pieces = [(qs, n, j == NKT - 1)]
                        for (pq, pn, pstop) in pieces:
                            acc_last = P.op("tensor",
                                            lambda t, hl_=hl, pc=pcol, q0=pq, nn=pn, vs=vslot, jj=j,
                                            qs_=qs, st=pstop:
                                            t.matmul(self.psum[:, hl_ * 1024 + q0:hl_ * 1024 + q0 + nn],
                                                     self.AB[:, AB_VR + vs * 256 + hl_ * 128:
                                                             AB_VR + vs * 256 + hl_ * 128 + 128],
                                                     self.AB[:, pc + (q0 - qs_):pc + (q0 - qs_) + nn],
                                                     start=(jj == 0), stop=st, skip_group_check=True),
                                            waits=waits, inc=PE)
                        pv_at_probs[pcol] = acc_last
                self.done(("vr", vslot), PE, acc_last)
                if j % 2 == 1:
                    for hl in range(2):
                        for si, (qs, n) in enumerate(slabs_for(j)):
                            _, ve0, pc0, _, _ = probs_info[(j - 1, hl, si)]
                            se1, ve1, pc1, _, _ = probs_info[(j, hl, si)]
                            pscol = AB_PRES + (hl * 2 + si) * 512
                            waits = [(SE, se1), (PE, pres_pe.get(pscol, 0))]
                            if ve0:
                                waits.append((VE, ve0))
                            if ve1:
                                waits.append((VE, ve1))
                            ve_c = P.op("vector", lambda v, a=pc0, b_=pc1, c=pscol, nn=n:
                                        v.tensor_tensor(out=self.AB[:, c:c + nn], in0=self.AB[:, a:a + nn],
                                                        in1=self.AB[:, b_:b_ + nn], op=ALU.add),
                                        waits=waits, inc=VE)
                            pres_ve[pc0] = ve_c
                            pres_ve[pc1] = ve_c
                            last = None
                            for sl in range(n // 128):
                                qc = (qs + sl * 128) // 128
                                lw = [(VE, ve_c)]
                                if l_first:
                                    lw += self.need("l_ps")
                                last = P.op("tensor", lambda t, c=pscol, sl_=sl, hl_=hl, qc_=qc,
                                            first=l_first, st=(j == 8 * qc + 7):
                                            t.matmul(self.psum[:, lcol + hl_ * 8 + qc_:lcol + hl_ * 8 + qc_ + 1],
                                                     self.AB[:, c + sl_ * 128:c + sl_ * 128 + 128],
                                                     self.ones_b[:, :], start=first, stop=st,
                                                     skip_group_check=True),
                                            waits=lw, inc=PE)
                                l_first = False
                            pres_pe[pscol] = last
            pe_done = P.counts[PE]
            last_exp = max(probs_info[(NKT - 1, hl2, si2)][0]
                           for hl2 in range(2) for si2 in range(len(slabs_for(NKT - 1))))
            self.done(("mhc", buf * 2), SE, last_exp)
            self.done(("mhc", buf * 2 + 1), SE, last_exp)
            vrl = P.op("vector", lambda e: e.reciprocal(out=self.AF[:, AF_MISC + 1024:AF_MISC + 1040],
                                                        in_=self.psum[:, lcol:lcol + 16]),
                       waits=[(PE, pe_done)], inc=VE)
            self.done("l_ps", VE, vrl)
            for hl in range(2):
                h = hp * 2 + hl
                tr = None
                for qc in range(8):
                    tw = [(VE, vrl)]
                    if qc == 0:
                        tw += [(SE, exp_at_slot.get(2048, 0)), (SE, exp_at_slot.get(2560, 0))]
                        tw += self.need("ep_ps")
                    tr = P.op("tensor", lambda t, hl_=hl, qc_=qc: t.transpose(
                        self.psum[0:1, 2048 + qc_ * 128:2048 + qc_ * 128 + 128],
                        self.AF[:, AF_MISC + 1024 + hl_ * 8 + qc_:AF_MISC + 1024 + hl_ * 8 + qc_ + 1],
                        self.idf[:, :]), waits=tw, inc=PE)
                vT = P.op("vector", lambda e: e.tensor_copy(out=self.AF[0:1, AF_MISC:AF_MISC + 1024],
                                                            in_=self.psum[0:1, 2048:3072]),
                          waits=[(PE, tr)], inc=VE)
                bc = None
                for i2 in range(2):
                    bc = P.op("tensor", lambda t, i_=i2: t.matmul(
                        self.psum[:, 2048 + i_ * 512:2048 + i_ * 512 + 512], self.ones_f1[:, :],
                        self.AF[0:1, AF_MISC + i_ * 512:AF_MISC + (i_ + 1) * 512],
                        start=True, stop=True), waits=[(VE, vT)], inc=PE)
                vb = P.op("vector", lambda e: e.tensor_copy(out=self.AF[:, AF_MISC:AF_MISC + 1024],
                                                            in_=self.psum[:, 2048:3072]),
                          waits=[(PE, bc)], inc=VE)
                self.done("ep_ps", VE, vb)
                self.done(("sc", 2048), VE, vb)
                self.done(("sc", 2560), VE, vb)
                vy = P.op("vector", lambda e, hl_=hl: e.tensor_tensor(
                    out=self.AB[:, AB_YS:AB_YS + 1024], in0=self.psum[:, hl_ * 1024:(hl_ + 1) * 1024],
                    in1=self.AF[:, AF_MISC:AF_MISC + 1024], op=ALU.mult),
                    waits=[(VE, vb)] + self.need("ys"), inc=VE)
                self.done(("acc", hl, 0), VE, vy)
                self.done(("acc", hl, 1), VE, vy)
                dy = self.dma("gpsimd", self.ctx_scr[h * 128:(h + 1) * 128, bb * NQB:(bb + 1) * NQB],
                              self.AB[:, AB_YS:AB_YS + 1024], waits=[(VE, vy)], stream="ys")
                self.done("ys", *dy)
            self.done(("ksb", buf), PE, acc_last)

    def dense_to_x(self, w_scr, xadd_src, xdst, kind, fuse_stats):
        P = self.P
        dctx = None
        for kc in range(DC):
            dctx = self.dma("sync", self.AB[:, AB_CTX + kc * TPC:AB_CTX + (kc + 1) * TPC],
                            self.ctx_scr[kc * 128:(kc + 1) * 128, :], stream="ctx")
        stev = None
        for tg in range(4):
            for og in range(4):
                mlast = None
                for kc in range(DC):
                    sl = kc % 4
                    ds = self.dma("sync", self.AB[:, AB_SLAB + sl * 512:AB_SLAB + (sl + 1) * 512],
                                  w_scr[kc * 128:(kc + 1) * 128, og * 512:(og + 1) * 512],
                                  waits=self.need(("dslab", sl)), stream="dsl%d" % sl)
                    for oi in range(4):
                        waits = [ds, dctx]
                        if kc == 0:
                            waits += self.need(("dps", oi))
                        mlast = P.op("tensor", lambda t, kc_=kc, tg_=tg, oi_=oi: t.matmul(
                            self.psum[:, oi_ * 512:oi_ * 512 + 512],
                            self.AB[:, AB_SLAB + (kc_ % 4) * 512 + oi_ * 128:
                                    AB_SLAB + (kc_ % 4) * 512 + oi_ * 128 + 128],
                            self.AB[:, AB_CTX + kc_ * TPC + tg_ * 512:AB_CTX + kc_ * TPC + tg_ * 512 + 512],
                            start=(kc_ == 0), stop=(kc_ == DC - 1)), waits=waits, inc=PE)
                    self.done(("dslab", sl), PE, mlast)
                for oi in range(4):
                    oc = og * 4 + oi
                    slot = oc % 4
                    fo = AF_IO + slot * 512
                    fx = AF_IO + 2048 + slot * 512
                    dx = self.dma("sync", self.AF[:, fo:fo + 512],
                                  xadd_src[oc * 128:(oc + 1) * 128, tg * 512:(tg + 1) * 512],
                                  waits=self.need(("dio", slot)) + self.need(("dio2", slot)),
                                  stream="dio%d" % slot)
                    if kind == "wo":
                        v = P.op("vector", lambda e, f_=fo, tg_=tg: e.tensor_tensor(
                            out=self.AF[:, f_:f_ + 512], in0=self.AF[:, f_:f_ + 512],
                            in1=self.AF[:, AF_MUB + tg_ * 512:AF_MUB + (tg_ + 1) * 512], op=ALU.subtract),
                            waits=[dx, (PE, mlast)], inc=VE)
                        v = P.op("vector", lambda e, f_=fo, tg_=tg: e.tensor_tensor(
                            out=self.AF[:, f_:f_ + 512], in0=self.AF[:, f_:f_ + 512],
                            in1=self.AF[:, AF_RSB + tg_ * 512:AF_RSB + (tg_ + 1) * 512], op=ALU.mult), inc=VE)
                        v = P.op("vector", lambda e, f_=fo, fx_=fx, oi_=oi: e.tensor_tensor(
                            out=self.AF[:, fx_:fx_ + 512], in0=self.AF[:, f_:f_ + 512],
                            in1=self.psum[:, oi_ * 512:oi_ * 512 + 512], op=ALU.add), inc=VE)
                    else:
                        v = P.op("vector", lambda e, f_=fo, fx_=fx, oi_=oi: e.tensor_tensor(
                            out=self.AF[:, fx_:fx_ + 512], in0=self.AF[:, f_:f_ + 512],
                            in1=self.psum[:, oi_ * 512:oi_ * 512 + 512], op=ALU.add),
                            waits=[dx, (PE, mlast)], inc=VE)
                    self.done(("dps", oi), VE, v)
                    d = self.dma("gpsimd", xdst[oc * 128:(oc + 1) * 128, tg * 512:(tg + 1) * 512],
                                 self.AF[:, fx:fx + 512], waits=[(VE, v)], stream="dxo%d" % slot)
                    if fuse_stats:
                        sq = P.op("scalar", lambda e, fx_=fx, f_=fo: e.activation(
                            out=self.AF[:, f_:f_ + 512], in_=self.AF[:, fx_:fx_ + 512], func=AF_.Square),
                            waits=[(VE, v)], inc=SE)
                        P.op("tensor", lambda t, fx_=fx, first=(oc == 0), last=(oc == DC - 1): t.matmul(
                            self.psum[0:1, 2048:2560], self.ones_f128[:, :], self.AF[:, fx_:fx_ + 512],
                            start=first, stop=last), waits=[(VE, v)] + (self.need("sst") if oc == 0 else []),
                            inc=PE)
                        m2 = P.op("tensor", lambda t, f_=fo, first=(oc == 0), last=(oc == DC - 1): t.matmul(
                            self.psum[0:1, 2560:3072], self.ones_f128[:, :], self.AF[:, f_:f_ + 512],
                            start=first, stop=last), waits=[(SE, sq)], inc=PE)
                        self.done(("dio", slot), PE, m2)
                        self.done(("dio2", slot), *d)
                    else:
                        self.done(("dio", slot), *d)
            if fuse_stats:
                v = P.op("vector", lambda e, tg_=tg: e.tensor_copy(
                    out=self.statn[:, tg_ * 512:tg_ * 512 + 512], in_=self.psum[0:1, 2048:2560]),
                    waits=[(PE, P.counts[PE])], inc=VE)
                stev = P.op("vector", lambda e, tg_=tg: e.tensor_copy(
                    out=self.statn[:, TPC + tg_ * 512:TPC + tg_ * 512 + 512], in_=self.psum[0:1, 2560:3072]),
                    waits=[(PE, P.counts[PE])], inc=VE)
                self.done("sst", VE, stev)

    def ffn_ff_select(self, xsrc, xdst, final):
        P = self.P
        for tg in range(4):
            sb0 = P.op("tensor", lambda t, tg_=tg: t.matmul(
                self.psum[:, 3072:3584], self.ones_b1[:, :], self.smask[:, tg_ * 512:tg_ * 512 + 512],
                start=True, stop=True), waits=self.need("sbc_ps"), inc=PE)
            v0 = P.op("vector", lambda e: e.tensor_copy(out=self.AB[:, AB_SBC:AB_SBC + 512],
                                                        in_=self.psum[:, 3072:3584]),
                      waits=[(PE, sb0)] + self.need("sbc"), inc=VE)
            sb1 = P.op("tensor", lambda t, tg_=tg: t.matmul(
                self.psum[:, 3072:3584], self.ones_b1[:, :],
                self.smask[:, TPC + tg_ * 512:TPC + tg_ * 512 + 512],
                start=True, stop=True), waits=[(VE, v0)], inc=PE)
            v1 = P.op("vector", lambda e: e.tensor_copy(out=self.AB[:, AB_SBC + 512:AB_SBC + 1024],
                                                        in_=self.psum[:, 3072:3584]),
                      waits=[(PE, sb1)], inc=VE)
            self.done("sbc_ps", VE, v1)
            sbc_ve = v1
            xnlast = None
            for kc in range(DC):
                p = kc % 2
                fo = AF_IO + p * 512
                d = self.dma("sync", self.AF[:, fo:fo + 512],
                             self.xt_scr[kc * 128:(kc + 1) * 128, tg * 512:(tg + 1) * 512],
                             waits=self.need(("xn", p)), stream="xn%d" % p)
                v = P.op("vector", lambda e, f_=fo, tg_=tg: e.tensor_tensor(
                    out=self.AF[:, f_:f_ + 512], in0=self.AF[:, f_:f_ + 512],
                    in1=self.AF[:, AF_MU2B + tg_ * 512:AF_MU2B + (tg_ + 1) * 512], op=ALU.subtract),
                    waits=[d], inc=VE)
                v = P.op("vector", lambda e, f_=fo, tg_=tg: e.tensor_tensor(
                    out=self.AF[:, f_:f_ + 512], in0=self.AF[:, f_:f_ + 512],
                    in1=self.AF[:, AF_RS2B + tg_ * 512:AF_RS2B + (tg_ + 1) * 512], op=ALU.mult), inc=VE)
                xnlast = P.op("vector", lambda e, f_=fo, kc_=kc: e.tensor_scalar(
                    out=self.AB[:, AB_XN2 + kc_ * 512:AB_XN2 + (kc_ + 1) * 512], in0=self.AF[:, f_:f_ + 512],
                    scalar1=self.vec_sb["ln2s_v"][:, kc_:kc_ + 1], scalar2=self.vec_sb["ln2b_v"][:, kc_:kc_ + 1],
                    op0=ALU.mult, op1=ALU.add), inc=VE)
                self.done(("xn", p), VE, xnlast)
            gelast = None
            for ffq in range(16):
                mlast = None
                for kc in range(DC):
                    sl = kc % 4
                    ds = self.dma("sync", self.AB[:, AB_FSLAB + sl * 512:AB_FSLAB + (sl + 1) * 512],
                                  self.wbf["ff1T"][kc * 128:(kc + 1) * 128, ffq * 512:(ffq + 1) * 512],
                                  waits=self.need(("fslab", sl)), stream="fsl%d" % sl)
                    for f4 in range(4):
                        waits = [ds, (VE, xnlast)]
                        if kc == 0:
                            waits += self.need(("fps", f4))
                        mlast = P.op("tensor", lambda t, kc_=kc, f4_=f4: t.matmul(
                            self.psum[:, f4_ * 512:f4_ * 512 + 512],
                            self.AB[:, AB_FSLAB + (kc_ % 4) * 512 + f4_ * 128:
                                    AB_FSLAB + (kc_ % 4) * 512 + f4_ * 128 + 128],
                            self.AB[:, AB_XN2 + kc_ * 512:AB_XN2 + (kc_ + 1) * 512],
                            start=(kc_ == 0), stop=(kc_ == DC - 1)), waits=waits, inc=PE)
                    self.done(("fslab", sl), PE, mlast)
                for f4 in range(4):
                    ffc = ffq * 4 + f4
                    gelast = P.op("scalar", lambda e, f4_=f4, ffc_=ffc: e.activation(
                        out=self.AB[:, AB_G + ffc_ * 512:AB_G + (ffc_ + 1) * 512],
                        in_=self.psum[:, f4_ * 512:f4_ * 512 + 512], func=AF_.Gelu_apprx_tanh),
                        waits=[(PE, mlast)], inc=SE)
                    self.done(("fps", f4), SE, gelast)
            for doq in range(4):
                mlast = None
                for ffc in range(FFD // 128):
                    sl = ffc % 4
                    ds = self.dma("sync", self.AB[:, AB_FSLAB + sl * 512:AB_FSLAB + (sl + 1) * 512],
                                  self.wbf["ff2T"][ffc * 128:(ffc + 1) * 128, doq * 512:(doq + 1) * 512],
                                  waits=self.need(("fslab", sl)), stream="fsl%d" % sl)
                    for d4 in range(4):
                        waits = [ds, (SE, gelast)]
                        if ffc == 0:
                            waits += self.need(("fps", d4))
                        mlast = P.op("tensor", lambda t, ffc_=ffc, d4_=d4: t.matmul(
                            self.psum[:, d4_ * 512:d4_ * 512 + 512],
                            self.AB[:, AB_FSLAB + (ffc_ % 4) * 512 + d4_ * 128:
                                    AB_FSLAB + (ffc_ % 4) * 512 + d4_ * 128 + 128],
                            self.AB[:, AB_G + ffc_ * 512:AB_G + (ffc_ + 1) * 512],
                            start=(ffc_ == 0), stop=(ffc_ == FFD // 128 - 1)), waits=waits, inc=PE)
                    self.done(("fslab", sl), PE, mlast)
                for d4 in range(4):
                    oc = doq * 4 + d4
                    self.select_tile(oc, d4, tg, mlast, xsrc, xdst, final, sbc_ve)
            if not final:
                v = P.op("vector", lambda e, tg_=tg: e.tensor_copy(
                    out=self.statn[:, tg_ * 512:tg_ * 512 + 512], in_=self.psum[0:1, 2048:2560]),
                    waits=[(PE, P.counts[PE])], inc=VE)
                v = P.op("vector", lambda e, tg_=tg: e.tensor_copy(
                    out=self.statn[:, TPC + tg_ * 512:TPC + tg_ * 512 + 512],
                    in_=self.psum[0:1, 2560:3072]),
                    waits=[(PE, P.counts[PE])], inc=VE)
                self.done("sst", VE, v)

    def select_tile(self, oc, d4, tg, mlast, xsrc, xdst, final, sbc_ve):
        P = self.P
        slot = oc % 2
        fXT = AF_IO + 1024 + slot * 512
        fXO = AF_IO + 2048 + slot * 512
        fO2 = AF_IO + 3072 + (slot % 1) * 512
        fE1 = AF_MISC + 1024
        fA = AF_MISC + 1536
        lr = AB_LR2T + slot * 512
        dxt = self.dma("sync", self.AF[:, fXT:fXT + 512],
                       self.xt_scr[oc * 128:(oc + 1) * 128, tg * 512:(tg + 1) * 512],
                       waits=self.need(("se_io", slot)) + self.need(("se_io2", slot)),
                       stream="sio%d" % slot)
        dxo = self.dma("sync", self.AF[:, fXO:fXO + 512],
                       xsrc[oc * 128:(oc + 1) * 128, tg * 512:(tg + 1) * 512], stream="sio%d" % slot)
        dlr = self.dma("sync", self.AB[:, lr:lr + 512],
                       self.lr2_scr[oc * 128:(oc + 1) * 128, tg * 512:(tg + 1) * 512],
                       stream="sio%d" % slot)
        v = P.op("vector", lambda e: e.tensor_tensor(
            out=self.AF[:, fO2:fO2 + 512], in0=self.AF[:, fXT:fXT + 512],
            in1=self.psum[:, d4 * 512:d4 * 512 + 512], op=ALU.add),
            waits=[dlr, (PE, mlast), (VE, sbc_ve)] + self.need("sel_o2"), inc=VE)
        v = P.op("vector", lambda e, tg_=tg: e.tensor_tensor(
            out=self.AF[:, fXO:fXO + 512], in0=self.AF[:, fXO:fXO + 512],
            in1=self.AF[:, AF_MUB + tg_ * 512:AF_MUB + (tg_ + 1) * 512], op=ALU.subtract), inc=VE)
        v = P.op("vector", lambda e, tg_=tg: e.tensor_tensor(
            out=self.AF[:, fXO:fXO + 512], in0=self.AF[:, fXO:fXO + 512],
            in1=self.AF[:, AF_RSB + tg_ * 512:AF_RSB + (tg_ + 1) * 512], op=ALU.mult), inc=VE)
        v = P.op("vector", lambda e: e.tensor_tensor(
            out=self.AF[:, fE1:fE1 + 512], in0=self.AF[:, fXO:fXO + 512],
            in1=self.AF[:, fO2:fO2 + 512], op=ALU.subtract),
            waits=self.need("st_sq2"), inc=VE)
        v = P.op("vector", lambda e, oc_=oc: e.tensor_scalar(
            out=self.AF[:, fXO:fXO + 512], in0=self.AF[:, fE1:fE1 + 512],
            scalar1=self.vec_sb["bias_v"][:, oc_:oc_ + 1], scalar2=None, op0=ALU.add), inc=VE)
        v = P.op("vector", lambda e: e.tensor_tensor(
            out=self.AF[:, fXO:fXO + 512], in0=self.AF[:, fXO:fXO + 512],
            in1=self.AB[:, AB_SBC:AB_SBC + 512], op=ALU.mult), inc=VE)
        v = P.op("vector", lambda e, lr_=lr: e.tensor_tensor(
            out=self.AF[:, fE1:fE1 + 512], in0=self.AF[:, fE1:fE1 + 512],
            in1=self.AB[:, lr_:lr_ + 512], op=ALU.add), inc=VE)
        v = P.op("vector", lambda e: e.tensor_tensor(
            out=self.AF[:, fE1:fE1 + 512], in0=self.AF[:, fE1:fE1 + 512],
            in1=self.AB[:, AB_SBC + 512:AB_SBC + 1024], op=ALU.mult), inc=VE)
        v = P.op("vector", lambda e: e.tensor_tensor(
            out=self.AF[:, fA:fA + 512], in0=self.AF[:, fXO:fXO + 512],
            in1=self.AF[:, fE1:fE1 + 512], op=ALU.add), inc=VE)
        dxo2 = self.dma("sync", self.AF[:, fXO:fXO + 512],
                        xsrc[oc * 128:(oc + 1) * 128, tg * 512:(tg + 1) * 512], waits=[(VE, v)],
                        stream="sio%d" % slot)
        v = P.op("vector", lambda e: e.tensor_tensor(
            out=self.AF[:, fA:fA + 512], in0=self.AF[:, fA:fA + 512],
            in1=self.AF[:, fXO:fXO + 512], op=ALU.add), waits=[dxo2], inc=VE)
        v = P.op("vector", lambda e: e.scalar_tensor_tensor(
            out=self.AF[:, fXT:fXT + 512], in0=self.AF[:, fO2:fO2 + 512], scalar=0.5,
            in1=self.AF[:, fA:fA + 512], op0=ALU.mult, op1=ALU.add), inc=VE)
        self.done("sel_o2", VE, v)
        self.done(("fps", d4), VE, v)
        d = self.dma("gpsimd", xdst[oc * 128:(oc + 1) * 128, tg * 512:(tg + 1) * 512],
                     self.AF[:, fXT:fXT + 512], waits=[(VE, v)], stream="sxn%d" % slot)
        self.done(("se_io", slot), *d)
        if not final:
            sq = P.op("scalar", lambda e: e.activation(
                out=self.AF[:, fE1:fE1 + 512], in_=self.AF[:, fXT:fXT + 512], func=AF_.Square),
                waits=[(VE, v)] + self.need("st_sq"), inc=SE)
            P.op("tensor", lambda t, first=(oc == 0), last=(oc == DC - 1): t.matmul(
                self.psum[0:1, 2048:2560], self.ones_f128[:, :], self.AF[:, fXT:fXT + 512],
                start=first, stop=last), waits=[(VE, v)] + (self.need("sst") if oc == 0 else []), inc=PE)
            m2 = P.op("tensor", lambda t, first=(oc == 0), last=(oc == DC - 1): t.matmul(
                self.psum[0:1, 2560:3072], self.ones_f128[:, :], self.AF[:, fE1:fE1 + 512],
                start=first, stop=last), waits=[(SE, sq)], inc=PE)
            self.done("st_sq", PE, m2)
            self.done("st_sq2", PE, m2)
            self.done(("se_io2", slot), PE, m2)

    def mla_sublayer(self, xsrc, xdst):
        P = self.P
        self.h_pass(xsrc, router=False)
        P.barrier()
        self.ck()
        self.mla_latents()
        P.barrier()
        self.ck()
        self.mla_q()
        P.barrier()
        self.ck()
        self.tap("qm", self.qm_scr)
        self.tap("contribm", self.contribM)
        cc = self.ccs.pop(0)
        P.barrier()
        P.op("gpsimd", lambda g: g.collective_compute(
            "AllGather", ALU.bypass, replica_groups=[list(range(NCORE))],
            ins=[self.contribM.opt()], outs=[self.gathM.opt()]), inc=cc, amt=1)
        self.cc_cur = cc
        P.barrier()
        self.ck()
        self.mla_attn()
        P.barrier()
        self.tap("ctxm", self.ctx_scr)
        self.ck()
        self.dense_to_x(self.wbf["attn_proj"], xsrc, xdst, kind="proj", fuse_stats=True)

    def mla_latents(self):
        P = self.P
        sub = int(os.environ.get("KSUB", "9"))
        dp = None
        for kc in range(DC):
            dp = self.dma("sync", self.AB[:, AB_KVPAN + kc * 128:AB_KVPAN + (kc + 1) * 128],
                          self.wbf["wkvlat"][kc * 128:(kc + 1) * 128, :], stream="kvp")
        for t128 in range(DC):
            if int(os.environ.get("KSUB", "9")) < 1:
                continue
            par = t128 % 2
            pc = par * 512
            ring = AB_RING + par * 512
            m = None
            for kc in range(DC):
                waits = [dp, (SE, self.h1_se)]
                if kc == 0:
                    waits += self.need(("lat_ps", par))
                m = P.op("tensor", lambda t, kc_=kc, t_=t128, pc_=pc: t.matmul(
                    self.psum[:, pc_:pc_ + 128],
                    self.AB[:, AB_H1 + kc_ * TPC + t_ * 128:AB_H1 + kc_ * TPC + t_ * 128 + 128],
                    self.AB[:, AB_KVPAN + kc_ * 128:AB_KVPAN + (kc_ + 1) * 128],
                    start=(kc_ == 0), stop=(kc_ == DC - 1)), waits=waits, inc=PE)
            s = P.op("scalar", lambda e, pc_=pc, r_=ring: e.copy(
                out=self.AB[:, r_:r_ + 128], in_=self.psum[:, pc_:pc_ + 128]),
                waits=[(PE, m)] + self.need(("lat_ev", par)), inc=SE)
            self.done(("lat_ps", par), SE, s)
            if sub < 2:
                self.done(("lat_cs", par), SE, s)
                self.done(("lat_ev", par), SE, s)
                self.done(("lat_ps", par), SE, s)
                continue
            fc = AF_MISC + par * 1024
            dc_ = self.dma("sync", self.AF[:, fc:fc + 32],
                           self.rk_cos_in[t128 * 128:(t128 + 1) * 128, :],
                           waits=self.need(("lat_cs", par)), stream="lcs%d" % par)
            ds_ = self.dma("sync", self.AF[:, fc + 32:fc + 64],
                           self.rk_sin_in[t128 * 128:(t128 + 1) * 128, :], stream="lcs%d" % par)
            ta, tb, tc, td = fc + 64, fc + 96, fc + 128, fc + 160
            v = P.op("vector", lambda e, r_=ring, a=ta, fc_=fc: e.tensor_tensor(
                out=self.AF[:, a:a + 32], in0=self.AB[:, r_:r_ + 64:2], in1=self.AF[:, fc_:fc_ + 32],
                op=ALU.mult), waits=[(SE, s), ds_], inc=VE)
            v = P.op("vector", lambda e, r_=ring, b_=tb, fc_=fc: e.tensor_tensor(
                out=self.AF[:, b_:b_ + 32], in0=self.AB[:, r_ + 1:r_ + 64:2],
                in1=self.AF[:, fc_ + 32:fc_ + 64], op=ALU.mult), inc=VE)
            v = P.op("vector", lambda e, r_=ring, c_=tc, fc_=fc: e.tensor_tensor(
                out=self.AF[:, c_:c_ + 32], in0=self.AB[:, r_:r_ + 64:2],
                in1=self.AF[:, fc_ + 32:fc_ + 64], op=ALU.mult), inc=VE)
            v = P.op("vector", lambda e, r_=ring, d_=td, fc_=fc: e.tensor_tensor(
                out=self.AF[:, d_:d_ + 32], in0=self.AB[:, r_ + 1:r_ + 64:2],
                in1=self.AF[:, fc_:fc_ + 32], op=ALU.mult), inc=VE)
            v = P.op("vector", lambda e, r_=ring, a=ta, b_=tb: e.tensor_tensor(
                out=self.AB[:, r_:r_ + 64:2], in0=self.AF[:, a:a + 32], in1=self.AF[:, b_:b_ + 32],
                op=ALU.subtract), inc=VE)
            v = P.op("vector", lambda e, r_=ring, c_=tc, d_=td: e.tensor_tensor(
                out=self.AB[:, r_ + 1:r_ + 64:2], in0=self.AF[:, c_:c_ + 32], in1=self.AF[:, d_:d_ + 32],
                op=ALU.add), inc=VE)
            self.done(("lat_cs", par), VE, v)
            if sub < 3:
                self.done(("lat_ev", par), VE, v)
                self.done(("lat_ps", par), VE, v)
                continue
            trp = P.op("tensor", lambda t, r_=ring, par_=par: t.transpose(
                self.psb[:, par_ * 128:par_ * 128 + 128], self.AB[:, r_:r_ + 128], self.idb[:, :]),
                waits=[(VE, v)] + self.need(("psb", par)) + self.need("psb_bank"), inc=PE)
            v2 = P.op("vector", lambda e, par_=par: e.tensor_copy(
                out=self.AB[:, AB_QME + par_ * 512:AB_QME + par_ * 512 + 128],
                in_=self.psb[:, par_ * 128:par_ * 128 + 128]),
                waits=[(PE, trp)] + self.need(("lat_e2", par)), inc=VE)
            self.done(("psb", par), VE, v2)
            self.done("psb_bank", VE, v2)
            self.done(("lat_ev", par), VE, v2)
            d = self.dma("gpsimd", self.contribM[:, t128 * 128:(t128 + 1) * 128],
                         self.AB[:, AB_QME + par * 512:AB_QME + par * 512 + 128], waits=[(VE, v2)],
                         stream="lco%d" % par)
            self.done(("lat_e2", par), *d)

    def mla_q(self):
        P = self.P
        dke = None
        for hh in range(H):
            dke = self.dma("sync", self.AB[:, AB_KEXP + hh * 64:AB_KEXP + (hh + 1) * 64],
                           self.wbf["kexpT"][hh * 128:(hh + 1) * 128, :], stream="kexp")
        for vg in range(4):
            dp = None
            for kc in range(DC):
                dp = self.dma("sync", self.AB[:, AB_PANEL + kc * 512:AB_PANEL + (kc + 1) * 512],
                              self.wbf["attn_wq"][kc * 128:(kc + 1) * 128, vg * 512:(vg + 1) * 512],
                              waits=self.need("pan"), stream="pan")
            for t128 in range(DC):
                par = t128 % 2
                pc = par * 512
                ring = AB_RING + par * 1024
                m = None
                for kc in range(DC):
                    waits = [dp, (SE, self.h1_se)]
                    if kc == 0:
                        waits += self.need(("q_ps", par))
                    m = P.op("tensor", lambda t, kc_=kc, t_=t128, pc_=pc: t.matmul(
                        self.psum[:, pc_:pc_ + 512],
                        self.AB[:, AB_H1 + kc_ * TPC + t_ * 128:AB_H1 + kc_ * TPC + t_ * 128 + 128],
                        self.AB[:, AB_PANEL + kc_ * 512:AB_PANEL + (kc_ + 1) * 512],
                        start=(kc_ == 0), stop=(kc_ == DC - 1)), waits=waits, inc=PE)
                s = P.op("scalar", lambda e, pc_=pc, r_=ring: e.copy(
                    out=self.AB[:, r_:r_ + 512], in_=self.psum[:, pc_:pc_ + 512]),
                    waits=[(PE, m)] + self.need(("q_ev", par)), inc=SE)
                self.done(("q_ps", par), SE, s)
                if int(os.environ.get("KSUB2", "9")) < 2:
                    self.done(("q_ev", par), SE, s)
                    self.done(("q_cs", par), SE, s)
                    self.done(("q_t", par), SE, s)
                    continue
                fc = AF_MISC + par * 1024
                dc_ = self.dma("sync", self.AF[:, fc:fc + 256],
                               self.rq_cos_in[t128 * 128:(t128 + 1) * 128, :],
                               waits=self.need(("q_cs", par)), stream="qcs%d" % par)
                ds_ = self.dma("sync", self.AF[:, fc + 256:fc + 512],
                               self.rq_sin_in[t128 * 128:(t128 + 1) * 128, :], stream="qcs%d" % par)
                ta, tb = AF_IO + 2048 + par * 1024, AF_IO + 2048 + par * 1024 + 256
                tc, td = AF_IO + 2048 + par * 1024 + 512, AF_IO + 2048 + par * 1024 + 768
                v = P.op("vector", lambda e, r_=ring, a=ta, fc_=fc: e.tensor_tensor(
                    out=self.AF[:, a:a + 256], in0=self.AB[:, r_:r_ + 512:2], in1=self.AF[:, fc_:fc_ + 256],
                    op=ALU.mult), waits=[(SE, s), ds_] + self.need(("q_t", par)), inc=VE)
                v = P.op("vector", lambda e, r_=ring, b_=tb, fc_=fc: e.tensor_tensor(
                    out=self.AF[:, b_:b_ + 256], in0=self.AB[:, r_ + 1:r_ + 512:2],
                    in1=self.AF[:, fc_ + 256:fc_ + 512], op=ALU.mult), inc=VE)
                v = P.op("vector", lambda e, r_=ring, c_=tc, fc_=fc: e.tensor_tensor(
                    out=self.AF[:, c_:c_ + 256], in0=self.AB[:, r_:r_ + 512:2],
                    in1=self.AF[:, fc_ + 256:fc_ + 512], op=ALU.mult), inc=VE)
                v = P.op("vector", lambda e, r_=ring, d_=td, fc_=fc: e.tensor_tensor(
                    out=self.AF[:, d_:d_ + 256], in0=self.AB[:, r_ + 1:r_ + 512:2],
                    in1=self.AF[:, fc_:fc_ + 256], op=ALU.mult), inc=VE)
                v = P.op("vector", lambda e, r_=ring, a=ta, b_=tb: e.tensor_tensor(
                    out=self.AB[:, r_:r_ + 512:2], in0=self.AF[:, a:a + 256], in1=self.AF[:, b_:b_ + 256],
                    op=ALU.subtract), inc=VE)
                v = P.op("vector", lambda e, r_=ring, c_=tc, d_=td: e.tensor_tensor(
                    out=self.AB[:, r_ + 1:r_ + 512:2], in0=self.AF[:, c_:c_ + 256], in1=self.AF[:, d_:d_ + 256],
                    op=ALU.add), inc=VE)
                self.done(("q_cs", par), VE, v)
                self.done(("q_t", par), VE, v)
                if int(os.environ.get("KSUB2", "9")) < 3:
                    self.done(("q_ev", par), VE, v)
                    continue
                for hl in range(4):
                    slot = (t128 * 4 + hl) % 8
                    trp = P.op("tensor", lambda t, r_=ring, hl_=hl, sl=slot: t.transpose(
                        self.psb[:, sl * 128:(sl + 1) * 128], self.AB[:, r_ + hl_ * 128:r_ + hl_ * 128 + 128],
                        self.idb[:, :]),
                        waits=[(VE, v)] + self.need(("psb", slot)) + self.need("psb_bank"), inc=PE)
                    v2 = P.op("vector", lambda e, hl_=hl, t_=t128, sl=slot: e.tensor_copy(
                        out=self.AB[:, AB_RQT + hl_ * 512 + (t_ % 4) * 128:
                                    AB_RQT + hl_ * 512 + (t_ % 4) * 128 + 128],
                        in_=self.psb[:, sl * 128:(sl + 1) * 128]),
                        waits=[(PE, trp)] + self.need(("rqt", hl, t128 % 4)), inc=VE)
                    self.done(("psb", slot), VE, v2)
                    self.done("psb_bank", VE, v2)
                    self.done(("q_t", par), VE, v2)
                    self.done(("rqt_v", hl, t128 % 4), VE, v2)
                if t128 % 4 == 3 and int(os.environ.get("KSUB2", "9")) >= 4:
                    tq = t128 // 4
                    bb = tq // 2
                    for hl in range(4):
                        h = vg * 4 + hl
                        wv = self.need(("rqt_v", hl, 0)) + self.need(("rqt_v", hl, 1)) + \
                            self.need(("rqt_v", hl, 2)) + self.need(("rqt_v", hl, 3)) + [dke]
                        mq = P.op("tensor", lambda t, h_=h, hl_=hl: t.matmul(
                            self.psum[0:DL, 1024:1536],
                            self.AB[:, AB_KEXP + h_ * 64:AB_KEXP + (h_ + 1) * 64],
                            self.AB[:, AB_RQT + hl_ * 512:AB_RQT + (hl_ + 1) * 512],
                            start=True, stop=True), waits=wv + self.need("qm_ps") + self.need("qn_rd"), inc=PE)
                        s2 = P.op("scalar", lambda e: e.copy(
                            out=self.AB[0:DL, AB_QME:AB_QME + 512], in_=self.psum[0:DL, 1024:1536]),
                            waits=[(PE, mq)] + self.need("qme"), inc=SE)
                        sq = P.op("scalar", lambda e: e.activation(
                            out=self.AB[0:DL, AB_SQ2:AB_SQ2 + 512], in_=self.psum[0:DL, 1024:1536],
                            func=AF_.Square), waits=[(PE, mq)], inc=SE)
                        self.done("qm_ps", SE, sq)
                        mn = P.op("tensor", lambda t: t.matmul(
                            self.psum[0:1, 1536:2048], self.ones_b[0:DL, :], self.AB[0:DL, AB_SQ2:AB_SQ2 + 512],
                            start=True, stop=True), waits=[(SE, sq)] + self.need("qn_ps"), inc=PE)
                        if tq % 2 == 0:
                            nv = P.op("vector", lambda e, b_=bb, h_=h: e.reduce_max(
                                out=self.rq2m[0:1, b_ * 16 + h_:b_ * 16 + h_ + 1],
                                in_=self.psum[0:1, 1536:2048], axis=AXX), waits=[(PE, mn)], inc=VE)
                            self.done("qn_rd", VE, nv)
                        else:
                            nv = P.op("vector", lambda e: e.reduce_max(
                                out=self.tiny[0:1, 48:49], in_=self.psum[0:1, 1536:2048], axis=AXX),
                                waits=[(PE, mn)], inc=VE)
                            nv = P.op("vector", lambda e, b_=bb, h_=h: e.tensor_tensor(
                                out=self.rq2m[0:1, b_ * 16 + h_:b_ * 16 + h_ + 1],
                                in0=self.rq2m[0:1, b_ * 16 + h_:b_ * 16 + h_ + 1],
                                in1=self.tiny[0:1, 48:49], op=ALU.max), inc=VE)
                            self.done("qn_rd", VE, nv)
                        self.done("qn_ps", VE, nv)
                        d = self.dma("gpsimd",
                                     self.qm_scr[(bb * H + h) * DL:(bb * H + h) * DL + DL,
                                                 (tq % 2) * 512:(tq % 2) * 512 + 512],
                                     self.AB[0:DL, AB_QME:AB_QME + 512], waits=[(SE, s2)],
                                     stream="qme")
                        self.done("qme", *d)
                    for hl in range(4):
                        for qq in range(4):
                            self.done(("rqt", hl, qq), PE, P.counts[PE])
            self.done("pan", PE, P.counts[PE])
        self.done(("q_ps", 0), SE, self.P.counts[SE])
        self.done(("q_ps", 1), SE, self.P.counts[SE])

    def mla_attn(self):
        P = self.P
        cc = self.cc_cur
        exp_at_slot = {}
        pv_at_probs = {}
        for bb in range(B):
            dk = None
            for rr in range(NCORE):
                self.dma("sync", self.AB[0:DL, AB_KLAT + rr * 1024:AB_KLAT + (rr + 1) * 1024],
                         self.gathM[rr * 128:rr * 128 + DL, bb * NQB:(bb + 1) * NQB],
                         waits=[(cc, 1)] + self.need("mlakv"), stream="mkv")
                dk = self.dma("sync", self.AB[0:DL, AB_VLT + rr * 1024:AB_VLT + (rr + 1) * 1024],
                              self.gathM[rr * 128 + DL:(rr + 1) * 128, bb * NQB:(bb + 1) * NQB],
                              stream="mkv")
            sqk = P.op("scalar", lambda e: e.activation(out=self.AB[0:DL, AB_SQK:AB_SQK + 8192],
                                                        in_=self.AB[0:DL, AB_KLAT:AB_KLAT + 8192],
                                                        func=AF_.Square), waits=[dk], inc=SE)
            nv = None
            for i in range(16):
                mn = P.op("tensor", lambda t, i_=i: t.matmul(
                    self.psum[0:1, 1536:2048], self.ones_b[0:DL, :],
                    self.AB[0:DL, AB_SQK + i_ * 512:AB_SQK + (i_ + 1) * 512],
                    start=True, stop=True),
                    waits=[(SE, sqk)] + ([(VE, nv)] if nv else []) + self.need("mh_m2"), inc=PE)
                if i == 0:
                    nv = P.op("vector", lambda e, b_=bb: e.reduce_max(
                        out=self.rk2m[0:1, b_:b_ + 1], in_=self.psum[0:1, 1536:2048], axis=AXX),
                        waits=[(PE, mn)], inc=VE)
                else:
                    nv = P.op("vector", lambda e: e.reduce_max(
                        out=self.tiny[0:1, 50:51], in_=self.psum[0:1, 1536:2048], axis=AXX),
                        waits=[(PE, mn)], inc=VE)
                    nv = P.op("vector", lambda e, b_=bb: e.tensor_tensor(
                        out=self.rk2m[0:1, b_:b_ + 1], in0=self.rk2m[0:1, b_:b_ + 1],
                        in1=self.tiny[0:1, 50:51], op=ALU.max), inc=VE)
            vm = P.op("vector", lambda e: e.memset(self.AB[:, AB_VT:AB_VT + NKT * 65], 1.0),
                      waits=self.need("vt_ws"), inc=VE)
            vtv = None
            for j in range(NKT):
                slot = j % 8
                trp = P.op("tensor", lambda t, j_=j, sl=slot: t.transpose(
                    self.psb[:, sl * 128:sl * 128 + 64],
                    self.AB[0:DL, AB_VLT + (j_ % 8) * 1024 + (j_ // 8) * 128:
                            AB_VLT + (j_ % 8) * 1024 + (j_ // 8) * 128 + 128],
                    self.idb[0:DL, 0:DL]),
                    waits=[dk, (VE, vm)] + self.need(("psb", slot)) + self.need("psb_bank"), inc=PE)
                vtv = P.op("vector", lambda e, j_=j, sl=slot: e.tensor_copy(
                    out=self.AB[:, AB_VT + j_ * 65:AB_VT + j_ * 65 + 64],
                    in_=self.psb[:, sl * 128:sl * 128 + 64]), waits=[(PE, trp)], inc=VE)
                self.done(("psb", slot), VE, vtv)
                self.done("psb_bank", VE, vtv)
            for h in range(H):
                buf = h % 2
                dq = self.dma("sync", self.AB[0:DL, AB_QM + buf * 1024:AB_QM + (buf + 1) * 1024],
                              self.qm_scr[(bb * H + h) * DL:(bb * H + h) * DL + DL, :],
                              waits=self.need(("qm_b", buf)), stream="qmb%d" % buf)
                v = P.op("vector", lambda e, b_=bb, h_=h: e.tensor_tensor(
                    out=self.tiny[0:1, 52:53], in0=self.rq2m[0:1, b_ * 16 + h_:b_ * 16 + h_ + 1],
                    in1=self.rk2m[0:1, b_:b_ + 1], op=ALU.add), waits=[(VE, nv)], inc=VE)
                v = P.op("vector", lambda e: e.tensor_scalar(
                    out=self.tiny[0:1, 53:54], in0=self.tiny[0:1, 52:53],
                    scalar1=float(-0.5 * SCALE), scalar2=None, op0=ALU.mult), inc=VE)
                mb = P.op("tensor", lambda t: t.matmul(
                    self.psum[:, 1536:1537], self.ones_f1[:, :], self.tiny[0:1, 53:54],
                    start=True, stop=True), waits=[(VE, v)] + self.need("mh_m") + self.need("mh_m2"),
                    inc=PE)
                mhv = P.op("vector", lambda e, c_=buf: e.tensor_copy(
                    out=self.mh[:, c_:c_ + 1], in_=self.psum[:, 1536:1537]), waits=[(PE, mb)], inc=VE)
                self.done("mh_m", VE, mhv)
                ring = 0
                probs_info = {}
                acc_last = None
                for j in range(NKT):
                    jo = j % 8
                    for si, (qs, n) in enumerate(slabs_for(j)):
                        scol = 2048 + 512 * (ring % 3)
                        ring += 1
                        pcol = AB_PROBS + ((si * 2 + (j % 2)) % 4) * 512
                        waits = [dq, (VE, mhv), (SE, exp_at_slot.get(scol, 0))]
                        pe_c = P.op("tensor", lambda t, sc=scol, jj=j, q0=qs, nn=n, b2=buf:
                                    t.matmul(self.psum[:, sc:sc + nn],
                                             self.AB[0:DL, AB_KLAT + (jj % 8) * 1024 + (jj // 8) * 128:
                                                     AB_KLAT + (jj % 8) * 1024 + (jj // 8) * 128 + 128],
                                             self.AB[0:DL, AB_QM + b2 * 1024 + q0:AB_QM + b2 * 1024 + q0 + nn],
                                             start=True, stop=True), waits=waits, inc=PE)
                        waits = [(PE, pe_c), (PE, pv_at_probs.get(pcol, 0))]
                        se_c = P.op("scalar", lambda s, sc=scol, pc=pcol, c_=buf, nn=n:
                                    s.activation(out=self.AB[:, pc:pc + nn], in_=self.psum[:, sc:sc + nn],
                                                 func=AF_.Exp, bias=self.mh[:, c_:c_ + 1], scale=SCALE),
                                    waits=waits, inc=SE)
                        exp_at_slot[scol] = se_c
                        ve_c = 0
                        if qs == 128 * (j // 8):
                            ve_c = P.op("vector", lambda v2, pc=pcol, jo_=jo: v2.tensor_tensor(
                                out=self.AB[:, pc:pc + 128], in0=self.AB[:, pc:pc + 128],
                                in1=self.masks[:, jo_ * 128:(jo_ + 1) * 128], op=ALU.mult),
                                waits=[(SE, se_c)], inc=VE)
                        probs_info[(j, si)] = (se_c, ve_c, pcol, qs, n)
                    for si, (qs, n) in enumerate(slabs_for(j)):
                        se_c, ve_c, pcol, _, _ = probs_info[(j, si)]
                        waits = [(SE, se_c), (VE, vtv)]
                        if ve_c:
                            waits.append((VE, ve_c))
                        if j == 0:
                            waits += self.need("macc")
                        if jo == 7 and qs == 128 * (j // 8):
                            pieces = [(qs, 128, True)]
                            if n > 128:
                                pieces.append((qs + 128, n - 128, j == NKT - 1))
                        else:
                            pieces = [(qs, n, j == NKT - 1)]
                        for (pq, pn, pstop) in pieces:
                            acc_last = P.op("tensor", lambda t, jj=j, pc=pcol, q0=pq, nn=pn, qs_=qs, st=pstop:
                                            t.matmul(self.psum[0:65, q0:q0 + nn],
                                                     self.AB[:, AB_VT + jj * 65:AB_VT + (jj + 1) * 65],
                                                     self.AB[:, pc + (q0 - qs_):pc + (q0 - qs_) + nn],
                                                     start=(jj == 0), stop=st, skip_group_check=True),
                                            waits=waits, inc=PE)
                        pv_at_probs[pcol] = acc_last
                vrl = P.op("vector", lambda e: e.reciprocal(out=self.AF[0:1, AF_MISC:AF_MISC + 1024],
                                                            in_=self.psum[64:65, 0:1024]),
                           waits=[(PE, acc_last)], inc=VE)
                bc = None
                for i2 in range(2):
                    bc = P.op("tensor", lambda t, i_=i2: t.matmul(
                        self.psum[0:DL, 1024 + i_ * 512:1024 + i_ * 512 + 512], self.ones_f1[:, 0:DL],
                        self.AF[0:1, AF_MISC + i_ * 512:AF_MISC + (i_ + 1) * 512],
                        start=True, stop=True), waits=[(VE, vrl)] + (self.need("mrb") if i2 == 0 else []),
                        inc=PE)
                vb = P.op("vector", lambda e: e.tensor_copy(out=self.AF[0:DL, AF_MISC:AF_MISC + 1024],
                                                            in_=self.psum[0:DL, 1024:2048]),
                          waits=[(PE, bc)], inc=VE)
                vy = P.op("vector", lambda e: e.tensor_tensor(
                    out=self.AB[0:DL, AB_YN:AB_YN + 1024], in0=self.psum[0:DL, 0:1024],
                    in1=self.AF[0:DL, AF_MISC:AF_MISC + 1024], op=ALU.mult),
                    waits=[(VE, vb)] + self.need("yn"), inc=VE)
                self.done("macc", VE, vy)
                dv = self.dma("sync", self.AB[0:DL, AB_VEXP:AB_VEXP + 128],
                              self.wbf["vexp"][h * DL:(h + 1) * DL, :], waits=self.need("vexp"),
                              stream="vex")
                my = P.op("tensor", lambda t: t.matmul(
                    self.psum[:, 1024:1536], self.AB[0:DL, AB_VEXP:AB_VEXP + 128],
                    self.AB[0:DL, AB_YN:AB_YN + 512], start=True, stop=True),
                    waits=[(VE, vy), dv, (VE, vb)], inc=PE)
                my2 = P.op("tensor", lambda t: t.matmul(
                    self.psum[:, 1536:2048], self.AB[0:DL, AB_VEXP:AB_VEXP + 128],
                    self.AB[0:DL, AB_YN + 512:AB_YN + 1024], start=True, stop=True),
                    waits=self.need("mh_m2"), inc=PE)
                self.done("mrb", PE, my2)
                self.done("vexp", PE, my2)
                sy = P.op("scalar", lambda e: e.copy(out=self.AB[:, AB_MYE:AB_MYE + 1024],
                                                     in_=self.psum[:, 1024:2048]),
                          waits=[(PE, my2)] + self.need("ydma"), inc=SE)
                self.done("yn", PE, my2)
                self.done("mh_m2", SE, sy)
                dy = self.dma("gpsimd", self.ctx_scr[h * 128:(h + 1) * 128, bb * NQB:(bb + 1) * NQB],
                              self.AB[:, AB_MYE:AB_MYE + 1024], waits=[(SE, sy)], stream="ymy")
                self.done("ydma", *dy)
                self.done(("qm_b", buf), PE, acc_last)
            self.done("mlakv", PE, self.P.counts[PE])
            self.done("vt_ws", PE, self.P.counts[PE])


def build_nc():
    nc = bass.Bass("TRN2", target_bir_lowering=False, debug=False)
    Builder(nc).build()
    return nc


def _host_inputs(inputs, r):
    asc = np.ascontiguousarray
    f32 = np.float32
    d = {}
    x = inputs["L0_attn_x"]
    d["x0"] = asc(x.reshape(2, 64, 128, 2048)[:, r::8].transpose(3, 0, 1, 2).reshape(2048, 2048))
    return d


_SHARED_CACHE = {}


def _host_shared(inputs):
    if _SHARED_CACHE.get("done"):
        return _SHARED_CACHE
    asc = np.ascontiguousarray
    f32 = np.float32
    s = _SHARED_CACHE
    s["ffn_wqT"] = asc(inputs["L0_ffn_wq"].T)
    s["ffn_wkT"] = asc(inputs["L0_ffn_wk"].T)
    s["ffn_wvT"] = asc(inputs["L0_ffn_wv"].T)
    s["ffn_woT"] = asc(inputs["L0_ffn_wo"].T)
    s["ff1T"] = asc(inputs["L0_ffn_w_ff1"].T)
    s["ff2T"] = asc(inputs["L0_ffn_w_ff2"].T)
    s["wdownT"] = asc(inputs["L0_ffn_w_down"].T)
    s["wupT"] = asc(inputs["L0_ffn_w_up"].T)
    s["attn_wq"] = asc(inputs["L0_attn_wq"])
    s["attn_proj"] = asc(inputs["L0_attn_proj_w"])
    s["wkvlat"] = asc(np.concatenate([inputs["L0_attn_wk_lat"], inputs["L0_attn_wv_lat"]], axis=1))
    s["kexpT"] = asc(inputs["L0_attn_k_expand"].transpose(0, 2, 1).reshape(H * DH, DL))
    s["vexp"] = asc(inputs["L0_attn_v_expand"].reshape(H * DL, DH))
    wrT = inputs["L0_ffn_w_router"].T
    s["wrouter_sb"] = asc(wrT.reshape(16, 128, 3).transpose(1, 0, 2).reshape(128, 48))
    for name, key in [("bias_v", "L0_ffn_simple_bias"), ("ln1s_v", "L0_ffn_ln1_scale"),
                      ("ln1b_v", "L0_ffn_ln1_bias"), ("ln2s_v", "L0_ffn_ln2_scale"),
                      ("ln2b_v", "L0_ffn_ln2_bias")]:
        s[name] = asc(inputs[key].reshape(16, 128).T)
    s["ident_f"] = np.eye(128, dtype=f32)
    s["ident_b"] = np.eye(128, dtype=f32).astype(BFNP)
    inv_q = (1.0 / 10000.0 ** (np.arange(0, DH, 2, dtype=f32) / DH)).astype(f32)
    inv_k = (1.0 / 10000.0 ** (np.arange(0, DL, 2, dtype=f32) / DL)).astype(f32)
    tpos = np.arange(T, dtype=f32)
    angq = tpos[:, None] * inv_q[None, :]
    angk = tpos[:, None] * inv_k[None, :]
    s["cos_q"] = np.cos(angq, dtype=f32)
    s["sin_q"] = np.sin(angq, dtype=f32)
    s["cos_k"] = np.cos(angk, dtype=f32)
    s["sin_k"] = np.sin(angk, dtype=f32)
    s["done"] = True
    return s


def _core_inputs(inputs, s, r):
    asc = np.ascontiguousarray
    f32 = np.float32
    d = {k: s[k] for k in ("ffn_wqT", "ffn_wkT", "ffn_wvT", "ffn_woT", "ff1T", "ff2T",
                           "wdownT", "wupT", "attn_wq", "attn_proj", "wkvlat", "kexpT",
                           "vexp", "wrouter_sb", "bias_v", "ln1s_v", "ln1b_v", "ln2s_v",
                           "ln2b_v", "ident_f", "ident_b")}
    x = inputs["L0_attn_x"]
    d["x0"] = asc(x.reshape(2, 64, 128, 2048)[:, r::8].transpose(3, 0, 1, 2).reshape(2048, 2048))
    kk = np.arange(128)[:, None]
    qq = np.arange(128)[None, :]
    m = np.zeros((8, 128, 128), f32)
    for jo in range(8):
        m[jo] = (qq + (r - jo) * 128 >= kk)
    d["masks_i"] = m.astype(BFNP)
    mm, ii = np.meshgrid(np.arange(8), np.arange(128), indexing="ij")
    pos1 = ((8 * mm + r) * 128 + ii).reshape(-1)
    pos = np.concatenate([pos1, pos1])
    d["rq_cos"] = asc(np.tile(s["cos_q"][pos], (1, 4)))
    d["rq_sin"] = asc(np.tile(s["sin_q"][pos], (1, 4)))
    d["rk_cos"] = asc(s["cos_k"][pos])
    d["rk_sin"] = asc(s["sin_k"][pos])
    return d


def kernel(**inputs):
    s = dict()
    _SHARED_CACHE.clear()
    sh = _host_shared(inputs)
    in_maps = [_core_inputs(inputs, sh, r) for r in range(NCORE)]
    nc = build_nc()
    res = run_bass_kernel_spmd(nc, in_maps, core_ids=list(range(NCORE)))
    y = np.empty((2, 8192, 2048), np.float32)
    yv = y.reshape(2, 64, 128, 2048)
    for r in range(NCORE):
        o = res.results[r]["out"]
        yv[:, r::8] = o.reshape(2048, 2, 8, 128).transpose(1, 2, 3, 0)
    if DEBUG:
        kernel.last_results = res
    return y


# version 02
# speedup vs baseline: 1.0220x; 1.0220x over previous
import os
import numpy as np
import ml_dtypes
import concourse.bass as bass
import concourse.mybir as mybir
from concourse.bass_utils import run_bass_kernel_spmd
from contextlib import ExitStack

F32 = mybir.dt.float32
BF16 = mybir.dt.bfloat16
AF_ = mybir.ActivationFunctionType
ALU = mybir.AluOpType
AXX = mybir.AxisListType.X
BFNP = ml_dtypes.bfloat16

B, T, D, H, DH, DL, FFD, RLOW = 2, 8192, 2048, 16, 128, 64, 8192, 32
NCORE = 8
TPC = 2048
NQB = 1024
DC = 16
NKT = 64
SCALE = float(1.0 / np.sqrt(128.0))
EPS = 1e-5

DEBUG = False
ENGINES = ("sync", "tensor", "vector", "scalar", "gpsimd")
DMA, DMB, PE, SE, VE = "dma", "dmb", "pe", "se", "ve"


class Plan:
    def __init__(self):
        self.ops = {e: [] for e in ENGINES}
        self.counts = {PE: 0, SE: 0, VE: 0}
        self._barrier = {}

    def op(self, engine, fn, waits=(), inc=None, amt=1):
        w = list(waits)
        if self._barrier.get(engine):
            w = list(self._barrier[engine]) + w
            self._barrier[engine] = None
        if engine in ("vector", "scalar") and inc is not None:
            prev = self.counts.get(inc, 0)
            if prev:
                w.append((inc, prev))
        if inc is not None:
            self.counts[inc] = self.counts.get(inc, 0) + amt
        self.ops[engine].append((fn, tuple(w), inc, amt))
        return self.counts.get(inc) if inc is not None else None

    def barrier(self):
        snap = [(s, c) for s, c in self.counts.items() if c > 0]
        for e in ENGINES:
            self._barrier[e] = snap

    def emit(self, block, sems):
        for ename in ENGINES:
            ops = self.ops[ename]
            if not ops:
                continue
            dec = getattr(block, ename)

            def section(eng, _ops=ops):
                waited = {}
                for fn, waits, inc, amt in _ops:
                    for sname, val in waits:
                        if val and val > 0 and waited.get(sname, 0) < val:
                            eng.wait_ge(sems[sname], val)
                            waited[sname] = val
                    ins = fn(eng)
                    if inc is not None:
                        ins.then_inc(sems[inc], amt)
            dec(section)


def slabs_for(j):
    qs = 128 * (j // 8)
    out = []
    if qs < 512:
        out.append((qs, 512 - qs))
    if 1024 - max(qs, 512) > 0:
        out.append((max(qs, 512), 1024 - max(qs, 512)))
    return out


AB_H1 = 0
AB_PANEL = 32768
AB_EVAC = 40960
AB_WDN = 45056
AB_WUP = 45568
AB_LR2E = 43008
AB_KSB = 0
AB_PROBS = 32768
AB_PRES = 36864
AB_VR = 38912
AB_Q = 39936
AB_SB1 = 44032
AB_YS = 46080
AB_CTX = 0
AB_SLAB = 32768
AB_G = 0
AB_XN2 = 32768
AB_FSLAB = 40960
AB_SBC = 49152
AB_LR2T = 50176
AB_KVPAN = 32768
AB_KEXP = 40960
AB_RQT = 41984
AB_RING = 44032
AB_QME = 46080
AB_SQ2 = 47104
AB_MYE = 45056
AB_KLAT = 0
AB_VLT = 8192
AB_SQK = 16384
AB_VT = 24576
AB_QM = 28736
AB_YN = 30784
AB_VEXP = 31808

AF_IO = 0
AF_MUB = 4096
AF_RSB = 6144
AF_MU2B = 8192
AF_RS2B = 10240
AF_MISC = 12288


class Builder:
    def __init__(self, nc):
        self.nc = nc
        self.P = Plan()
        self.war = {}
        self.ccs = ["cc0", "cc1", "cc2"]

    def need(self, key):
        v = self.war.get(key)
        return list(v.items()) if v else []

    def done(self, key, sem, count):
        d = self.war.setdefault(key, {})
        if count and count > d.get(sem, 0):
            d[sem] = count

    def dma(self, q, dst, src, waits=(), stream="misc"):
        sem = "D" + stream
        c = self.P.op(q, lambda e, d=dst, s=src: e.dma_start(out=d, in_=s),
                      waits=waits, inc=sem, amt=16)
        return (sem, c)

    def build(self):
        nc = self.nc
        ap = lambda name, shape, dt, kind="ExternalInput": nc.dram_tensor(name, shape, dt, kind=kind).ap()
        self.x0 = ap("x0", [D, TPC], F32)
        self.out = ap("out", [D, TPC], F32, "ExternalOutput")
        self.w_in = {}
        for name, shape in [
            ("ffn_wqT", [D, D]), ("ffn_wkT", [D, D]), ("ffn_wvT", [D, D]), ("ffn_woT", [D, D]),
            ("ff1T", [D, FFD]), ("ff2T", [FFD, D]),
            ("wdownT", [D, RLOW]), ("wupT", [RLOW, D]),
            ("attn_wq", [D, D]), ("attn_proj", [D, D]), ("wkvlat", [D, 2 * DL]),
            ("kexpT", [H * DH, DL]), ("vexp", [H * DL, DH]),
        ]:
            self.w_in[name] = ap(name, shape, F32)
        self.wrouter_in = ap("wrouter_sb", [128, DC * 3], F32)
        self.vec_in = {n: ap(n, [128, DC], F32)
                       for n in ("bias_v", "ln1s_v", "ln1b_v", "ln2s_v", "ln2b_v")}
        self.ident_f_in = ap("ident_f", [128, 128], F32)
        self.ident_b_in = ap("ident_b", [128, 128], BF16)
        self.masks_in = ap("masks_i", [8, 128, 128], BF16)
        self.rq_cos_in = ap("rq_cos", [TPC, 256], F32)
        self.rq_sin_in = ap("rq_sin", [TPC, 256], F32)
        self.rk_cos_in = ap("rk_cos", [TPC, 32], F32)
        self.rk_sin_in = ap("rk_sin", [TPC, 32], F32)

        dt_ = lambda name, shape, dt, **kw: nc.dram_tensor(name, shape, dt, kind="Internal", **kw).ap()
        self.tiled = {"ffn_wqT", "ffn_wkT", "ffn_wvT", "ffn_woT", "attn_wq", "attn_proj",
                      "ff1T", "ff2T"}
        def bf_shape(n, t):
            if n in self.tiled:
                rows, cols = t.shape
                return [(cols // 512) * 128, (rows // 128) * 512]
            return list(t.shape)
        self.wbf = {n: dt_("bf_" + n, bf_shape(n, t), BF16) for n, t in self.w_in.items()}
        self.x_a = dt_("x_a", [D, TPC], F32)
        self.x_b = dt_("x_b", [D, TPC], F32)
        self.xt_scr = dt_("xt_scr", [D, TPC], F32)
        self.lr2_scr = dt_("lr2_scr", [D, TPC], BF16)
        self.q_scr = dt_("q_scr", [B * H * 128, NQB], BF16)
        self.qm_scr = dt_("qm_scr", [B * H * DL, NQB], BF16)
        self.ctx_scr = dt_("ctx_scr", [D, TPC], BF16)
        self.CF_K, self.CF_KN, self.CF_V, self.CF_ROWS = 0, 2048, 2064, 4112
        self.contribF = dt_("contribF", [self.CF_ROWS, TPC], BF16)
        self.gathF = dt_("gathF", [NCORE * self.CF_ROWS, TPC], BF16, addr_space="Shared")
        self.contribM = dt_("contribM", [128, TPC], BF16)
        self.gathM = dt_("gathM", [NCORE * 128, TPC], BF16, addr_space="Shared")
        self.dbg = {}
        self.dbg_out = {}
        if DEBUG:
            for n, shape, dt2 in [("x1", [D, TPC], F32), ("xt", [D, TPC], F32),
                                  ("ctx", [D, TPC], BF16), ("q", [B * H * 128, NQB], BF16),
                                  ("contrib", [self.CF_ROWS, TPC], BF16),
                                  ("lr2", [D, TPC], BF16), ("smask", [1, 2 * TPC], BF16),
                                  ("x2", [D, TPC], F32), ("ctxm", [D, TPC], BF16),
                                  ("qm", [B * H * DL, NQB], BF16), ("contribm", [128, TPC], BF16)]:
                self.dbg[n] = dt_("dbg_" + n, shape, dt2)
                self.dbg_out[n] = ap("dout_" + n, shape, dt2, "ExternalOutput")

        with ExitStack() as st:
            E = st.enter_context
            sb = lambda name, shape, dt: E(nc.sbuf_tensor(name, shape, dt))
            self.masks = sb("masks", [128, 8 * 128], BF16)
            self.idf = sb("idf", [128, 128], F32)
            self.idb = sb("idb", [128, 128], BF16)
            self.ones_b = sb("ones_b", [128, 1], BF16)
            self.ones_b1 = sb("ones_b1", [1, 128], BF16)
            self.ones_f1 = sb("ones_f1", [1, 128], F32)
            self.ones_f128 = sb("ones_f128", [128, 1], F32)
            self.vec_sb = {n: sb("sb_" + n, [128, DC], F32) for n in self.vec_in}
            self.wrouter = sb("wrouter", [128, DC * 3], F32)
            self.statn = sb("statn", [1, 2 * TPC], F32)
            self.smask = sb("smask", [1, 2 * TPC], BF16)
            self.rq2 = sb("rq2", [1, 64], F32)
            self.rq2m = sb("rq2m", [1, 64], F32)
            self.rk2m = sb("rk2m", [1, 4], F32)
            self.mh = sb("mh", [128, 4], F32)
            self.tiny = sb("tiny", [1, 64], F32)
            self.epsv = sb("epsv", [1, 1], F32)
            self.AB = sb("AB", [128, 53248], BF16)
            self.AF = sb("AF", [128, 14336], F32)
            self.psum = E(nc.psum_tensor("psum", [128, 3584], F32))
            self.psb = E(nc.psum_tensor("psb", [128, 1024], BF16))

            self.emit_all()

            sems = {}
            for name in sorted(self.P.counts):
                sems[name] = E(nc.semaphore(name.replace(":", "_")))
            block = E(nc.Block())
            self.P.emit(block, sems)
        return nc

    def tap(self, name, src):
        if not DEBUG or name not in self.dbg:
            return
        w = [(s, c) for s, c in self.P.counts.items() if c > 0]
        d = self.dma("gpsimd", self.dbg[name], src, waits=w, stream="tap")
        self.dma("gpsimd", self.dbg_out[name], self.dbg[name], waits=[d], stream="tap")

    class _Stop(Exception):
        pass

    def ck(self):
        self._phase += 1
        if self._phase > int(os.environ.get("KSTOP", "99")):
            raise Builder._Stop()

    def emit_all(self):
        self._phase = 0
        try:
            if os.environ.get("KONLY") == "lat":
                P = self.P
                self.load_consts()
                self.prep_weights()
                P.barrier()
                self.stats_pass(self.x0)
                P.barrier()
                self.finalize_stats(AF_MUB, AF_RSB)
                P.barrier()
                self.h_pass(self.x0, router=False)
                P.barrier()
                self.mla_latents()
                raise Builder._Stop()
            self._emit_all()
        except Builder._Stop:
            pass
        fin = [(s, c) for s, c in self.P.counts.items() if c > 0]
        self.P.op("gpsimd", lambda g: g.engine_nop(), waits=fin)

    def _emit_all(self):
        P = self.P
        self.load_consts()
        self.ck()
        self.prep_weights()
        P.barrier()
        self.ck()
        self.stats_pass(self.x0)
        P.barrier()
        self.finalize_stats(AF_MUB, AF_RSB)
        P.barrier()
        self.ck()
        self.ffn_sublayer(self.x0, self.x_a, final=False)
        self.tap("x1", self.x_a)
        P.barrier()
        self.finalize_stats(AF_MUB, AF_RSB)
        P.barrier()
        self.ck()
        self.mla_sublayer(self.x_a, self.x_b)
        self.tap("x2", self.x_b)
        P.barrier()
        self.finalize_stats(AF_MUB, AF_RSB)
        P.barrier()
        self.ck()
        self.ffn_sublayer(self.x_b, self.out, final=True)

    def load_consts(self):
        for jo in range(8):
            self.dma("sync", self.masks[:, jo * 128:(jo + 1) * 128], self.masks_in[jo], stream="consts")
        self.dma("sync", self.idf[:, :], self.ident_f_in, stream="consts")
        self.dma("sync", self.idb[:, :], self.ident_b_in, stream="consts")
        for n in self.vec_in:
            self.dma("sync", self.vec_sb[n][:, :], self.vec_in[n], stream="consts")
        self.dma("sync", self.wrouter[:, :], self.wrouter_in, stream="consts")
        P = self.P
        P.op("vector", lambda v: v.memset(self.ones_b[:], 1.0), inc=VE)
        P.op("vector", lambda v: v.memset(self.ones_b1[:], 1.0), inc=VE)
        P.op("vector", lambda v: v.memset(self.ones_f1[:], 1.0), inc=VE)
        P.op("vector", lambda v: v.memset(self.ones_f128[:], 1.0), inc=VE)
        P.op("vector", lambda v: v.memset(self.epsv[:], float(EPS)), inc=VE)

    def prep_weights(self):
        P = self.P
        par = 0
        for name, src in self.w_in.items():
            dst = self.wbf[name]
            rows, cols = src.shape
            for ch in range((rows + 127) // 128):
                r0 = ch * 128
                pr = min(128, rows - r0)
                for c0 in range(0, cols, 4096):
                    w = min(4096, cols - c0)
                    p = par % 2
                    par += 1
                    fo = p * 4096
                    bo = AB_PANEL + p * 4096
                    d = self.dma("sync", self.AF[0:pr, fo:fo + w], src[r0:r0 + pr, c0:c0 + w],
                                 waits=self.need(("wp", p)), stream="wpi%d" % p)
                    v = P.op("vector", lambda e, pr_=pr, f_=fo, b_=bo, w_=w:
                             e.tensor_copy(out=self.AB[0:pr_, b_:b_ + w_], in_=self.AF[0:pr_, f_:f_ + w_]),
                             waits=[d], inc=VE)
                    if name in self.tiled:
                        d2 = None
                        kc = ch
                        for oi in range(w // 512):
                            og = (c0 // 512) + oi
                            d2 = self.dma("gpsimd",
                                          dst[og * 128:(og + 1) * 128, kc * 512:(kc + 1) * 512],
                                          self.AB[0:pr, bo + oi * 512:bo + (oi + 1) * 512],
                                          waits=[(VE, v)], stream="wpo%d" % p)
                    else:
                        d2 = self.dma("gpsimd", dst[r0:r0 + pr, c0:c0 + w], self.AB[0:pr, bo:bo + w],
                                      waits=[(VE, v)], stream="wpo%d" % p)
                    self.done(("wp", p), *d2)

    def stats_pass(self, xsrc):
        P = self.P
        for tg in range(4):
            m2 = None
            for kc in range(DC):
                p = kc % 2
                fo = AF_IO + p * 512
                d = self.dma("sync", self.AF[:, fo:fo + 512],
                             xsrc[kc * 128:(kc + 1) * 128, tg * 512:(tg + 1) * 512],
                             waits=self.need(("st", p)), stream="stx%d" % p)
                s = P.op("scalar", lambda e, f_=fo: e.activation(
                    out=self.AF[:, f_ + 1024:f_ + 1536], in_=self.AF[:, f_:f_ + 512], func=AF_.Square),
                    waits=[d], inc=SE)
                P.op("tensor", lambda t, f_=fo, first=(kc == 0), last=(kc == DC - 1): t.matmul(
                    self.psum[0:1, 0:512], self.ones_f128[:, :], self.AF[:, f_:f_ + 512],
                    start=first, stop=last), waits=[d], inc=PE)
                m2 = P.op("tensor", lambda t, f_=fo, first=(kc == 0), last=(kc == DC - 1): t.matmul(
                    self.psum[0:1, 512:1024], self.ones_f128[:, :], self.AF[:, f_ + 1024:f_ + 1536],
                    start=first, stop=last), waits=[(SE, s)], inc=PE)
                self.done(("st", p), PE, m2)
            v = P.op("vector", lambda e, tg_=tg: e.tensor_copy(
                out=self.statn[:, tg_ * 512:tg_ * 512 + 512], in_=self.psum[0:1, 0:512]),
                waits=[(PE, m2)], inc=VE)
            v = P.op("vector", lambda e, tg_=tg: e.tensor_copy(
                out=self.statn[:, TPC + tg_ * 512:TPC + tg_ * 512 + 512], in_=self.psum[0:1, 512:1024]),
                waits=[(PE, m2)], inc=VE)
            self.done(("st", 0), VE, v)
            self.done(("st", 1), VE, v)

    def finalize_stats(self, mub_col, rsb_col):
        P = self.P
        v = P.op("vector", lambda e: e.tensor_scalar(out=self.statn[:, :], in0=self.statn[:, :],
                                                     scalar1=float(1.0 / D), scalar2=None, op0=ALU.mult),
                 inc=VE)
        v = P.op("vector", lambda e: e.tensor_tensor(out=self.AF[0:1, AF_MISC:AF_MISC + TPC],
                                                     in0=self.statn[:, 0:TPC], in1=self.statn[:, 0:TPC],
                                                     op=ALU.mult), waits=[(VE, v)], inc=VE)
        v = P.op("vector", lambda e: e.tensor_tensor(out=self.statn[:, TPC:2 * TPC],
                                                     in0=self.statn[:, TPC:2 * TPC],
                                                     in1=self.AF[0:1, AF_MISC:AF_MISC + TPC],
                                                     op=ALU.subtract), waits=[(VE, v)], inc=VE)
        s = P.op("scalar", lambda e: e.activation(out=self.AF[0:1, AF_MISC:AF_MISC + TPC],
                                                  in_=self.statn[:, TPC:2 * TPC],
                                                  func=AF_.Sqrt, bias=self.epsv[:, :]),
                 waits=[(VE, v)], inc=SE)
        v = P.op("vector", lambda e: e.reciprocal(out=self.statn[:, TPC:2 * TPC],
                                                  in_=self.AF[0:1, AF_MISC:AF_MISC + TPC]),
                 waits=[(SE, s)], inc=VE)
        evs = {}
        for half, col in ((0, mub_col), (1, rsb_col)):
            for i in range(4):
                w = [(VE, v)] + ([(VE, evs[i])] if i in evs else [])
                m = P.op("tensor", lambda t, h_=half, i_=i: t.matmul(
                    self.psum[:, i_ * 512:i_ * 512 + 512], self.ones_f1[:, :],
                    self.statn[:, h_ * TPC + i_ * 512:h_ * TPC + (i_ + 1) * 512],
                    start=True, stop=True), waits=w, inc=PE)
                evs[i] = P.op("vector", lambda e, c_=col, i_=i: e.tensor_copy(
                    out=self.AF[:, c_ + i_ * 512:c_ + i_ * 512 + 512],
                    in_=self.psum[:, i_ * 512:i_ * 512 + 512]), waits=[(PE, m)], inc=VE)

    def ffn_sublayer(self, xsrc, xdst, final):
        P = self.P
        self.h_pass(xsrc, router=True)
        P.barrier()
        self.ck()
        self.ffn_lowrank()
        P.barrier()
        self.tap("lr2", self.lr2_scr)
        self.tap("smask", self.smask[:, :])
        self.ffn_inner_ln()
        P.barrier()
        self.ck()
        self.ffn_qk("ffn_wkT", True)
        P.barrier()
        self.ck()
        self.ffn_v()
        P.barrier()
        self.tap("contrib", self.contribF)
        self.ck()
        cc = self.ccs.pop(0)
        P.barrier()
        P.op("gpsimd", lambda g: g.collective_compute(
            "AllGather", ALU.bypass, replica_groups=[list(range(NCORE))],
            ins=[self.contribF.opt()], outs=[self.gathF.opt()]), inc=cc, amt=1)
        self.cc_cur = cc
        self.ck()
        self.ffn_qk("ffn_wqT", False)
        P.barrier()
        self.tap("q", self.q_scr)
        self.ck()
        self.ffn_attn()
        P.barrier()
        self.tap("ctx", self.ctx_scr)
        self.ck()
        self.dense_to_x(self.wbf["ffn_woT"], xsrc, self.xt_scr, kind="wo", fuse_stats=True)
        P.barrier()
        self.tap("xt", self.xt_scr)
        self.finalize_stats(AF_MU2B, AF_RS2B)
        P.barrier()
        self.ck()
        self.ffn_ff_select(xsrc, xdst, final)

    def h_pass(self, xsrc, router):
        P = self.P
        mlast = None
        selast = None
        for kc in range(DC):
            p = kc % 2
            fo = AF_IO + p * 2048
            d = self.dma("sync", self.AF[:, fo:fo + TPC], xsrc[kc * 128:(kc + 1) * 128, :],
                         waits=self.need(("h1", p)), stream="h1x%d" % p)
            v = P.op("vector", lambda e, f_=fo: e.tensor_tensor(
                out=self.AF[:, f_:f_ + TPC], in0=self.AF[:, f_:f_ + TPC],
                in1=self.AF[:, AF_MUB:AF_MUB + TPC], op=ALU.subtract), waits=[d], inc=VE)
            v = P.op("vector", lambda e, f_=fo: e.tensor_tensor(
                out=self.AF[:, f_:f_ + TPC], in0=self.AF[:, f_:f_ + TPC],
                in1=self.AF[:, AF_RSB:AF_RSB + TPC], op=ALU.mult), inc=VE)
            selast = P.op("scalar", lambda e, f_=fo, kc_=kc: e.copy(
                out=self.AB[:, AB_H1 + kc_ * TPC:AB_H1 + (kc_ + 1) * TPC], in_=self.AF[:, f_:f_ + TPC]),
                waits=[(VE, v)], inc=SE)
            if router:
                for tg in range(4):
                    mlast = P.op("tensor", lambda t, f_=fo, kc_=kc, tg_=tg: t.matmul(
                        self.psum[0:3, tg_ * 512:tg_ * 512 + 512],
                        self.wrouter[:, kc_ * 3:(kc_ + 1) * 3],
                        self.AF[:, f_ + tg_ * 512:f_ + tg_ * 512 + 512],
                        start=(kc_ == 0), stop=(kc_ == DC - 1)), waits=[(VE, v)], inc=PE)
                self.done(("h1", p), PE, mlast)
                self.done(("h1", p), SE, selast)
            else:
                self.done(("h1", p), SE, selast)
        self.h1_se = selast
        if not router:
            return
        v = P.op("vector", lambda e: e.tensor_copy(out=self.AF[0:3, AF_IO:AF_IO + TPC],
                                                   in_=self.psum[0:3, 0:TPC]),
                 waits=[(PE, mlast), (SE, selast)], inc=VE)
        d1 = self.dma("gpsimd", self.AF[0:1, AF_IO + TPC:AF_IO + 2 * TPC],
                      self.AF[1:2, AF_IO:AF_IO + TPC], waits=[(VE, v)], stream="rrow")
        d2 = self.dma("gpsimd", self.AF[0:1, AF_MU2B:AF_MU2B + TPC],
                      self.AF[2:3, AF_IO:AF_IO + TPC], waits=[(VE, v)], stream="rrow")
        L0 = self.AF[0:1, AF_IO:AF_IO + TPC]
        L1 = self.AF[0:1, AF_IO + TPC:AF_IO + 2 * TPC]
        L2 = self.AF[0:1, AF_MU2B:AF_MU2B + TPC]
        TA = self.AF[0:1, AF_RS2B:AF_RS2B + TPC]
        TB = self.AF[0:1, AF_MISC:AF_MISC + TPC]
        v = P.op("vector", lambda e: e.tensor_tensor(out=TA, in0=L0, in1=L1, op=ALU.is_ge),
                 waits=[d2], inc=VE)
        v = P.op("vector", lambda e: e.tensor_tensor(out=TB, in0=L0, in1=L2, op=ALU.is_ge), inc=VE)
        v = P.op("vector", lambda e: e.tensor_tensor(out=self.smask[:, 0:TPC], in0=TA, in1=TB,
                                                     op=ALU.mult), inc=VE)
        v = P.op("vector", lambda e: e.tensor_tensor(out=TA, in0=L1, in1=L0, op=ALU.is_gt), inc=VE)
        v = P.op("vector", lambda e: e.tensor_tensor(out=TB, in0=L1, in1=L2, op=ALU.is_ge), inc=VE)
        v = P.op("vector", lambda e: e.tensor_tensor(out=self.smask[:, TPC:2 * TPC], in0=TA, in1=TB,
                                                     op=ALU.mult), inc=VE)
        v = P.op("vector", lambda e: e.tensor_scalar(out=self.smask[:, :], in0=self.smask[:, :],
                                                     scalar1=0.5, scalar2=None, op0=ALU.mult), inc=VE)

    def ffn_lowrank(self):
        P = self.P
        for kc in range(DC):
            self.dma("sync", self.AB[:, AB_WDN + kc * 32:AB_WDN + (kc + 1) * 32],
                     self.wbf["wdownT"][kc * 128:(kc + 1) * 128, :], stream="lrw")
        du = self.dma("sync", self.AB[0:RLOW, AB_WUP:AB_WUP + TPC], self.wbf["wupT"][:, :],
                      stream="lrw")
        m = None
        for tg in range(4):
            for kc in range(DC):
                m = P.op("tensor", lambda t, kc_=kc, tg_=tg: t.matmul(
                    self.psum[0:RLOW, tg_ * 512:tg_ * 512 + 512],
                    self.AB[:, AB_WDN + kc_ * 32:AB_WDN + (kc_ + 1) * 32],
                    self.AB[:, AB_H1 + kc_ * TPC + tg_ * 512:AB_H1 + kc_ * TPC + tg_ * 512 + 512],
                    start=(kc_ == 0), stop=(kc_ == DC - 1)),
                    waits=[du, (SE, self.h1_se)], inc=PE)
        v = P.op("vector", lambda e: e.tensor_copy(out=self.AB[0:RLOW, AB_LR2E:AB_LR2E + TPC],
                                                   in_=self.psum[0:RLOW, 0:TPC]),
                 waits=[(PE, m)], inc=VE)
        for oc in range(DC):
            for tg in range(4):
                i = oc * 4 + tg
                pcol = 2048 + (i % 2) * 512
                ecol = AB_EVAC + (i % 2) * 512
                waits = [(VE, v), du] + self.need(("lrp", i % 2))
                m = P.op("tensor", lambda t, oc_=oc, tg_=tg, pc=pcol: t.matmul(
                    self.psum[:, pc:pc + 512],
                    self.AB[0:RLOW, AB_WUP + oc_ * 128:AB_WUP + (oc_ + 1) * 128],
                    self.AB[0:RLOW, AB_LR2E + tg_ * 512:AB_LR2E + tg_ * 512 + 512],
                    start=True, stop=True), waits=waits, inc=PE)
                s = P.op("scalar", lambda e, pc=pcol, ec=ecol: e.copy(
                    out=self.AB[:, ec:ec + 512], in_=self.psum[:, pc:pc + 512]),
                    waits=[(PE, m)] + self.need(("lre", i % 2)), inc=SE)
                d = self.dma("gpsimd", self.lr2_scr[oc * 128:(oc + 1) * 128, tg * 512:(tg + 1) * 512],
                             self.AB[:, ecol:ecol + 512], waits=[(SE, s)], stream="lre%d" % (i % 2))
                self.done(("lrp", i % 2), SE, s)
                self.done(("lre", i % 2), *d)

    def ffn_inner_ln(self):
        P = self.P
        P.barrier()
        sq = AB_EVAC
        for tg in range(4):
            m2 = None
            sprev = None
            for kc in range(DC):
                p = kc % 2
                sprev = P.op("scalar", lambda e, kc_=kc, tg_=tg, p_=p: e.activation(
                    out=self.AB[:, sq + p_ * 512:sq + p_ * 512 + 512],
                    in_=self.AB[:, AB_H1 + kc_ * TPC + tg_ * 512:AB_H1 + kc_ * TPC + tg_ * 512 + 512],
                    func=AF_.Square), waits=self.need(("iln", p)), inc=SE)
                P.op("tensor", lambda t, kc_=kc, tg_=tg, first=(kc == 0), last=(kc == DC - 1): t.matmul(
                    self.psum[0:1, 2048:2560], self.ones_b[:, :],
                    self.AB[:, AB_H1 + kc_ * TPC + tg_ * 512:AB_H1 + kc_ * TPC + tg_ * 512 + 512],
                    start=first, stop=last), inc=PE)
                m2 = P.op("tensor", lambda t, p_=p, first=(kc == 0), last=(kc == DC - 1): t.matmul(
                    self.psum[0:1, 2560:3072], self.ones_b[:, :],
                    self.AB[:, sq + p_ * 512:sq + p_ * 512 + 512],
                    start=first, stop=last), waits=[(SE, sprev)], inc=PE)
                self.done(("iln", p), PE, m2)
            v = P.op("vector", lambda e, tg_=tg: e.tensor_copy(
                out=self.statn[:, tg_ * 512:tg_ * 512 + 512], in_=self.psum[0:1, 2048:2560]),
                waits=[(PE, m2)], inc=VE)
            v = P.op("vector", lambda e, tg_=tg: e.tensor_copy(
                out=self.statn[:, TPC + tg_ * 512:TPC + tg_ * 512 + 512], in_=self.psum[0:1, 2560:3072]),
                waits=[(PE, m2)], inc=VE)
            self.done(("iln", 0), VE, v)
            self.done(("iln", 1), VE, v)
        P.barrier()
        self.finalize_stats(AF_MU2B, AF_RS2B)
        P.barrier()
        for kc in range(DC):
            p = kc % 2
            fo = AF_IO + p * 2048
            v = P.op("vector", lambda e, kc_=kc, f_=fo: e.tensor_tensor(
                out=self.AF[:, f_:f_ + TPC], in0=self.AB[:, AB_H1 + kc_ * TPC:AB_H1 + (kc_ + 1) * TPC],
                in1=self.AF[:, AF_MU2B:AF_MU2B + TPC], op=ALU.subtract),
                waits=self.need(("ilna", p)), inc=VE)
            v = P.op("vector", lambda e, f_=fo: e.tensor_tensor(
                out=self.AF[:, f_:f_ + TPC], in0=self.AF[:, f_:f_ + TPC],
                in1=self.AF[:, AF_RS2B:AF_RS2B + TPC], op=ALU.mult), inc=VE)
            v = P.op("vector", lambda e, kc_=kc, f_=fo: e.tensor_scalar(
                out=self.AB[:, AB_H1 + kc_ * TPC:AB_H1 + (kc_ + 1) * TPC], in0=self.AF[:, f_:f_ + TPC],
                scalar1=self.vec_sb["ln1s_v"][:, kc_:kc_ + 1], scalar2=self.vec_sb["ln1b_v"][:, kc_:kc_ + 1],
                op0=ALU.mult, op1=ALU.add), inc=VE)
            self.done(("ilna", p), VE, v)
            self.h1_se = max(self.h1_se, 0)
        self.iln_ve = v

    def ffn_qk(self, wname, is_k):
        P = self.P
        for og in range(4):
            dp = self.dma("sync", self.AB[:, AB_PANEL:AB_PANEL + 8192],
                          self.wbf[wname][og * 128:(og + 1) * 128, :],
                          waits=self.need("pan"), stream="pan")
            for oi in range(4):
                h = og * 4 + oi
                m = None
                for kc in range(DC):
                    for tg in range(4):
                        waits = [dp, (SE, self.h1_se), (VE, self.iln_ve)]
                        if kc == 0 and tg == 0:
                            waits += self.need("qk_ps")
                        m = P.op("tensor", lambda t, kc_=kc, tg_=tg, oi_=oi: t.matmul(
                            self.psum[:, tg_ * 512:tg_ * 512 + 512],
                            self.AB[:, AB_PANEL + kc_ * 512 + oi_ * 128:AB_PANEL + kc_ * 512 + oi_ * 128 + 128],
                            self.AB[:, AB_H1 + kc_ * TPC + tg_ * 512:AB_H1 + kc_ * TPC + tg_ * 512 + 512],
                            start=(kc_ == 0), stop=(kc_ == DC - 1)), waits=waits, inc=PE)
                s = P.op("scalar", lambda e: e.copy(out=self.AB[:, AB_EVAC:AB_EVAC + TPC],
                                                    in_=self.psum[:, 0:TPC]),
                         waits=[(PE, m)] + self.need("qk_ev"), inc=SE)
                s2 = P.op("scalar", lambda e: e.activation(out=self.AB[:, AB_EVAC + TPC:AB_EVAC + 2 * TPC],
                                                           in_=self.psum[:, 0:TPC], func=AF_.Square),
                          waits=[(PE, m)], inc=SE)
                self.done("qk_ps", SE, s2)
                nv = None
                for tg in range(4):
                    waits = [(SE, s2)] + ([(VE, nv)] if nv else []) + (self.need("n_ps") if tg == 0 else [])
                    mn = P.op("tensor", lambda t, tg_=tg: t.matmul(
                        self.psum[0:1, 2048:2560], self.ones_b[:, :],
                        self.AB[:, AB_EVAC + TPC + tg_ * 512:AB_EVAC + TPC + tg_ * 512 + 512],
                        start=True, stop=True), waits=waits, inc=PE)
                    if not is_k:
                        nv = P.op("vector", lambda e, tg_=tg: e.reduce_max(
                            out=self.tiny[0:1, 32 + tg_ % 2:33 + tg_ % 2], in_=self.psum[0:1, 2048:2560],
                            axis=AXX), waits=[(PE, mn)], inc=VE)
                        if tg % 2 == 1:
                            nv = P.op("vector", lambda e, b_=tg // 2, h_=h: e.tensor_tensor(
                                out=self.rq2[0:1, b_ * 16 + h_:b_ * 16 + h_ + 1],
                                in0=self.tiny[0:1, 32:33], in1=self.tiny[0:1, 33:34], op=ALU.max), inc=VE)
                    else:
                        nv = P.op("vector", lambda e, tg_=tg: e.tensor_copy(
                            out=self.AB[0:1, AB_SB1 + tg_ * 512:AB_SB1 + tg_ * 512 + 512],
                            in_=self.psum[0:1, 2048:2560]), waits=[(PE, mn)], inc=VE)
                self.done("n_ps", VE, nv)
                if is_k:
                    dn = self.dma("gpsimd", self.contribF[self.CF_KN + h:self.CF_KN + h + 1, :],
                                  self.AB[0:1, AB_SB1:AB_SB1 + TPC], waits=[(VE, nv)], stream="qke")
                    d = self.dma("gpsimd", self.contribF[h * 128:(h + 1) * 128, :],
                                 self.AB[:, AB_EVAC:AB_EVAC + TPC], waits=[(SE, s)], stream="qke")
                else:
                    for bb in range(2):
                        d = self.dma("gpsimd", self.q_scr[(bb * H + h) * 128:(bb * H + h) * 128 + 128, :],
                                     self.AB[:, AB_EVAC + bb * NQB:AB_EVAC + (bb + 1) * NQB],
                                     waits=[(SE, s)], stream="qke")
                self.done("qk_ev", *d)
            self.done("pan", PE, P.counts[PE])

    def ffn_v(self):
        P = self.P
        for vg in range(4):
            dp = self.dma("sync", self.AB[:, AB_PANEL:AB_PANEL + 8192],
                          self.wbf["ffn_wvT"][vg * 128:(vg + 1) * 128, :],
                          waits=self.need("pan"), stream="pan")
            for t128 in range(DC):
                par = t128 % 2
                pc = par * 512
                m = None
                for kc in range(DC):
                    waits = [dp, (SE, self.h1_se), (VE, self.iln_ve)]
                    if kc == 0:
                        waits += self.need(("v_ps", par))
                    m = P.op("tensor", lambda t, kc_=kc, t_=t128, pc_=pc: t.matmul(
                        self.psum[:, pc_:pc_ + 512],
                        self.AB[:, AB_H1 + kc_ * TPC + t_ * 128:AB_H1 + kc_ * TPC + t_ * 128 + 128],
                        self.AB[:, AB_PANEL + kc_ * 512:AB_PANEL + (kc_ + 1) * 512],
                        start=(kc_ == 0), stop=(kc_ == DC - 1)), waits=waits, inc=PE)
                s = P.op("scalar", lambda e, par_=par, pc_=pc: e.copy(
                    out=self.AB[:, AB_EVAC + par_ * 512:AB_EVAC + par_ * 512 + 512],
                    in_=self.psum[:, pc_:pc_ + 512]),
                    waits=[(PE, m)] + self.need(("v_ev", par)), inc=SE)
                d = self.dma("gpsimd",
                             self.contribF[self.CF_V + t128 * 128:self.CF_V + (t128 + 1) * 128,
                                           vg * 512:(vg + 1) * 512],
                             self.AB[:, AB_EVAC + par * 512:AB_EVAC + par * 512 + 512],
                             waits=[(SE, s)], stream="ve%d" % par)
                self.done(("v_ps", par), SE, s)
                self.done(("v_ev", par), *d)
            self.done("pan", PE, P.counts[PE])

    def ffn_attn(self):
        P = self.P
        cc = self.cc_cur
        lcol = 3072
        exp_at_slot = {}
        pv_at_probs = {}
        pres_ve = {}
        pres_pe = {}
        for it in range(B * 8):
            bb, hp = it // 8, it % 8
            buf = it % 2
            kbase = AB_KSB + buf * 16384
            qbase = AB_Q + buf * 2048
            for hl in range(2):
                h = hp * 2 + hl
                for rr in range(NCORE):
                    self.dma("sync",
                             self.AB[:, kbase + hl * 8192 + rr * 1024:kbase + hl * 8192 + (rr + 1) * 1024],
                             self.gathF[rr * self.CF_ROWS + h * 128:rr * self.CF_ROWS + (h + 1) * 128,
                                        bb * NQB:(bb + 1) * NQB],
                             waits=[(cc, 1)] + self.need(("ksb", buf)), stream="ksb%d" % buf)
                dq = self.dma("sync", self.AB[:, qbase + hl * 1024:qbase + (hl + 1) * 1024],
                              self.q_scr[(bb * H + h) * 128:(bb * H + h) * 128 + 128, :],
                              waits=self.need(("ksb", buf)), stream="ksb%d" % buf)
            vlast = None
            for hl in range(2):
                h = hp * 2 + hl
                for rr in range(NCORE):
                    sl = rr % 2
                    dn = self.dma("sync", self.AB[0:1, AB_SB1 + sl * 1024:AB_SB1 + (sl + 1) * 1024],
                                  self.gathF[rr * self.CF_ROWS + self.CF_KN + h:
                                             rr * self.CF_ROWS + self.CF_KN + h + 1,
                                             bb * NQB:(bb + 1) * NQB],
                                  waits=self.need(("knt", sl)), stream="knt%d" % sl)
                    if rr == 0:
                        v = P.op("vector", lambda e, sl_=sl: e.reduce_max(
                            out=self.tiny[0:1, 40:41],
                            in_=self.AB[0:1, AB_SB1 + sl_ * 1024:AB_SB1 + sl_ * 1024 + 1024],
                            axis=AXX), waits=[dn], inc=VE)
                    else:
                        v = P.op("vector", lambda e, sl_=sl: e.reduce_max(
                            out=self.tiny[0:1, 43:44],
                            in_=self.AB[0:1, AB_SB1 + sl_ * 1024:AB_SB1 + sl_ * 1024 + 1024],
                            axis=AXX), waits=[dn], inc=VE)
                        v = P.op("vector", lambda e: e.tensor_tensor(
                            out=self.tiny[0:1, 40:41], in0=self.tiny[0:1, 40:41],
                            in1=self.tiny[0:1, 43:44], op=ALU.max), inc=VE)
                    self.done(("knt", sl), VE, v)
                v = P.op("vector", lambda e, b_=bb, h_=h: e.tensor_tensor(
                    out=self.tiny[0:1, 41:42], in0=self.tiny[0:1, 40:41],
                    in1=self.rq2[0:1, b_ * 16 + h_:b_ * 16 + h_ + 1], op=ALU.add), inc=VE)
                v = P.op("vector", lambda e: e.tensor_scalar(
                    out=self.tiny[0:1, 42:43], in0=self.tiny[0:1, 41:42],
                    scalar1=float(-0.5 * SCALE), scalar2=None, op0=ALU.mult), inc=VE)
                mb = P.op("tensor", lambda t: t.matmul(
                    self.psum[:, 2048:2049], self.ones_f1[:, :], self.tiny[0:1, 42:43],
                    start=True, stop=True),
                    waits=[(VE, v)] + self.need(("sc", 2048)) + self.need(("mhc", buf * 2 + hl)), inc=PE)
                vlast = P.op("vector", lambda e, c_=buf * 2 + hl: e.tensor_copy(
                    out=self.mh[:, c_:c_ + 1], in_=self.psum[:, 2048:2049]),
                    waits=[(PE, mb)], inc=VE)
            mh_ve = vlast
            ring = 0
            l_first = True
            probs_info = {}
            acc_last = None
            for j in range(NKT):
                jo = j % 8
                rr, mm_ = j % 8, j // 8
                vslot = j % 4
                dv = self.dma("sync", self.AB[:, AB_VR + vslot * 256:AB_VR + (vslot + 1) * 256],
                              self.gathF[rr * self.CF_ROWS + self.CF_V + bb * NQB + mm_ * 128:
                                         rr * self.CF_ROWS + self.CF_V + bb * NQB + (mm_ + 1) * 128,
                                         hp * 256:(hp + 1) * 256],
                              waits=self.need(("vr", vslot)), stream="vr%d" % vslot)
                for hl in range(2):
                    for si, (qs, n) in enumerate(slabs_for(j)):
                        scol = 2048 + 512 * (ring % 2)
                        ring += 1
                        pcol = AB_PROBS + ((hl * 2 + si) * 2 + (j % 2)) * 512
                        waits = [dq, (VE, mh_ve), (SE, exp_at_slot.get(scol, 0))] + \
                            self.need(("sc", scol))
                        pe_c = P.op("tensor", lambda t, sc=scol, hl_=hl, jj=j, q0=qs, nn=n, kb=kbase, qb=qbase:
                                    t.matmul(self.psum[:, sc:sc + nn],
                                             self.AB[:, kb + hl_ * 8192 + (jj % 8) * 1024 + (jj // 8) * 128:
                                                     kb + hl_ * 8192 + (jj % 8) * 1024 + (jj // 8) * 128 + 128],
                                             self.AB[:, qb + hl_ * 1024 + q0:qb + hl_ * 1024 + q0 + nn],
                                             start=True, stop=True), waits=waits, inc=PE)
                        waits = [(PE, pe_c), (PE, pv_at_probs.get(pcol, 0)), (VE, pres_ve.get(pcol, 0))]
                        se_c = P.op("scalar", lambda s, sc=scol, pc=pcol, c_=buf * 2 + hl, nn=n:
                                    s.activation(out=self.AB[:, pc:pc + nn], in_=self.psum[:, sc:sc + nn],
                                                 func=AF_.Exp, bias=self.mh[:, c_:c_ + 1], scale=SCALE),
                                    waits=waits, inc=SE)
                        exp_at_slot[scol] = se_c
                        ve_c = 0
                        if qs == 128 * (j // 8):
                            ve_c = P.op("vector", lambda v, pc=pcol, jo_=jo: v.tensor_tensor(
                                out=self.AB[:, pc:pc + 128], in0=self.AB[:, pc:pc + 128],
                                in1=self.masks[:, jo_ * 128:(jo_ + 1) * 128], op=ALU.mult),
                                waits=[(SE, se_c)], inc=VE)
                        probs_info[(j, hl, si)] = (se_c, ve_c, pcol, qs, n)
                    for si, (qs, n) in enumerate(slabs_for(j)):
                        se_c, ve_c, pcol, _, _ = probs_info[(j, hl, si)]
                        waits = [(SE, se_c), dv]
                        if ve_c:
                            waits.append((VE, ve_c))
                        if j == 0:
                            waits += self.need(("acc", hl, 0)) + self.need(("acc", hl, 1))
                        if jo == 7 and qs == 128 * (j // 8):
                            pieces = [(qs, 128, True)]
                            if n > 128:
                                pieces.append((qs + 128, n - 128, j == NKT - 1))
                        else:
                            pieces = [(qs, n, j == NKT - 1)]
                        for (pq, pn, pstop) in pieces:
                            acc_last = P.op("tensor",
                                            lambda t, hl_=hl, pc=pcol, q0=pq, nn=pn, vs=vslot, jj=j,
                                            qs_=qs, st=pstop:
                                            t.matmul(self.psum[:, hl_ * 1024 + q0:hl_ * 1024 + q0 + nn],
                                                     self.AB[:, AB_VR + vs * 256 + hl_ * 128:
                                                             AB_VR + vs * 256 + hl_ * 128 + 128],
                                                     self.AB[:, pc + (q0 - qs_):pc + (q0 - qs_) + nn],
                                                     start=(jj == 0), stop=st, skip_group_check=True),
                                            waits=waits, inc=PE)
                        pv_at_probs[pcol] = acc_last
                self.done(("vr", vslot), PE, acc_last)
                if j % 2 == 1:
                    for hl in range(2):
                        for si, (qs, n) in enumerate(slabs_for(j)):
                            _, ve0, pc0, _, _ = probs_info[(j - 1, hl, si)]
                            se1, ve1, pc1, _, _ = probs_info[(j, hl, si)]
                            pscol = AB_PRES + (hl * 2 + si) * 512
                            waits = [(SE, se1), (PE, pres_pe.get(pscol, 0))]
                            if ve0:
                                waits.append((VE, ve0))
                            if ve1:
                                waits.append((VE, ve1))
                            ve_c = P.op("vector", lambda v, a=pc0, b_=pc1, c=pscol, nn=n:
                                        v.tensor_tensor(out=self.AB[:, c:c + nn], in0=self.AB[:, a:a + nn],
                                                        in1=self.AB[:, b_:b_ + nn], op=ALU.add),
                                        waits=waits, inc=VE)
                            pres_ve[pc0] = ve_c
                            pres_ve[pc1] = ve_c
                            last = None
                            for sl in range(n // 128):
                                qc = (qs + sl * 128) // 128
                                lw = [(VE, ve_c)]
                                if l_first:
                                    lw += self.need("l_ps")
                                last = P.op("tensor", lambda t, c=pscol, sl_=sl, hl_=hl, qc_=qc,
                                            first=l_first, st=(j == 8 * qc + 7):
                                            t.matmul(self.psum[:, lcol + hl_ * 8 + qc_:lcol + hl_ * 8 + qc_ + 1],
                                                     self.AB[:, c + sl_ * 128:c + sl_ * 128 + 128],
                                                     self.ones_b[:, :], start=first, stop=st,
                                                     skip_group_check=True),
                                            waits=lw, inc=PE)
                                l_first = False
                            pres_pe[pscol] = last
            pe_done = P.counts[PE]
            last_exp = max(probs_info[(NKT - 1, hl2, si2)][0]
                           for hl2 in range(2) for si2 in range(len(slabs_for(NKT - 1))))
            self.done(("mhc", buf * 2), SE, last_exp)
            self.done(("mhc", buf * 2 + 1), SE, last_exp)
            vrl = P.op("vector", lambda e: e.reciprocal(out=self.AF[:, AF_MISC + 1024:AF_MISC + 1040],
                                                        in_=self.psum[:, lcol:lcol + 16]),
                       waits=[(PE, pe_done)], inc=VE)
            self.done("l_ps", VE, vrl)
            for hl in range(2):
                h = hp * 2 + hl
                tr = None
                for qc in range(8):
                    tw = [(VE, vrl)]
                    if qc == 0:
                        tw += [(SE, exp_at_slot.get(2048, 0)), (SE, exp_at_slot.get(2560, 0))]
                        tw += self.need("ep_ps")
                    tr = P.op("tensor", lambda t, hl_=hl, qc_=qc: t.transpose(
                        self.psum[0:1, 2048 + qc_ * 128:2048 + qc_ * 128 + 128],
                        self.AF[:, AF_MISC + 1024 + hl_ * 8 + qc_:AF_MISC + 1024 + hl_ * 8 + qc_ + 1],
                        self.idf[:, :]), waits=tw, inc=PE)
                vT = P.op("vector", lambda e: e.tensor_copy(out=self.AF[0:1, AF_MISC:AF_MISC + 1024],
                                                            in_=self.psum[0:1, 2048:3072]),
                          waits=[(PE, tr)], inc=VE)
                bc = None
                for i2 in range(2):
                    bc = P.op("tensor", lambda t, i_=i2: t.matmul(
                        self.psum[:, 2048 + i_ * 512:2048 + i_ * 512 + 512], self.ones_f1[:, :],
                        self.AF[0:1, AF_MISC + i_ * 512:AF_MISC + (i_ + 1) * 512],
                        start=True, stop=True), waits=[(VE, vT)], inc=PE)
                vb = P.op("vector", lambda e: e.tensor_copy(out=self.AF[:, AF_MISC:AF_MISC + 1024],
                                                            in_=self.psum[:, 2048:3072]),
                          waits=[(PE, bc)], inc=VE)
                self.done("ep_ps", VE, vb)
                self.done(("sc", 2048), VE, vb)
                self.done(("sc", 2560), VE, vb)
                vy = P.op("vector", lambda e, hl_=hl: e.tensor_tensor(
                    out=self.AB[:, AB_YS:AB_YS + 1024], in0=self.psum[:, hl_ * 1024:(hl_ + 1) * 1024],
                    in1=self.AF[:, AF_MISC:AF_MISC + 1024], op=ALU.mult),
                    waits=[(VE, vb)] + self.need("ys"), inc=VE)
                self.done(("acc", hl, 0), VE, vy)
                self.done(("acc", hl, 1), VE, vy)
                dy = self.dma("gpsimd", self.ctx_scr[h * 128:(h + 1) * 128, bb * NQB:(bb + 1) * NQB],
                              self.AB[:, AB_YS:AB_YS + 1024], waits=[(VE, vy)], stream="ys")
                self.done("ys", *dy)
            self.done(("ksb", buf), PE, acc_last)

    def dense_to_x(self, w_scr, xadd_src, xdst, kind, fuse_stats):
        P = self.P
        dctx = None
        for kc in range(DC):
            dctx = self.dma("sync", self.AB[:, AB_CTX + kc * TPC:AB_CTX + (kc + 1) * TPC],
                            self.ctx_scr[kc * 128:(kc + 1) * 128, :], stream="ctx")
        stev = None
        for tg in range(4):
            for og in range(4):
                ds = self.dma("sync", self.AB[:, AB_SLAB:AB_SLAB + 8192],
                              w_scr[og * 128:(og + 1) * 128, :],
                              waits=self.need("dslab"), stream="dsl0")
                mlast = None
                for kc in range(DC):
                    for oi in range(4):
                        waits = [ds, dctx]
                        if kc == 0:
                            waits += self.need(("dps", oi))
                        mlast = P.op("tensor", lambda t, kc_=kc, tg_=tg, oi_=oi: t.matmul(
                            self.psum[:, oi_ * 512:oi_ * 512 + 512],
                            self.AB[:, AB_SLAB + kc_ * 512 + oi_ * 128:
                                    AB_SLAB + kc_ * 512 + oi_ * 128 + 128],
                            self.AB[:, AB_CTX + kc_ * TPC + tg_ * 512:AB_CTX + kc_ * TPC + tg_ * 512 + 512],
                            start=(kc_ == 0), stop=(kc_ == DC - 1)), waits=waits, inc=PE)
                self.done("dslab", PE, mlast)
                for oi in range(4):
                    oc = og * 4 + oi
                    slot = oc % 4
                    fo = AF_IO + slot * 512
                    fx = AF_IO + 2048 + slot * 512
                    dx = self.dma("sync", self.AF[:, fo:fo + 512],
                                  xadd_src[oc * 128:(oc + 1) * 128, tg * 512:(tg + 1) * 512],
                                  waits=self.need(("dio", slot)) + self.need(("dio2", slot)),
                                  stream="dio%d" % slot)
                    if kind == "wo":
                        v = P.op("vector", lambda e, f_=fo, tg_=tg: e.tensor_tensor(
                            out=self.AF[:, f_:f_ + 512], in0=self.AF[:, f_:f_ + 512],
                            in1=self.AF[:, AF_MUB + tg_ * 512:AF_MUB + (tg_ + 1) * 512], op=ALU.subtract),
                            waits=[dx, (PE, mlast)], inc=VE)
                        v = P.op("vector", lambda e, f_=fo, tg_=tg: e.tensor_tensor(
                            out=self.AF[:, f_:f_ + 512], in0=self.AF[:, f_:f_ + 512],
                            in1=self.AF[:, AF_RSB + tg_ * 512:AF_RSB + (tg_ + 1) * 512], op=ALU.mult), inc=VE)
                        v = P.op("vector", lambda e, f_=fo, fx_=fx, oi_=oi: e.tensor_tensor(
                            out=self.AF[:, fx_:fx_ + 512], in0=self.AF[:, f_:f_ + 512],
                            in1=self.psum[:, oi_ * 512:oi_ * 512 + 512], op=ALU.add), inc=VE)
                    else:
                        v = P.op("vector", lambda e, f_=fo, fx_=fx, oi_=oi: e.tensor_tensor(
                            out=self.AF[:, fx_:fx_ + 512], in0=self.AF[:, f_:f_ + 512],
                            in1=self.psum[:, oi_ * 512:oi_ * 512 + 512], op=ALU.add),
                            waits=[dx, (PE, mlast)], inc=VE)
                    self.done(("dps", oi), VE, v)
                    d = self.dma("gpsimd", xdst[oc * 128:(oc + 1) * 128, tg * 512:(tg + 1) * 512],
                                 self.AF[:, fx:fx + 512], waits=[(VE, v)], stream="dxo%d" % slot)
                    if fuse_stats:
                        sq = P.op("scalar", lambda e, fx_=fx, f_=fo: e.activation(
                            out=self.AF[:, f_:f_ + 512], in_=self.AF[:, fx_:fx_ + 512], func=AF_.Square),
                            waits=[(VE, v)], inc=SE)
                        P.op("tensor", lambda t, fx_=fx, first=(oc == 0), last=(oc == DC - 1): t.matmul(
                            self.psum[0:1, 2048:2560], self.ones_f128[:, :], self.AF[:, fx_:fx_ + 512],
                            start=first, stop=last), waits=[(VE, v)] + (self.need("sst") if oc == 0 else []),
                            inc=PE)
                        m2 = P.op("tensor", lambda t, f_=fo, first=(oc == 0), last=(oc == DC - 1): t.matmul(
                            self.psum[0:1, 2560:3072], self.ones_f128[:, :], self.AF[:, f_:f_ + 512],
                            start=first, stop=last), waits=[(SE, sq)], inc=PE)
                        self.done(("dio", slot), PE, m2)
                        self.done(("dio2", slot), *d)
                    else:
                        self.done(("dio", slot), *d)
            if fuse_stats:
                v = P.op("vector", lambda e, tg_=tg: e.tensor_copy(
                    out=self.statn[:, tg_ * 512:tg_ * 512 + 512], in_=self.psum[0:1, 2048:2560]),
                    waits=[(PE, P.counts[PE])], inc=VE)
                stev = P.op("vector", lambda e, tg_=tg: e.tensor_copy(
                    out=self.statn[:, TPC + tg_ * 512:TPC + tg_ * 512 + 512], in_=self.psum[0:1, 2560:3072]),
                    waits=[(PE, P.counts[PE])], inc=VE)
                self.done("sst", VE, stev)

    def ffn_ff_select(self, xsrc, xdst, final):
        P = self.P
        for tg in range(4):
            sb0 = P.op("tensor", lambda t, tg_=tg: t.matmul(
                self.psum[:, 3072:3584], self.ones_b1[:, :], self.smask[:, tg_ * 512:tg_ * 512 + 512],
                start=True, stop=True), waits=self.need("sbc_ps"), inc=PE)
            v0 = P.op("vector", lambda e: e.tensor_copy(out=self.AB[:, AB_SBC:AB_SBC + 512],
                                                        in_=self.psum[:, 3072:3584]),
                      waits=[(PE, sb0)] + self.need("sbc"), inc=VE)
            sb1 = P.op("tensor", lambda t, tg_=tg: t.matmul(
                self.psum[:, 3072:3584], self.ones_b1[:, :],
                self.smask[:, TPC + tg_ * 512:TPC + tg_ * 512 + 512],
                start=True, stop=True), waits=[(VE, v0)], inc=PE)
            v1 = P.op("vector", lambda e: e.tensor_copy(out=self.AB[:, AB_SBC + 512:AB_SBC + 1024],
                                                        in_=self.psum[:, 3072:3584]),
                      waits=[(PE, sb1)], inc=VE)
            self.done("sbc_ps", VE, v1)
            sbc_ve = v1
            xnlast = None
            for kc in range(DC):
                p = kc % 2
                fo = AF_IO + p * 512
                d = self.dma("sync", self.AF[:, fo:fo + 512],
                             self.xt_scr[kc * 128:(kc + 1) * 128, tg * 512:(tg + 1) * 512],
                             waits=self.need(("xn", p)), stream="xn%d" % p)
                v = P.op("vector", lambda e, f_=fo, tg_=tg: e.tensor_tensor(
                    out=self.AF[:, f_:f_ + 512], in0=self.AF[:, f_:f_ + 512],
                    in1=self.AF[:, AF_MU2B + tg_ * 512:AF_MU2B + (tg_ + 1) * 512], op=ALU.subtract),
                    waits=[d], inc=VE)
                v = P.op("vector", lambda e, f_=fo, tg_=tg: e.tensor_tensor(
                    out=self.AF[:, f_:f_ + 512], in0=self.AF[:, f_:f_ + 512],
                    in1=self.AF[:, AF_RS2B + tg_ * 512:AF_RS2B + (tg_ + 1) * 512], op=ALU.mult), inc=VE)
                xnlast = P.op("vector", lambda e, f_=fo, kc_=kc: e.tensor_scalar(
                    out=self.AB[:, AB_XN2 + kc_ * 512:AB_XN2 + (kc_ + 1) * 512], in0=self.AF[:, f_:f_ + 512],
                    scalar1=self.vec_sb["ln2s_v"][:, kc_:kc_ + 1], scalar2=self.vec_sb["ln2b_v"][:, kc_:kc_ + 1],
                    op0=ALU.mult, op1=ALU.add), inc=VE)
                self.done(("xn", p), VE, xnlast)
            gelast = None
            for ffq in range(16):
                mlast = None
                for half in range(2):
                    sl = half
                    ds = self.dma("sync", self.AB[:, AB_FSLAB + sl * 4096:AB_FSLAB + (sl + 1) * 4096],
                                  self.wbf["ff1T"][ffq * 128:(ffq + 1) * 128,
                                                   half * 4096:(half + 1) * 4096],
                                  waits=self.need(("fslab", sl)), stream="fsl%d" % sl)
                    for kc8 in range(8):
                        kc = half * 8 + kc8
                        for f4 in range(4):
                            waits = [ds, (VE, xnlast)]
                            if kc == 0:
                                waits += self.need(("fps", f4))
                            mlast = P.op("tensor", lambda t, kc_=kc, kc8_=kc8, f4_=f4, sl_=sl: t.matmul(
                                self.psum[:, f4_ * 512:f4_ * 512 + 512],
                                self.AB[:, AB_FSLAB + sl_ * 4096 + kc8_ * 512 + f4_ * 128:
                                        AB_FSLAB + sl_ * 4096 + kc8_ * 512 + f4_ * 128 + 128],
                                self.AB[:, AB_XN2 + kc_ * 512:AB_XN2 + (kc_ + 1) * 512],
                                start=(kc_ == 0), stop=(kc_ == DC - 1)), waits=waits, inc=PE)
                    self.done(("fslab", sl), PE, mlast)
                for f4 in range(4):
                    ffc = ffq * 4 + f4
                    gelast = P.op("scalar", lambda e, f4_=f4, ffc_=ffc: e.activation(
                        out=self.AB[:, AB_G + ffc_ * 512:AB_G + (ffc_ + 1) * 512],
                        in_=self.psum[:, f4_ * 512:f4_ * 512 + 512], func=AF_.Gelu_apprx_tanh),
                        waits=[(PE, mlast)], inc=SE)
                    self.done(("fps", f4), SE, gelast)
            for doq in range(4):
                mlast = None
                for run in range(8):
                    sl = run % 2
                    ds = self.dma("sync", self.AB[:, AB_FSLAB + sl * 4096:AB_FSLAB + (sl + 1) * 4096],
                                  self.wbf["ff2T"][doq * 128:(doq + 1) * 128,
                                                   run * 4096:(run + 1) * 4096],
                                  waits=self.need(("fslab", sl)), stream="fsl%d" % sl)
                    for f8 in range(8):
                        ffc = run * 8 + f8
                        for d4 in range(4):
                            waits = [ds, (SE, gelast)]
                            if ffc == 0:
                                waits += self.need(("fps", d4))
                            mlast = P.op("tensor", lambda t, ffc_=ffc, f8_=f8, d4_=d4, sl_=sl: t.matmul(
                                self.psum[:, d4_ * 512:d4_ * 512 + 512],
                                self.AB[:, AB_FSLAB + sl_ * 4096 + f8_ * 512 + d4_ * 128:
                                        AB_FSLAB + sl_ * 4096 + f8_ * 512 + d4_ * 128 + 128],
                                self.AB[:, AB_G + ffc_ * 512:AB_G + (ffc_ + 1) * 512],
                                start=(ffc_ == 0), stop=(ffc_ == FFD // 128 - 1)), waits=waits, inc=PE)
                    self.done(("fslab", sl), PE, mlast)
                for d4 in range(4):
                    oc = doq * 4 + d4
                    self.select_tile(oc, d4, tg, mlast, xsrc, xdst, final, sbc_ve)
            if not final:
                v = P.op("vector", lambda e, tg_=tg: e.tensor_copy(
                    out=self.statn[:, tg_ * 512:tg_ * 512 + 512], in_=self.psum[0:1, 2048:2560]),
                    waits=[(PE, P.counts[PE])], inc=VE)
                v = P.op("vector", lambda e, tg_=tg: e.tensor_copy(
                    out=self.statn[:, TPC + tg_ * 512:TPC + tg_ * 512 + 512],
                    in_=self.psum[0:1, 2560:3072]),
                    waits=[(PE, P.counts[PE])], inc=VE)
                self.done("sst", VE, v)

    def select_tile(self, oc, d4, tg, mlast, xsrc, xdst, final, sbc_ve):
        P = self.P
        slot = oc % 2
        fXT = AF_IO + 1024 + slot * 512
        fXO = AF_IO + 2048 + slot * 512
        fO2 = AF_IO + 3072 + (slot % 1) * 512
        fE1 = AF_MISC + 1024
        fA = AF_MISC + 1536
        lr = AB_LR2T + slot * 512
        dxt = self.dma("sync", self.AF[:, fXT:fXT + 512],
                       self.xt_scr[oc * 128:(oc + 1) * 128, tg * 512:(tg + 1) * 512],
                       waits=self.need(("se_io", slot)) + self.need(("se_io2", slot)),
                       stream="sio%d" % slot)
        dxo = self.dma("sync", self.AF[:, fXO:fXO + 512],
                       xsrc[oc * 128:(oc + 1) * 128, tg * 512:(tg + 1) * 512], stream="sio%d" % slot)
        dlr = self.dma("sync", self.AB[:, lr:lr + 512],
                       self.lr2_scr[oc * 128:(oc + 1) * 128, tg * 512:(tg + 1) * 512],
                       stream="sio%d" % slot)
        v = P.op("vector", lambda e: e.tensor_tensor(
            out=self.AF[:, fO2:fO2 + 512], in0=self.AF[:, fXT:fXT + 512],
            in1=self.psum[:, d4 * 512:d4 * 512 + 512], op=ALU.add),
            waits=[dlr, (PE, mlast), (VE, sbc_ve)] + self.need("sel_o2"), inc=VE)
        v = P.op("vector", lambda e, tg_=tg: e.tensor_tensor(
            out=self.AF[:, fXO:fXO + 512], in0=self.AF[:, fXO:fXO + 512],
            in1=self.AF[:, AF_MUB + tg_ * 512:AF_MUB + (tg_ + 1) * 512], op=ALU.subtract), inc=VE)
        v = P.op("vector", lambda e, tg_=tg: e.tensor_tensor(
            out=self.AF[:, fXO:fXO + 512], in0=self.AF[:, fXO:fXO + 512],
            in1=self.AF[:, AF_RSB + tg_ * 512:AF_RSB + (tg_ + 1) * 512], op=ALU.mult), inc=VE)
        v = P.op("vector", lambda e: e.tensor_tensor(
            out=self.AF[:, fE1:fE1 + 512], in0=self.AF[:, fXO:fXO + 512],
            in1=self.AF[:, fO2:fO2 + 512], op=ALU.subtract),
            waits=self.need("st_sq2"), inc=VE)
        v = P.op("vector", lambda e, oc_=oc: e.tensor_scalar(
            out=self.AF[:, fXO:fXO + 512], in0=self.AF[:, fE1:fE1 + 512],
            scalar1=self.vec_sb["bias_v"][:, oc_:oc_ + 1], scalar2=None, op0=ALU.add), inc=VE)
        v = P.op("vector", lambda e: e.tensor_tensor(
            out=self.AF[:, fXO:fXO + 512], in0=self.AF[:, fXO:fXO + 512],
            in1=self.AB[:, AB_SBC:AB_SBC + 512], op=ALU.mult), inc=VE)
        v = P.op("vector", lambda e, lr_=lr: e.tensor_tensor(
            out=self.AF[:, fE1:fE1 + 512], in0=self.AF[:, fE1:fE1 + 512],
            in1=self.AB[:, lr_:lr_ + 512], op=ALU.add), inc=VE)
        v = P.op("vector", lambda e: e.tensor_tensor(
            out=self.AF[:, fE1:fE1 + 512], in0=self.AF[:, fE1:fE1 + 512],
            in1=self.AB[:, AB_SBC + 512:AB_SBC + 1024], op=ALU.mult), inc=VE)
        v = P.op("vector", lambda e: e.tensor_tensor(
            out=self.AF[:, fA:fA + 512], in0=self.AF[:, fXO:fXO + 512],
            in1=self.AF[:, fE1:fE1 + 512], op=ALU.add), inc=VE)
        dxo2 = self.dma("sync", self.AF[:, fXO:fXO + 512],
                        xsrc[oc * 128:(oc + 1) * 128, tg * 512:(tg + 1) * 512], waits=[(VE, v)],
                        stream="sio%d" % slot)
        v = P.op("vector", lambda e: e.tensor_tensor(
            out=self.AF[:, fA:fA + 512], in0=self.AF[:, fA:fA + 512],
            in1=self.AF[:, fXO:fXO + 512], op=ALU.add), waits=[dxo2], inc=VE)
        v = P.op("vector", lambda e: e.scalar_tensor_tensor(
            out=self.AF[:, fXT:fXT + 512], in0=self.AF[:, fO2:fO2 + 512], scalar=0.5,
            in1=self.AF[:, fA:fA + 512], op0=ALU.mult, op1=ALU.add), inc=VE)
        self.done("sel_o2", VE, v)
        self.done(("fps", d4), VE, v)
        d = self.dma("gpsimd", xdst[oc * 128:(oc + 1) * 128, tg * 512:(tg + 1) * 512],
                     self.AF[:, fXT:fXT + 512], waits=[(VE, v)], stream="sxn%d" % slot)
        self.done(("se_io", slot), *d)
        if not final:
            sq = P.op("scalar", lambda e: e.activation(
                out=self.AF[:, fE1:fE1 + 512], in_=self.AF[:, fXT:fXT + 512], func=AF_.Square),
                waits=[(VE, v)] + self.need("st_sq"), inc=SE)
            P.op("tensor", lambda t, first=(oc == 0), last=(oc == DC - 1): t.matmul(
                self.psum[0:1, 2048:2560], self.ones_f128[:, :], self.AF[:, fXT:fXT + 512],
                start=first, stop=last), waits=[(VE, v)] + (self.need("sst") if oc == 0 else []), inc=PE)
            m2 = P.op("tensor", lambda t, first=(oc == 0), last=(oc == DC - 1): t.matmul(
                self.psum[0:1, 2560:3072], self.ones_f128[:, :], self.AF[:, fE1:fE1 + 512],
                start=first, stop=last), waits=[(SE, sq)], inc=PE)
            self.done("st_sq", PE, m2)
            self.done("st_sq2", PE, m2)
            self.done(("se_io2", slot), PE, m2)

    def mla_sublayer(self, xsrc, xdst):
        P = self.P
        self.h_pass(xsrc, router=False)
        P.barrier()
        self.ck()
        self.mla_latents()
        P.barrier()
        self.ck()
        self.mla_q()
        P.barrier()
        self.ck()
        self.tap("qm", self.qm_scr)
        self.tap("contribm", self.contribM)
        cc = self.ccs.pop(0)
        P.barrier()
        P.op("gpsimd", lambda g: g.collective_compute(
            "AllGather", ALU.bypass, replica_groups=[list(range(NCORE))],
            ins=[self.contribM.opt()], outs=[self.gathM.opt()]), inc=cc, amt=1)
        self.cc_cur = cc
        P.barrier()
        self.ck()
        self.mla_attn()
        P.barrier()
        self.tap("ctxm", self.ctx_scr)
        self.ck()
        self.dense_to_x(self.wbf["attn_proj"], xsrc, xdst, kind="proj", fuse_stats=True)

    def mla_latents(self):
        P = self.P
        sub = int(os.environ.get("KSUB", "9"))
        dp = None
        for kc in range(DC):
            dp = self.dma("sync", self.AB[:, AB_KVPAN + kc * 128:AB_KVPAN + (kc + 1) * 128],
                          self.wbf["wkvlat"][kc * 128:(kc + 1) * 128, :], stream="kvp")
        for t128 in range(DC):
            if int(os.environ.get("KSUB", "9")) < 1:
                continue
            par = t128 % 2
            pc = par * 512
            ring = AB_RING + par * 512
            m = None
            for kc in range(DC):
                waits = [dp, (SE, self.h1_se)]
                if kc == 0:
                    waits += self.need(("lat_ps", par))
                m = P.op("tensor", lambda t, kc_=kc, t_=t128, pc_=pc: t.matmul(
                    self.psum[:, pc_:pc_ + 128],
                    self.AB[:, AB_H1 + kc_ * TPC + t_ * 128:AB_H1 + kc_ * TPC + t_ * 128 + 128],
                    self.AB[:, AB_KVPAN + kc_ * 128:AB_KVPAN + (kc_ + 1) * 128],
                    start=(kc_ == 0), stop=(kc_ == DC - 1)), waits=waits, inc=PE)
            s = P.op("scalar", lambda e, pc_=pc, r_=ring: e.copy(
                out=self.AB[:, r_:r_ + 128], in_=self.psum[:, pc_:pc_ + 128]),
                waits=[(PE, m)] + self.need(("lat_ev", par)), inc=SE)
            self.done(("lat_ps", par), SE, s)
            if sub < 2:
                self.done(("lat_cs", par), SE, s)
                self.done(("lat_ev", par), SE, s)
                self.done(("lat_ps", par), SE, s)
                continue
            fc = AF_MISC + par * 1024
            dc_ = self.dma("sync", self.AF[:, fc:fc + 32],
                           self.rk_cos_in[t128 * 128:(t128 + 1) * 128, :],
                           waits=self.need(("lat_cs", par)), stream="lcs%d" % par)
            ds_ = self.dma("sync", self.AF[:, fc + 32:fc + 64],
                           self.rk_sin_in[t128 * 128:(t128 + 1) * 128, :], stream="lcs%d" % par)
            ta, tb, tc, td = fc + 64, fc + 96, fc + 128, fc + 160
            v = P.op("vector", lambda e, r_=ring, a=ta, fc_=fc: e.tensor_tensor(
                out=self.AF[:, a:a + 32], in0=self.AB[:, r_:r_ + 64:2], in1=self.AF[:, fc_:fc_ + 32],
                op=ALU.mult), waits=[(SE, s), ds_], inc=VE)
            v = P.op("vector", lambda e, r_=ring, b_=tb, fc_=fc: e.tensor_tensor(
                out=self.AF[:, b_:b_ + 32], in0=self.AB[:, r_ + 1:r_ + 64:2],
                in1=self.AF[:, fc_ + 32:fc_ + 64], op=ALU.mult), inc=VE)
            v = P.op("vector", lambda e, r_=ring, c_=tc, fc_=fc: e.tensor_tensor(
                out=self.AF[:, c_:c_ + 32], in0=self.AB[:, r_:r_ + 64:2],
                in1=self.AF[:, fc_ + 32:fc_ + 64], op=ALU.mult), inc=VE)
            v = P.op("vector", lambda e, r_=ring, d_=td, fc_=fc: e.tensor_tensor(
                out=self.AF[:, d_:d_ + 32], in0=self.AB[:, r_ + 1:r_ + 64:2],
                in1=self.AF[:, fc_:fc_ + 32], op=ALU.mult), inc=VE)
            v = P.op("vector", lambda e, r_=ring, a=ta, b_=tb: e.tensor_tensor(
                out=self.AB[:, r_:r_ + 64:2], in0=self.AF[:, a:a + 32], in1=self.AF[:, b_:b_ + 32],
                op=ALU.subtract), inc=VE)
            v = P.op("vector", lambda e, r_=ring, c_=tc, d_=td: e.tensor_tensor(
                out=self.AB[:, r_ + 1:r_ + 64:2], in0=self.AF[:, c_:c_ + 32], in1=self.AF[:, d_:d_ + 32],
                op=ALU.add), inc=VE)
            self.done(("lat_cs", par), VE, v)
            if sub < 3:
                self.done(("lat_ev", par), VE, v)
                self.done(("lat_ps", par), VE, v)
                continue
            trp = P.op("tensor", lambda t, r_=ring, par_=par: t.transpose(
                self.psb[:, par_ * 128:par_ * 128 + 128], self.AB[:, r_:r_ + 128], self.idb[:, :]),
                waits=[(VE, v)] + self.need(("psb", par)) + self.need("psb_bank"), inc=PE)
            v2 = P.op("vector", lambda e, par_=par: e.tensor_copy(
                out=self.AB[:, AB_QME + par_ * 512:AB_QME + par_ * 512 + 128],
                in_=self.psb[:, par_ * 128:par_ * 128 + 128]),
                waits=[(PE, trp)] + self.need(("lat_e2", par)), inc=VE)
            self.done(("psb", par), VE, v2)
            self.done("psb_bank", VE, v2)
            self.done(("lat_ev", par), VE, v2)
            d = self.dma("gpsimd", self.contribM[:, t128 * 128:(t128 + 1) * 128],
                         self.AB[:, AB_QME + par * 512:AB_QME + par * 512 + 128], waits=[(VE, v2)],
                         stream="lco%d" % par)
            self.done(("lat_e2", par), *d)

    def mla_q(self):
        P = self.P
        dke = None
        for hh in range(H):
            dke = self.dma("sync", self.AB[:, AB_KEXP + hh * 64:AB_KEXP + (hh + 1) * 64],
                           self.wbf["kexpT"][hh * 128:(hh + 1) * 128, :], stream="kexp")
        for vg in range(4):
            dp = self.dma("sync", self.AB[:, AB_PANEL:AB_PANEL + 8192],
                          self.wbf["attn_wq"][vg * 128:(vg + 1) * 128, :],
                          waits=self.need("pan"), stream="pan")
            for t128 in range(DC):
                par = t128 % 2
                pc = par * 512
                ring = AB_RING + par * 1024
                m = None
                for kc in range(DC):
                    waits = [dp, (SE, self.h1_se)]
                    if kc == 0:
                        waits += self.need(("q_ps", par))
                    m = P.op("tensor", lambda t, kc_=kc, t_=t128, pc_=pc: t.matmul(
                        self.psum[:, pc_:pc_ + 512],
                        self.AB[:, AB_H1 + kc_ * TPC + t_ * 128:AB_H1 + kc_ * TPC + t_ * 128 + 128],
                        self.AB[:, AB_PANEL + kc_ * 512:AB_PANEL + (kc_ + 1) * 512],
                        start=(kc_ == 0), stop=(kc_ == DC - 1)), waits=waits, inc=PE)
                s = P.op("scalar", lambda e, pc_=pc, r_=ring: e.copy(
                    out=self.AB[:, r_:r_ + 512], in_=self.psum[:, pc_:pc_ + 512]),
                    waits=[(PE, m)] + self.need(("q_ev", par)), inc=SE)
                self.done(("q_ps", par), SE, s)
                if int(os.environ.get("KSUB2", "9")) < 2:
                    self.done(("q_ev", par), SE, s)
                    self.done(("q_cs", par), SE, s)
                    self.done(("q_t", par), SE, s)
                    continue
                fc = AF_MISC + par * 1024
                dc_ = self.dma("sync", self.AF[:, fc:fc + 256],
                               self.rq_cos_in[t128 * 128:(t128 + 1) * 128, :],
                               waits=self.need(("q_cs", par)), stream="qcs%d" % par)
                ds_ = self.dma("sync", self.AF[:, fc + 256:fc + 512],
                               self.rq_sin_in[t128 * 128:(t128 + 1) * 128, :], stream="qcs%d" % par)
                ta, tb = AF_IO + 2048 + par * 1024, AF_IO + 2048 + par * 1024 + 256
                tc, td = AF_IO + 2048 + par * 1024 + 512, AF_IO + 2048 + par * 1024 + 768
                v = P.op("vector", lambda e, r_=ring, a=ta, fc_=fc: e.tensor_tensor(
                    out=self.AF[:, a:a + 256], in0=self.AB[:, r_:r_ + 512:2], in1=self.AF[:, fc_:fc_ + 256],
                    op=ALU.mult), waits=[(SE, s), ds_] + self.need(("q_t", par)), inc=VE)
                v = P.op("vector", lambda e, r_=ring, b_=tb, fc_=fc: e.tensor_tensor(
                    out=self.AF[:, b_:b_ + 256], in0=self.AB[:, r_ + 1:r_ + 512:2],
                    in1=self.AF[:, fc_ + 256:fc_ + 512], op=ALU.mult), inc=VE)
                v = P.op("vector", lambda e, r_=ring, c_=tc, fc_=fc: e.tensor_tensor(
                    out=self.AF[:, c_:c_ + 256], in0=self.AB[:, r_:r_ + 512:2],
                    in1=self.AF[:, fc_ + 256:fc_ + 512], op=ALU.mult), inc=VE)
                v = P.op("vector", lambda e, r_=ring, d_=td, fc_=fc: e.tensor_tensor(
                    out=self.AF[:, d_:d_ + 256], in0=self.AB[:, r_ + 1:r_ + 512:2],
                    in1=self.AF[:, fc_:fc_ + 256], op=ALU.mult), inc=VE)
                v = P.op("vector", lambda e, r_=ring, a=ta, b_=tb: e.tensor_tensor(
                    out=self.AB[:, r_:r_ + 512:2], in0=self.AF[:, a:a + 256], in1=self.AF[:, b_:b_ + 256],
                    op=ALU.subtract), inc=VE)
                v = P.op("vector", lambda e, r_=ring, c_=tc, d_=td: e.tensor_tensor(
                    out=self.AB[:, r_ + 1:r_ + 512:2], in0=self.AF[:, c_:c_ + 256], in1=self.AF[:, d_:d_ + 256],
                    op=ALU.add), inc=VE)
                self.done(("q_cs", par), VE, v)
                self.done(("q_t", par), VE, v)
                if int(os.environ.get("KSUB2", "9")) < 3:
                    self.done(("q_ev", par), VE, v)
                    continue
                for hl in range(4):
                    slot = (t128 * 4 + hl) % 8
                    trp = P.op("tensor", lambda t, r_=ring, hl_=hl, sl=slot: t.transpose(
                        self.psb[:, sl * 128:(sl + 1) * 128], self.AB[:, r_ + hl_ * 128:r_ + hl_ * 128 + 128],
                        self.idb[:, :]),
                        waits=[(VE, v)] + self.need(("psb", slot)) + self.need("psb_bank"), inc=PE)
                    v2 = P.op("vector", lambda e, hl_=hl, t_=t128, sl=slot: e.tensor_copy(
                        out=self.AB[:, AB_RQT + hl_ * 512 + (t_ % 4) * 128:
                                    AB_RQT + hl_ * 512 + (t_ % 4) * 128 + 128],
                        in_=self.psb[:, sl * 128:(sl + 1) * 128]),
                        waits=[(PE, trp)] + self.need(("rqt", hl, t128 % 4)), inc=VE)
                    self.done(("psb", slot), VE, v2)
                    self.done("psb_bank", VE, v2)
                    self.done(("q_t", par), VE, v2)
                    self.done(("rqt_v", hl, t128 % 4), VE, v2)
                if t128 % 4 == 3 and int(os.environ.get("KSUB2", "9")) >= 4:
                    tq = t128 // 4
                    bb = tq // 2
                    for hl in range(4):
                        h = vg * 4 + hl
                        wv = self.need(("rqt_v", hl, 0)) + self.need(("rqt_v", hl, 1)) + \
                            self.need(("rqt_v", hl, 2)) + self.need(("rqt_v", hl, 3)) + [dke]
                        mq = P.op("tensor", lambda t, h_=h, hl_=hl: t.matmul(
                            self.psum[0:DL, 1024:1536],
                            self.AB[:, AB_KEXP + h_ * 64:AB_KEXP + (h_ + 1) * 64],
                            self.AB[:, AB_RQT + hl_ * 512:AB_RQT + (hl_ + 1) * 512],
                            start=True, stop=True), waits=wv + self.need("qm_ps") + self.need("qn_rd"), inc=PE)
                        s2 = P.op("scalar", lambda e: e.copy(
                            out=self.AB[0:DL, AB_QME:AB_QME + 512], in_=self.psum[0:DL, 1024:1536]),
                            waits=[(PE, mq)] + self.need("qme"), inc=SE)
                        sq = P.op("scalar", lambda e: e.activation(
                            out=self.AB[0:DL, AB_SQ2:AB_SQ2 + 512], in_=self.psum[0:DL, 1024:1536],
                            func=AF_.Square), waits=[(PE, mq)], inc=SE)
                        self.done("qm_ps", SE, sq)
                        mn = P.op("tensor", lambda t: t.matmul(
                            self.psum[0:1, 1536:2048], self.ones_b[0:DL, :], self.AB[0:DL, AB_SQ2:AB_SQ2 + 512],
                            start=True, stop=True), waits=[(SE, sq)] + self.need("qn_ps"), inc=PE)
                        if tq % 2 == 0:
                            nv = P.op("vector", lambda e, b_=bb, h_=h: e.reduce_max(
                                out=self.rq2m[0:1, b_ * 16 + h_:b_ * 16 + h_ + 1],
                                in_=self.psum[0:1, 1536:2048], axis=AXX), waits=[(PE, mn)], inc=VE)
                            self.done("qn_rd", VE, nv)
                        else:
                            nv = P.op("vector", lambda e: e.reduce_max(
                                out=self.tiny[0:1, 48:49], in_=self.psum[0:1, 1536:2048], axis=AXX),
                                waits=[(PE, mn)], inc=VE)
                            nv = P.op("vector", lambda e, b_=bb, h_=h: e.tensor_tensor(
                                out=self.rq2m[0:1, b_ * 16 + h_:b_ * 16 + h_ + 1],
                                in0=self.rq2m[0:1, b_ * 16 + h_:b_ * 16 + h_ + 1],
                                in1=self.tiny[0:1, 48:49], op=ALU.max), inc=VE)
                            self.done("qn_rd", VE, nv)
                        self.done("qn_ps", VE, nv)
                        d = self.dma("gpsimd",
                                     self.qm_scr[(bb * H + h) * DL:(bb * H + h) * DL + DL,
                                                 (tq % 2) * 512:(tq % 2) * 512 + 512],
                                     self.AB[0:DL, AB_QME:AB_QME + 512], waits=[(SE, s2)],
                                     stream="qme")
                        self.done("qme", *d)
                    for hl in range(4):
                        for qq in range(4):
                            self.done(("rqt", hl, qq), PE, P.counts[PE])
            self.done("pan", PE, P.counts[PE])
        self.done(("q_ps", 0), SE, self.P.counts[SE])
        self.done(("q_ps", 1), SE, self.P.counts[SE])

    def mla_attn(self):
        P = self.P
        cc = self.cc_cur
        exp_at_slot = {}
        pv_at_probs = {}
        for bb in range(B):
            dk = None
            for rr in range(NCORE):
                self.dma("sync", self.AB[0:DL, AB_KLAT + rr * 1024:AB_KLAT + (rr + 1) * 1024],
                         self.gathM[rr * 128:rr * 128 + DL, bb * NQB:(bb + 1) * NQB],
                         waits=[(cc, 1)] + self.need("mlakv"), stream="mkv")
                dk = self.dma("sync", self.AB[0:DL, AB_VLT + rr * 1024:AB_VLT + (rr + 1) * 1024],
                              self.gathM[rr * 128 + DL:(rr + 1) * 128, bb * NQB:(bb + 1) * NQB],
                              stream="mkv")
            sqk = P.op("scalar", lambda e: e.activation(out=self.AB[0:DL, AB_SQK:AB_SQK + 8192],
                                                        in_=self.AB[0:DL, AB_KLAT:AB_KLAT + 8192],
                                                        func=AF_.Square), waits=[dk], inc=SE)
            nv = None
            for i in range(16):
                mn = P.op("tensor", lambda t, i_=i: t.matmul(
                    self.psum[0:1, 1536:2048], self.ones_b[0:DL, :],
                    self.AB[0:DL, AB_SQK + i_ * 512:AB_SQK + (i_ + 1) * 512],
                    start=True, stop=True),
                    waits=[(SE, sqk)] + ([(VE, nv)] if nv else []) + self.need("mh_m2"), inc=PE)
                if i == 0:
                    nv = P.op("vector", lambda e, b_=bb: e.reduce_max(
                        out=self.rk2m[0:1, b_:b_ + 1], in_=self.psum[0:1, 1536:2048], axis=AXX),
                        waits=[(PE, mn)], inc=VE)
                else:
                    nv = P.op("vector", lambda e: e.reduce_max(
                        out=self.tiny[0:1, 50:51], in_=self.psum[0:1, 1536:2048], axis=AXX),
                        waits=[(PE, mn)], inc=VE)
                    nv = P.op("vector", lambda e, b_=bb: e.tensor_tensor(
                        out=self.rk2m[0:1, b_:b_ + 1], in0=self.rk2m[0:1, b_:b_ + 1],
                        in1=self.tiny[0:1, 50:51], op=ALU.max), inc=VE)
            vm = P.op("vector", lambda e: e.memset(self.AB[:, AB_VT:AB_VT + NKT * 65], 1.0),
                      waits=self.need("vt_ws"), inc=VE)
            vtv = None
            for j in range(NKT):
                slot = j % 8
                trp = P.op("tensor", lambda t, j_=j, sl=slot: t.transpose(
                    self.psb[:, sl * 128:sl * 128 + 64],
                    self.AB[0:DL, AB_VLT + (j_ % 8) * 1024 + (j_ // 8) * 128:
                            AB_VLT + (j_ % 8) * 1024 + (j_ // 8) * 128 + 128],
                    self.idb[0:DL, 0:DL]),
                    waits=[dk, (VE, vm)] + self.need(("psb", slot)) + self.need("psb_bank"), inc=PE)
                vtv = P.op("vector", lambda e, j_=j, sl=slot: e.tensor_copy(
                    out=self.AB[:, AB_VT + j_ * 65:AB_VT + j_ * 65 + 64],
                    in_=self.psb[:, sl * 128:sl * 128 + 64]), waits=[(PE, trp)], inc=VE)
                self.done(("psb", slot), VE, vtv)
                self.done("psb_bank", VE, vtv)
            for h in range(H):
                buf = h % 2
                dq = self.dma("sync", self.AB[0:DL, AB_QM + buf * 1024:AB_QM + (buf + 1) * 1024],
                              self.qm_scr[(bb * H + h) * DL:(bb * H + h) * DL + DL, :],
                              waits=self.need(("qm_b", buf)), stream="qmb%d" % buf)
                v = P.op("vector", lambda e, b_=bb, h_=h: e.tensor_tensor(
                    out=self.tiny[0:1, 52:53], in0=self.rq2m[0:1, b_ * 16 + h_:b_ * 16 + h_ + 1],
                    in1=self.rk2m[0:1, b_:b_ + 1], op=ALU.add), waits=[(VE, nv)], inc=VE)
                v = P.op("vector", lambda e: e.tensor_scalar(
                    out=self.tiny[0:1, 53:54], in0=self.tiny[0:1, 52:53],
                    scalar1=float(-0.5 * SCALE), scalar2=None, op0=ALU.mult), inc=VE)
                mb = P.op("tensor", lambda t: t.matmul(
                    self.psum[:, 1536:1537], self.ones_f1[:, :], self.tiny[0:1, 53:54],
                    start=True, stop=True), waits=[(VE, v)] + self.need("mh_m") + self.need("mh_m2"),
                    inc=PE)
                mhv = P.op("vector", lambda e, c_=buf: e.tensor_copy(
                    out=self.mh[:, c_:c_ + 1], in_=self.psum[:, 1536:1537]), waits=[(PE, mb)], inc=VE)
                self.done("mh_m", VE, mhv)
                ring = 0
                probs_info = {}
                acc_last = None
                for j in range(NKT):
                    jo = j % 8
                    for si, (qs, n) in enumerate(slabs_for(j)):
                        scol = 2048 + 512 * (ring % 3)
                        ring += 1
                        pcol = AB_PROBS + ((si * 2 + (j % 2)) % 4) * 512
                        waits = [dq, (VE, mhv), (SE, exp_at_slot.get(scol, 0))]
                        pe_c = P.op("tensor", lambda t, sc=scol, jj=j, q0=qs, nn=n, b2=buf:
                                    t.matmul(self.psum[:, sc:sc + nn],
                                             self.AB[0:DL, AB_KLAT + (jj % 8) * 1024 + (jj // 8) * 128:
                                                     AB_KLAT + (jj % 8) * 1024 + (jj // 8) * 128 + 128],
                                             self.AB[0:DL, AB_QM + b2 * 1024 + q0:AB_QM + b2 * 1024 + q0 + nn],
                                             start=True, stop=True), waits=waits, inc=PE)
                        waits = [(PE, pe_c), (PE, pv_at_probs.get(pcol, 0))]
                        se_c = P.op("scalar", lambda s, sc=scol, pc=pcol, c_=buf, nn=n:
                                    s.activation(out=self.AB[:, pc:pc + nn], in_=self.psum[:, sc:sc + nn],
                                                 func=AF_.Exp, bias=self.mh[:, c_:c_ + 1], scale=SCALE),
                                    waits=waits, inc=SE)
                        exp_at_slot[scol] = se_c
                        ve_c = 0
                        if qs == 128 * (j // 8):
                            ve_c = P.op("vector", lambda v2, pc=pcol, jo_=jo: v2.tensor_tensor(
                                out=self.AB[:, pc:pc + 128], in0=self.AB[:, pc:pc + 128],
                                in1=self.masks[:, jo_ * 128:(jo_ + 1) * 128], op=ALU.mult),
                                waits=[(SE, se_c)], inc=VE)
                        probs_info[(j, si)] = (se_c, ve_c, pcol, qs, n)
                    for si, (qs, n) in enumerate(slabs_for(j)):
                        se_c, ve_c, pcol, _, _ = probs_info[(j, si)]
                        waits = [(SE, se_c), (VE, vtv)]
                        if ve_c:
                            waits.append((VE, ve_c))
                        if j == 0:
                            waits += self.need("macc")
                        if jo == 7 and qs == 128 * (j // 8):
                            pieces = [(qs, 128, True)]
                            if n > 128:
                                pieces.append((qs + 128, n - 128, j == NKT - 1))
                        else:
                            pieces = [(qs, n, j == NKT - 1)]
                        for (pq, pn, pstop) in pieces:
                            acc_last = P.op("tensor", lambda t, jj=j, pc=pcol, q0=pq, nn=pn, qs_=qs, st=pstop:
                                            t.matmul(self.psum[0:65, q0:q0 + nn],
                                                     self.AB[:, AB_VT + jj * 65:AB_VT + (jj + 1) * 65],
                                                     self.AB[:, pc + (q0 - qs_):pc + (q0 - qs_) + nn],
                                                     start=(jj == 0), stop=st, skip_group_check=True),
                                            waits=waits, inc=PE)
                        pv_at_probs[pcol] = acc_last
                vrl = P.op("vector", lambda e: e.reciprocal(out=self.AF[0:1, AF_MISC:AF_MISC + 1024],
                                                            in_=self.psum[64:65, 0:1024]),
                           waits=[(PE, acc_last)], inc=VE)
                bc = None
                for i2 in range(2):
                    bc = P.op("tensor", lambda t, i_=i2: t.matmul(
                        self.psum[0:DL, 1024 + i_ * 512:1024 + i_ * 512 + 512], self.ones_f1[:, 0:DL],
                        self.AF[0:1, AF_MISC + i_ * 512:AF_MISC + (i_ + 1) * 512],
                        start=True, stop=True), waits=[(VE, vrl)] + (self.need("mrb") if i2 == 0 else []),
                        inc=PE)
                vb = P.op("vector", lambda e: e.tensor_copy(out=self.AF[0:DL, AF_MISC:AF_MISC + 1024],
                                                            in_=self.psum[0:DL, 1024:2048]),
                          waits=[(PE, bc)], inc=VE)
                vy = P.op("vector", lambda e: e.tensor_tensor(
                    out=self.AB[0:DL, AB_YN:AB_YN + 1024], in0=self.psum[0:DL, 0:1024],
                    in1=self.AF[0:DL, AF_MISC:AF_MISC + 1024], op=ALU.mult),
                    waits=[(VE, vb)] + self.need("yn"), inc=VE)
                self.done("macc", VE, vy)
                dv = self.dma("sync", self.AB[0:DL, AB_VEXP:AB_VEXP + 128],
                              self.wbf["vexp"][h * DL:(h + 1) * DL, :], waits=self.need("vexp"),
                              stream="vex")
                my = P.op("tensor", lambda t: t.matmul(
                    self.psum[:, 1024:1536], self.AB[0:DL, AB_VEXP:AB_VEXP + 128],
                    self.AB[0:DL, AB_YN:AB_YN + 512], start=True, stop=True),
                    waits=[(VE, vy), dv, (VE, vb)], inc=PE)
                my2 = P.op("tensor", lambda t: t.matmul(
                    self.psum[:, 1536:2048], self.AB[0:DL, AB_VEXP:AB_VEXP + 128],
                    self.AB[0:DL, AB_YN + 512:AB_YN + 1024], start=True, stop=True),
                    waits=self.need("mh_m2"), inc=PE)
                self.done("mrb", PE, my2)
                self.done("vexp", PE, my2)
                sy = P.op("scalar", lambda e: e.copy(out=self.AB[:, AB_MYE:AB_MYE + 1024],
                                                     in_=self.psum[:, 1024:2048]),
                          waits=[(PE, my2)] + self.need("ydma"), inc=SE)
                self.done("yn", PE, my2)
                self.done("mh_m2", SE, sy)
                dy = self.dma("gpsimd", self.ctx_scr[h * 128:(h + 1) * 128, bb * NQB:(bb + 1) * NQB],
                              self.AB[:, AB_MYE:AB_MYE + 1024], waits=[(SE, sy)], stream="ymy")
                self.done("ydma", *dy)
                self.done(("qm_b", buf), PE, acc_last)
            self.done("mlakv", PE, self.P.counts[PE])
            self.done("vt_ws", PE, self.P.counts[PE])


def build_nc():
    nc = bass.Bass("TRN2", target_bir_lowering=False, debug=False)
    Builder(nc).build()
    return nc


def _host_inputs(inputs, r):
    asc = np.ascontiguousarray
    f32 = np.float32
    d = {}
    x = inputs["L0_attn_x"]
    d["x0"] = asc(x.reshape(2, 64, 128, 2048)[:, r::8].transpose(3, 0, 1, 2).reshape(2048, 2048))
    return d


_SHARED_CACHE = {}


def _host_shared(inputs):
    if _SHARED_CACHE.get("done"):
        return _SHARED_CACHE
    asc = np.ascontiguousarray
    f32 = np.float32
    s = _SHARED_CACHE
    s["ffn_wqT"] = asc(inputs["L0_ffn_wq"].T)
    s["ffn_wkT"] = asc(inputs["L0_ffn_wk"].T)
    s["ffn_wvT"] = asc(inputs["L0_ffn_wv"].T)
    s["ffn_woT"] = asc(inputs["L0_ffn_wo"].T)
    s["ff1T"] = asc(inputs["L0_ffn_w_ff1"].T)
    s["ff2T"] = asc(inputs["L0_ffn_w_ff2"].T)
    s["wdownT"] = asc(inputs["L0_ffn_w_down"].T)
    s["wupT"] = asc(inputs["L0_ffn_w_up"].T)
    s["attn_wq"] = asc(inputs["L0_attn_wq"])
    s["attn_proj"] = asc(inputs["L0_attn_proj_w"])
    s["wkvlat"] = asc(np.concatenate([inputs["L0_attn_wk_lat"], inputs["L0_attn_wv_lat"]], axis=1))
    s["kexpT"] = asc(inputs["L0_attn_k_expand"].transpose(0, 2, 1).reshape(H * DH, DL))
    s["vexp"] = asc(inputs["L0_attn_v_expand"].reshape(H * DL, DH))
    wrT = inputs["L0_ffn_w_router"].T
    s["wrouter_sb"] = asc(wrT.reshape(16, 128, 3).transpose(1, 0, 2).reshape(128, 48))
    for name, key in [("bias_v", "L0_ffn_simple_bias"), ("ln1s_v", "L0_ffn_ln1_scale"),
                      ("ln1b_v", "L0_ffn_ln1_bias"), ("ln2s_v", "L0_ffn_ln2_scale"),
                      ("ln2b_v", "L0_ffn_ln2_bias")]:
        s[name] = asc(inputs[key].reshape(16, 128).T)
    s["ident_f"] = np.eye(128, dtype=f32)
    s["ident_b"] = np.eye(128, dtype=f32).astype(BFNP)
    inv_q = (1.0 / 10000.0 ** (np.arange(0, DH, 2, dtype=f32) / DH)).astype(f32)
    inv_k = (1.0 / 10000.0 ** (np.arange(0, DL, 2, dtype=f32) / DL)).astype(f32)
    tpos = np.arange(T, dtype=f32)
    angq = tpos[:, None] * inv_q[None, :]
    angk = tpos[:, None] * inv_k[None, :]
    s["cos_q"] = np.cos(angq, dtype=f32)
    s["sin_q"] = np.sin(angq, dtype=f32)
    s["cos_k"] = np.cos(angk, dtype=f32)
    s["sin_k"] = np.sin(angk, dtype=f32)
    s["done"] = True
    return s


def _core_inputs(inputs, s, r):
    asc = np.ascontiguousarray
    f32 = np.float32
    d = {k: s[k] for k in ("ffn_wqT", "ffn_wkT", "ffn_wvT", "ffn_woT", "ff1T", "ff2T",
                           "wdownT", "wupT", "attn_wq", "attn_proj", "wkvlat", "kexpT",
                           "vexp", "wrouter_sb", "bias_v", "ln1s_v", "ln1b_v", "ln2s_v",
                           "ln2b_v", "ident_f", "ident_b")}
    x = inputs["L0_attn_x"]
    d["x0"] = asc(x.reshape(2, 64, 128, 2048)[:, r::8].transpose(3, 0, 1, 2).reshape(2048, 2048))
    kk = np.arange(128)[:, None]
    qq = np.arange(128)[None, :]
    m = np.zeros((8, 128, 128), f32)
    for jo in range(8):
        m[jo] = (qq + (r - jo) * 128 >= kk)
    d["masks_i"] = m.astype(BFNP)
    mm, ii = np.meshgrid(np.arange(8), np.arange(128), indexing="ij")
    pos1 = ((8 * mm + r) * 128 + ii).reshape(-1)
    pos = np.concatenate([pos1, pos1])
    d["rq_cos"] = asc(np.tile(s["cos_q"][pos], (1, 4)))
    d["rq_sin"] = asc(np.tile(s["sin_q"][pos], (1, 4)))
    d["rk_cos"] = asc(s["cos_k"][pos])
    d["rk_sin"] = asc(s["sin_k"][pos])
    return d


def kernel(**inputs):
    s = dict()
    _SHARED_CACHE.clear()
    sh = _host_shared(inputs)
    in_maps = [_core_inputs(inputs, sh, r) for r in range(NCORE)]
    nc = build_nc()
    res = run_bass_kernel_spmd(nc, in_maps, core_ids=list(range(NCORE)))
    y = np.empty((2, 8192, 2048), np.float32)
    yv = y.reshape(2, 64, 128, 2048)
    for r in range(NCORE):
        o = res.results[r]["out"]
        yv[:, r::8] = o.reshape(2048, 2, 8, 128).transpose(1, 2, 3, 0)
    if DEBUG:
        kernel.last_results = res
    return y


# version 03
# speedup vs baseline: 1.1796x; 1.1541x over previous
import os
import numpy as np
import ml_dtypes
import concourse.bass as bass
import concourse.mybir as mybir
from concourse.bass_utils import run_bass_kernel_spmd
from contextlib import ExitStack

F32 = mybir.dt.float32
BF16 = mybir.dt.bfloat16
AF_ = mybir.ActivationFunctionType
ALU = mybir.AluOpType
AXX = mybir.AxisListType.X
BFNP = ml_dtypes.bfloat16

B, T, D, H, DH, DL, FFD, RLOW = 2, 8192, 2048, 16, 128, 64, 8192, 32
NCORE = 8
TPC = 2048
NQB = 1024
DC = 16
NKT = 64
SCALE = float(1.0 / np.sqrt(128.0))
EPS = 1e-5

DEBUG = False
ENGINES = ("sync", "tensor", "vector", "scalar", "gpsimd")
DMA, DMB, PE, SE, VE = "dma", "dmb", "pe", "se", "ve"


class Plan:
    def __init__(self):
        self.ops = {e: [] for e in ENGINES}
        self.counts = {PE: 0, SE: 0, VE: 0}
        self._barrier = {}

    def op(self, engine, fn, waits=(), inc=None, amt=1):
        w = list(waits)
        if self._barrier.get(engine):
            w = list(self._barrier[engine]) + w
            self._barrier[engine] = None
        if engine in ("vector", "scalar") and inc is not None:
            prev = self.counts.get(inc, 0)
            if prev:
                w.append((inc, prev))
        if inc is not None:
            self.counts[inc] = self.counts.get(inc, 0) + amt
        self.ops[engine].append((fn, tuple(w), inc, amt))
        return self.counts.get(inc) if inc is not None else None

    def barrier(self):
        snap = [(s, c) for s, c in self.counts.items() if c > 0]
        for e in ENGINES:
            self._barrier[e] = snap

    def emit(self, block, sems):
        for ename in ENGINES:
            ops = self.ops[ename]
            if not ops:
                continue
            dec = getattr(block, ename)

            def section(eng, _ops=ops):
                waited = {}
                for fn, waits, inc, amt in _ops:
                    for sname, val in waits:
                        if val and val > 0 and waited.get(sname, 0) < val:
                            eng.wait_ge(sems[sname], val)
                            waited[sname] = val
                    ins = fn(eng)
                    if inc is not None:
                        ins.then_inc(sems[inc], amt)
            dec(section)


def slabs_for(j):
    qs = 128 * (j // 8)
    out = []
    if qs < 512:
        out.append((qs, 512 - qs))
    if 1024 - max(qs, 512) > 0:
        out.append((max(qs, 512), 1024 - max(qs, 512)))
    return out


AB_H1 = 0
AB_PANEL = 32768
AB_EVAC = 49152
AB_WDN = 32768
AB_WUP = 33280
AB_LR2E = 35328
AB_KSB = 0
AB_PROBS = 32768
AB_PRES = 36864
AB_VR = 38912
AB_Q = 39936
AB_SB1 = 44032
AB_YS = 46080
AB_CTX = 0
AB_SLAB = 32768
AB_G = 0
AB_XN2 = 32768
AB_FSLAB = 40960
AB_SBC = 49152
AB_LR2T = 50176
AB_KVPAN = 32768
AB_KEXP = 40960
AB_RQT = 41984
AB_RING = 44032
AB_QME = 46080
AB_SQ2 = 47104
AB_MYE = 45056
AB_KLAT = 0
AB_VLT = 8192
AB_SQK = 16384
AB_VT = 24576
AB_QM = 28736
AB_YN = 30784
AB_VEXP = 31808

AF_IO = 0
AF_MUB = 4096
AF_RSB = 6144
AF_MU2B = 8192
AF_RS2B = 10240
AF_MISC = 12288


class Builder:
    def __init__(self, nc):
        self.nc = nc
        self.P = Plan()
        self.war = {}
        self.ccs = ["cc0", "cc1", "cc2"]

    def need(self, key):
        v = self.war.get(key)
        return list(v.items()) if v else []

    def done(self, key, sem, count):
        d = self.war.setdefault(key, {})
        if count and count > d.get(sem, 0):
            d[sem] = count

    def dma(self, q, dst, src, waits=(), stream="misc"):
        sem = "D" + stream
        c = self.P.op(q, lambda e, d=dst, s=src: e.dma_start(out=d, in_=s),
                      waits=waits, inc=sem, amt=16)
        return (sem, c)

    def build(self):
        nc = self.nc
        ap = lambda name, shape, dt, kind="ExternalInput": nc.dram_tensor(name, shape, dt, kind=kind).ap()
        self.x0 = ap("x0", [D, TPC], F32)
        self.out = ap("out", [D, TPC], F32, "ExternalOutput")
        self.w_in = {}
        for name, shape in [
            ("ffn_wqT", [D, D]), ("ffn_wkT", [D, D]), ("ffn_wvT", [D, D]), ("ffn_woT", [D, D]),
            ("ff1T", [D, FFD]), ("ff2T", [FFD, D]),
            ("wdownT", [D, RLOW]), ("wupT", [RLOW, D]),
            ("attn_wq", [D, D]), ("attn_proj", [D, D]), ("wkvlat", [D, 2 * DL]),
            ("kexpT", [H * DH, DL]), ("vexp", [H * DL, DH]),
        ]:
            self.w_in[name] = ap(name, shape, F32)
        self.wrouter_in = ap("wrouter_sb", [128, DC * 3], F32)
        self.vec_in = {n: ap(n, [128, DC], F32)
                       for n in ("bias_v", "ln1s_v", "ln1b_v", "ln2s_v", "ln2b_v")}
        self.ident_f_in = ap("ident_f", [128, 128], F32)
        self.ident_b_in = ap("ident_b", [128, 128], BF16)
        self.masks_in = ap("masks_i", [8, 128, 128], BF16)
        self.rq_cos_in = ap("rq_cos", [TPC, 256], F32)
        self.rq_sin_in = ap("rq_sin", [TPC, 256], F32)
        self.rk_cos_in = ap("rk_cos", [TPC, 32], F32)
        self.rk_sin_in = ap("rk_sin", [TPC, 32], F32)

        dt_ = lambda name, shape, dt, **kw: nc.dram_tensor(name, shape, dt, kind="Internal", **kw).ap()
        self.tiled = {"ffn_wqT", "ffn_wkT", "ffn_wvT", "ffn_woT", "attn_wq", "attn_proj",
                      "ff1T", "ff2T"}
        def bf_shape(n, t):
            if n in self.tiled:
                rows, cols = t.shape
                return [(cols // 512) * 128, (rows // 128) * 512]
            return list(t.shape)
        self.wbf = {n: dt_("bf_" + n, bf_shape(n, t), BF16) for n, t in self.w_in.items()}
        self.x_a = dt_("x_a", [D, TPC], F32)
        self.x_b = dt_("x_b", [D, TPC], F32)
        self.xt_scr = dt_("xt_scr", [D, TPC], F32)
        self.lr2_scr = dt_("lr2_scr", [D, TPC], BF16)
        self.q_scr = dt_("q_scr", [B * H * 128, NQB], BF16)
        self.qm_scr = dt_("qm_scr", [B * H * DL, NQB], BF16)
        self.ctx_scr = dt_("ctx_scr", [D, TPC], BF16)
        self.CF_K, self.CF_KN, self.CF_V, self.CF_ROWS = 0, 2048, 2064, 4112
        self.contribF = dt_("contribF", [self.CF_ROWS, TPC], BF16)
        self.gathF = dt_("gathF", [NCORE * self.CF_ROWS, TPC], BF16, addr_space="Shared")
        self.contribM = dt_("contribM", [128, TPC], BF16)
        self.gathM = dt_("gathM", [NCORE * 128, TPC], BF16, addr_space="Shared")
        self.dbg = {}
        self.dbg_out = {}
        if DEBUG:
            for n, shape, dt2 in [("x1", [D, TPC], F32), ("xt", [D, TPC], F32),
                                  ("ctx", [D, TPC], BF16), ("q", [B * H * 128, NQB], BF16),
                                  ("contrib", [self.CF_ROWS, TPC], BF16),
                                  ("lr2", [D, TPC], BF16), ("smask", [1, 2 * TPC], BF16),
                                  ("x2", [D, TPC], F32), ("ctxm", [D, TPC], BF16),
                                  ("qm", [B * H * DL, NQB], BF16), ("contribm", [128, TPC], BF16)]:
                self.dbg[n] = dt_("dbg_" + n, shape, dt2)
                self.dbg_out[n] = ap("dout_" + n, shape, dt2, "ExternalOutput")

        with ExitStack() as st:
            E = st.enter_context
            sb = lambda name, shape, dt: E(nc.sbuf_tensor(name, shape, dt))
            self.masks = sb("masks", [128, 8 * 128], BF16)
            self.idf = sb("idf", [128, 128], F32)
            self.idb = sb("idb", [128, 128], BF16)
            self.ones_b = sb("ones_b", [128, 1], BF16)
            self.ones_b1 = sb("ones_b1", [1, 128], BF16)
            self.ones_f1 = sb("ones_f1", [1, 128], F32)
            self.ones_f128 = sb("ones_f128", [128, 1], F32)
            self.vec_sb = {n: sb("sb_" + n, [128, DC], F32) for n in self.vec_in}
            self.wrouter = sb("wrouter", [128, DC * 3], F32)
            self.statn = sb("statn", [1, 2 * TPC], F32)
            self.smask = sb("smask", [1, 2 * TPC], BF16)
            self.rq2 = sb("rq2", [1, 64], F32)
            self.rq2m = sb("rq2m", [1, 64], F32)
            self.rk2m = sb("rk2m", [1, 4], F32)
            self.mh = sb("mh", [128, 4], F32)
            self.tiny = sb("tiny", [1, 64], F32)
            self.epsv = sb("epsv", [1, 1], F32)
            self.AB = sb("AB", [128, 53248], BF16)
            self.AF = sb("AF", [128, 14336], F32)
            self.psum = E(nc.psum_tensor("psum", [128, 3584], F32))
            self.psb = E(nc.psum_tensor("psb", [128, 1024], BF16))

            self.emit_all()

            sems = {}
            for name in sorted(self.P.counts):
                sems[name] = E(nc.semaphore(name.replace(":", "_")))
            block = E(nc.Block())
            self.P.emit(block, sems)
        return nc

    def tap(self, name, src):
        if not DEBUG or name not in self.dbg:
            return
        w = [(s, c) for s, c in self.P.counts.items() if c > 0]
        d = self.dma("gpsimd", self.dbg[name], src, waits=w, stream="tap")
        self.dma("gpsimd", self.dbg_out[name], self.dbg[name], waits=[d], stream="tap")

    class _Stop(Exception):
        pass

    def ck(self):
        self._phase += 1
        if self._phase > int(os.environ.get("KSTOP", "99")):
            raise Builder._Stop()

    def emit_all(self):
        self._phase = 0
        try:
            if os.environ.get("KONLY") == "lat":
                P = self.P
                self.load_consts()
                self.prep_weights()
                P.barrier()
                self.stats_pass(self.x0)
                P.barrier()
                self.finalize_stats(AF_MUB, AF_RSB)
                P.barrier()
                self.h_pass(self.x0, router=False)
                P.barrier()
                self.mla_latents()
                raise Builder._Stop()
            self._emit_all()
        except Builder._Stop:
            pass
        fin = [(s, c) for s, c in self.P.counts.items() if c > 0]
        self.P.op("gpsimd", lambda g: g.engine_nop(), waits=fin)

    def _emit_all(self):
        P = self.P
        self.load_consts()
        self.ck()
        self.prep_weights()
        P.barrier()
        self.ck()
        self.stats_pass(self.x0)
        P.barrier()
        self.finalize_stats(AF_MUB, AF_RSB)
        P.barrier()
        self.ck()
        self.ffn_sublayer(self.x0, self.x_a, final=False)
        self.tap("x1", self.x_a)
        P.barrier()
        self.finalize_stats(AF_MUB, AF_RSB)
        P.barrier()
        self.ck()
        self.mla_sublayer(self.x_a, self.x_b)
        self.tap("x2", self.x_b)
        P.barrier()
        self.finalize_stats(AF_MUB, AF_RSB)
        P.barrier()
        self.ck()
        self.ffn_sublayer(self.x_b, self.out, final=True)

    def load_consts(self):
        for jo in range(8):
            self.dma("sync", self.masks[:, jo * 128:(jo + 1) * 128], self.masks_in[jo], stream="consts")
        self.dma("sync", self.idf[:, :], self.ident_f_in, stream="consts")
        self.dma("sync", self.idb[:, :], self.ident_b_in, stream="consts")
        for n in self.vec_in:
            self.dma("sync", self.vec_sb[n][:, :], self.vec_in[n], stream="consts")
        self.dma("sync", self.wrouter[:, :], self.wrouter_in, stream="consts")
        P = self.P
        P.op("vector", lambda v: v.memset(self.ones_b[:], 1.0), inc=VE)
        P.op("vector", lambda v: v.memset(self.ones_b1[:], 1.0), inc=VE)
        P.op("vector", lambda v: v.memset(self.ones_f1[:], 1.0), inc=VE)
        P.op("vector", lambda v: v.memset(self.ones_f128[:], 1.0), inc=VE)
        P.op("vector", lambda v: v.memset(self.epsv[:], float(EPS)), inc=VE)

    def prep_weights(self):
        P = self.P
        par = 0
        for name, src in self.w_in.items():
            dst = self.wbf[name]
            rows, cols = src.shape
            for ch in range((rows + 127) // 128):
                r0 = ch * 128
                pr = min(128, rows - r0)
                for c0 in range(0, cols, 4096):
                    w = min(4096, cols - c0)
                    p = par % 2
                    par += 1
                    fo = p * 4096
                    bo = AB_PANEL + p * 4096
                    d = self.dma("sync", self.AF[0:pr, fo:fo + w], src[r0:r0 + pr, c0:c0 + w],
                                 waits=self.need(("wp", p)), stream="wpi%d" % p)
                    v = P.op("vector", lambda e, pr_=pr, f_=fo, b_=bo, w_=w:
                             e.tensor_copy(out=self.AB[0:pr_, b_:b_ + w_], in_=self.AF[0:pr_, f_:f_ + w_]),
                             waits=[d], inc=VE)
                    if name in self.tiled:
                        d2 = None
                        kc = ch
                        for oi in range(w // 512):
                            og = (c0 // 512) + oi
                            d2 = self.dma("gpsimd",
                                          dst[og * 128:(og + 1) * 128, kc * 512:(kc + 1) * 512],
                                          self.AB[0:pr, bo + oi * 512:bo + (oi + 1) * 512],
                                          waits=[(VE, v)], stream="wpo%d" % p)
                    else:
                        d2 = self.dma("gpsimd", dst[r0:r0 + pr, c0:c0 + w], self.AB[0:pr, bo:bo + w],
                                      waits=[(VE, v)], stream="wpo%d" % p)
                    self.done(("wp", p), *d2)

    def stats_pass(self, xsrc):
        P = self.P
        for tg in range(4):
            m2 = None
            for kc in range(DC):
                p = kc % 2
                fo = AF_IO + p * 512
                d = self.dma("sync", self.AF[:, fo:fo + 512],
                             xsrc[kc * 128:(kc + 1) * 128, tg * 512:(tg + 1) * 512],
                             waits=self.need(("st", p)), stream="stx%d" % p)
                s = P.op("scalar", lambda e, f_=fo: e.activation(
                    out=self.AF[:, f_ + 1024:f_ + 1536], in_=self.AF[:, f_:f_ + 512], func=AF_.Square),
                    waits=[d], inc=SE)
                P.op("tensor", lambda t, f_=fo, first=(kc == 0), last=(kc == DC - 1): t.matmul(
                    self.psum[0:1, 0:512], self.ones_f128[:, :], self.AF[:, f_:f_ + 512],
                    start=first, stop=last),
                    waits=[d] + (self.need("st_ev") if kc == 0 else []), inc=PE)
                m2 = P.op("tensor", lambda t, f_=fo, first=(kc == 0), last=(kc == DC - 1): t.matmul(
                    self.psum[0:1, 512:1024], self.ones_f128[:, :], self.AF[:, f_ + 1024:f_ + 1536],
                    start=first, stop=last), waits=[(SE, s)], inc=PE)
                self.done(("st", p), PE, m2)
            v = P.op("vector", lambda e, tg_=tg: e.tensor_copy(
                out=self.statn[:, tg_ * 512:tg_ * 512 + 512], in_=self.psum[0:1, 0:512]),
                waits=[(PE, m2)], inc=VE)
            v = P.op("vector", lambda e, tg_=tg: e.tensor_copy(
                out=self.statn[:, TPC + tg_ * 512:TPC + tg_ * 512 + 512], in_=self.psum[0:1, 512:1024]),
                waits=[(PE, m2)], inc=VE)
            self.done(("st", 0), VE, v)
            self.done(("st", 1), VE, v)
            self.done("st_ev", VE, v)

    def finalize_stats(self, mub_col, rsb_col):
        P = self.P
        v = P.op("vector", lambda e: e.tensor_scalar(out=self.statn[:, :], in0=self.statn[:, :],
                                                     scalar1=float(1.0 / D), scalar2=None, op0=ALU.mult),
                 inc=VE)
        v = P.op("vector", lambda e: e.tensor_tensor(out=self.AF[0:1, AF_MISC:AF_MISC + TPC],
                                                     in0=self.statn[:, 0:TPC], in1=self.statn[:, 0:TPC],
                                                     op=ALU.mult), waits=[(VE, v)], inc=VE)
        v = P.op("vector", lambda e: e.tensor_tensor(out=self.statn[:, TPC:2 * TPC],
                                                     in0=self.statn[:, TPC:2 * TPC],
                                                     in1=self.AF[0:1, AF_MISC:AF_MISC + TPC],
                                                     op=ALU.subtract), waits=[(VE, v)], inc=VE)
        s = P.op("scalar", lambda e: e.activation(out=self.AF[0:1, AF_MISC:AF_MISC + TPC],
                                                  in_=self.statn[:, TPC:2 * TPC],
                                                  func=AF_.Sqrt, bias=self.epsv[:, :]),
                 waits=[(VE, v)], inc=SE)
        v = P.op("vector", lambda e: e.reciprocal(out=self.statn[:, TPC:2 * TPC],
                                                  in_=self.AF[0:1, AF_MISC:AF_MISC + TPC]),
                 waits=[(SE, s)], inc=VE)
        evs = {}
        for half, col in ((0, mub_col), (1, rsb_col)):
            for i in range(4):
                w = [(VE, v)] + ([(VE, evs[i])] if i in evs else [])
                m = P.op("tensor", lambda t, h_=half, i_=i: t.matmul(
                    self.psum[:, i_ * 512:i_ * 512 + 512], self.ones_f1[:, :],
                    self.statn[:, h_ * TPC + i_ * 512:h_ * TPC + (i_ + 1) * 512],
                    start=True, stop=True), waits=w, inc=PE)
                evs[i] = P.op("vector", lambda e, c_=col, i_=i: e.tensor_copy(
                    out=self.AF[:, c_ + i_ * 512:c_ + i_ * 512 + 512],
                    in_=self.psum[:, i_ * 512:i_ * 512 + 512]), waits=[(PE, m)], inc=VE)

    def ffn_sublayer(self, xsrc, xdst, final):
        P = self.P
        self.h_pass(xsrc, router=True)
        P.barrier()
        self.ck()
        self.ffn_lowrank()
        P.barrier()
        self.tap("lr2", self.lr2_scr)
        self.tap("smask", self.smask[:, :])
        self.ffn_inner_ln()
        P.barrier()
        self.ck()
        self.ffn_qk("ffn_wkT", True)
        P.barrier()
        self.ck()
        self.ffn_v()
        P.barrier()
        self.tap("contrib", self.contribF)
        self.ck()
        cc = self.ccs.pop(0)
        P.barrier()
        P.op("gpsimd", lambda g: g.collective_compute(
            "AllGather", ALU.bypass, replica_groups=[list(range(NCORE))],
            ins=[self.contribF.opt()], outs=[self.gathF.opt()]), inc=cc, amt=1)
        self.cc_cur = cc
        self.ck()
        self.ffn_qk("ffn_wqT", False)
        P.barrier()
        self.tap("q", self.q_scr)
        self.ck()
        self.ffn_attn()
        P.barrier()
        self.tap("ctx", self.ctx_scr)
        self.ck()
        self.dense_to_x(self.wbf["ffn_woT"], xsrc, self.xt_scr, kind="wo", fuse_stats=True)
        P.barrier()
        self.tap("xt", self.xt_scr)
        self.finalize_stats(AF_MU2B, AF_RS2B)
        P.barrier()
        self.ck()
        self.ffn_ff_select(xsrc, xdst, final)

    def h_pass(self, xsrc, router):
        P = self.P
        mlast = None
        selast = None
        for kc in range(DC):
            p = kc % 2
            fo = AF_IO + p * 2048
            d = self.dma("sync", self.AF[:, fo:fo + TPC], xsrc[kc * 128:(kc + 1) * 128, :],
                         waits=self.need(("h1", p)), stream="h1x%d" % p)
            v = P.op("vector", lambda e, f_=fo: e.tensor_tensor(
                out=self.AF[:, f_:f_ + TPC], in0=self.AF[:, f_:f_ + TPC],
                in1=self.AF[:, AF_MUB:AF_MUB + TPC], op=ALU.subtract), waits=[d], inc=VE)
            v = P.op("vector", lambda e, f_=fo: e.tensor_tensor(
                out=self.AF[:, f_:f_ + TPC], in0=self.AF[:, f_:f_ + TPC],
                in1=self.AF[:, AF_RSB:AF_RSB + TPC], op=ALU.mult), inc=VE)
            selast = P.op("scalar", lambda e, f_=fo, kc_=kc: e.copy(
                out=self.AB[:, AB_H1 + kc_ * TPC:AB_H1 + (kc_ + 1) * TPC], in_=self.AF[:, f_:f_ + TPC]),
                waits=[(VE, v)], inc=SE)
            if router:
                for tg in range(4):
                    mlast = P.op("tensor", lambda t, f_=fo, kc_=kc, tg_=tg: t.matmul(
                        self.psum[0:3, tg_ * 512:tg_ * 512 + 512],
                        self.wrouter[:, kc_ * 3:(kc_ + 1) * 3],
                        self.AF[:, f_ + tg_ * 512:f_ + tg_ * 512 + 512],
                        start=(kc_ == 0), stop=(kc_ == DC - 1)), waits=[(VE, v)], inc=PE)
                self.done(("h1", p), PE, mlast)
                self.done(("h1", p), SE, selast)
            else:
                self.done(("h1", p), SE, selast)
        self.h1_se = selast
        if not router:
            return
        v = P.op("vector", lambda e: e.tensor_copy(out=self.AF[0:3, AF_IO:AF_IO + TPC],
                                                   in_=self.psum[0:3, 0:TPC]),
                 waits=[(PE, mlast), (SE, selast)], inc=VE)
        d1 = self.dma("gpsimd", self.AF[0:1, AF_IO + TPC:AF_IO + 2 * TPC],
                      self.AF[1:2, AF_IO:AF_IO + TPC], waits=[(VE, v)], stream="rrow")
        d2 = self.dma("gpsimd", self.AF[0:1, AF_MU2B:AF_MU2B + TPC],
                      self.AF[2:3, AF_IO:AF_IO + TPC], waits=[(VE, v)], stream="rrow")
        L0 = self.AF[0:1, AF_IO:AF_IO + TPC]
        L1 = self.AF[0:1, AF_IO + TPC:AF_IO + 2 * TPC]
        L2 = self.AF[0:1, AF_MU2B:AF_MU2B + TPC]
        TA = self.AF[0:1, AF_RS2B:AF_RS2B + TPC]
        TB = self.AF[0:1, AF_MISC:AF_MISC + TPC]
        v = P.op("vector", lambda e: e.tensor_tensor(out=TA, in0=L0, in1=L1, op=ALU.is_ge),
                 waits=[d2], inc=VE)
        v = P.op("vector", lambda e: e.tensor_tensor(out=TB, in0=L0, in1=L2, op=ALU.is_ge), inc=VE)
        v = P.op("vector", lambda e: e.tensor_tensor(out=self.smask[:, 0:TPC], in0=TA, in1=TB,
                                                     op=ALU.mult), inc=VE)
        v = P.op("vector", lambda e: e.tensor_tensor(out=TA, in0=L1, in1=L0, op=ALU.is_gt), inc=VE)
        v = P.op("vector", lambda e: e.tensor_tensor(out=TB, in0=L1, in1=L2, op=ALU.is_ge), inc=VE)
        v = P.op("vector", lambda e: e.tensor_tensor(out=self.smask[:, TPC:2 * TPC], in0=TA, in1=TB,
                                                     op=ALU.mult), inc=VE)
        v = P.op("vector", lambda e: e.tensor_scalar(out=self.smask[:, :], in0=self.smask[:, :],
                                                     scalar1=0.5, scalar2=None, op0=ALU.mult), inc=VE)

    def ffn_lowrank(self):
        P = self.P
        for kc in range(DC):
            self.dma("sync", self.AB[:, AB_WDN + kc * 32:AB_WDN + (kc + 1) * 32],
                     self.wbf["wdownT"][kc * 128:(kc + 1) * 128, :], stream="lrw")
        du = self.dma("sync", self.AB[0:RLOW, AB_WUP:AB_WUP + TPC], self.wbf["wupT"][:, :],
                      stream="lrw")
        m = None
        for tg in range(4):
            for kc in range(DC):
                m = P.op("tensor", lambda t, kc_=kc, tg_=tg: t.matmul(
                    self.psum[0:RLOW, tg_ * 512:tg_ * 512 + 512],
                    self.AB[:, AB_WDN + kc_ * 32:AB_WDN + (kc_ + 1) * 32],
                    self.AB[:, AB_H1 + kc_ * TPC + tg_ * 512:AB_H1 + kc_ * TPC + tg_ * 512 + 512],
                    start=(kc_ == 0), stop=(kc_ == DC - 1)),
                    waits=[du, (SE, self.h1_se)], inc=PE)
        v = P.op("vector", lambda e: e.tensor_copy(out=self.AB[0:RLOW, AB_LR2E:AB_LR2E + TPC],
                                                   in_=self.psum[0:RLOW, 0:TPC]),
                 waits=[(PE, m)], inc=VE)
        for oc in range(DC):
            for tg in range(4):
                i = oc * 4 + tg
                pcol = 2048 + (i % 2) * 512
                ecol = AB_EVAC + (i % 2) * 512
                waits = [(VE, v), du] + self.need(("lrp", i % 2))
                m = P.op("tensor", lambda t, oc_=oc, tg_=tg, pc=pcol: t.matmul(
                    self.psum[:, pc:pc + 512],
                    self.AB[0:RLOW, AB_WUP + oc_ * 128:AB_WUP + (oc_ + 1) * 128],
                    self.AB[0:RLOW, AB_LR2E + tg_ * 512:AB_LR2E + tg_ * 512 + 512],
                    start=True, stop=True), waits=waits, inc=PE)
                s = P.op("scalar", lambda e, pc=pcol, ec=ecol: e.copy(
                    out=self.AB[:, ec:ec + 512], in_=self.psum[:, pc:pc + 512]),
                    waits=[(PE, m)] + self.need(("lre", i % 2)), inc=SE)
                d = self.dma("gpsimd", self.lr2_scr[oc * 128:(oc + 1) * 128, tg * 512:(tg + 1) * 512],
                             self.AB[:, ecol:ecol + 512], waits=[(SE, s)], stream="lre%d" % (i % 2))
                self.done(("lrp", i % 2), SE, s)
                self.done(("lre", i % 2), *d)

    def ffn_inner_ln(self):
        P = self.P
        P.barrier()
        sq = AB_EVAC
        for tg in range(4):
            m2 = None
            sprev = None
            for kc in range(DC):
                p = kc % 2
                sprev = P.op("scalar", lambda e, kc_=kc, tg_=tg, p_=p: e.activation(
                    out=self.AB[:, sq + p_ * 512:sq + p_ * 512 + 512],
                    in_=self.AB[:, AB_H1 + kc_ * TPC + tg_ * 512:AB_H1 + kc_ * TPC + tg_ * 512 + 512],
                    func=AF_.Square), waits=self.need(("iln", p)), inc=SE)
                P.op("tensor", lambda t, kc_=kc, tg_=tg, first=(kc == 0), last=(kc == DC - 1): t.matmul(
                    self.psum[0:1, 2048:2560], self.ones_b[:, :],
                    self.AB[:, AB_H1 + kc_ * TPC + tg_ * 512:AB_H1 + kc_ * TPC + tg_ * 512 + 512],
                    start=first, stop=last),
                    waits=(self.need("st_ev") if kc == 0 else []), inc=PE)
                m2 = P.op("tensor", lambda t, p_=p, first=(kc == 0), last=(kc == DC - 1): t.matmul(
                    self.psum[0:1, 2560:3072], self.ones_b[:, :],
                    self.AB[:, sq + p_ * 512:sq + p_ * 512 + 512],
                    start=first, stop=last), waits=[(SE, sprev)], inc=PE)
                self.done(("iln", p), PE, m2)
            v = P.op("vector", lambda e, tg_=tg: e.tensor_copy(
                out=self.statn[:, tg_ * 512:tg_ * 512 + 512], in_=self.psum[0:1, 2048:2560]),
                waits=[(PE, m2)], inc=VE)
            v = P.op("vector", lambda e, tg_=tg: e.tensor_copy(
                out=self.statn[:, TPC + tg_ * 512:TPC + tg_ * 512 + 512], in_=self.psum[0:1, 2560:3072]),
                waits=[(PE, m2)], inc=VE)
            self.done(("iln", 0), VE, v)
            self.done(("iln", 1), VE, v)
            self.done("st_ev", VE, v)
        P.barrier()
        self.finalize_stats(AF_MU2B, AF_RS2B)
        P.barrier()
        for kc in range(DC):
            p = kc % 2
            fo = AF_IO + p * 2048
            v = P.op("vector", lambda e, kc_=kc, f_=fo: e.tensor_tensor(
                out=self.AF[:, f_:f_ + TPC], in0=self.AB[:, AB_H1 + kc_ * TPC:AB_H1 + (kc_ + 1) * TPC],
                in1=self.AF[:, AF_MU2B:AF_MU2B + TPC], op=ALU.subtract),
                waits=self.need(("ilna", p)), inc=VE)
            v = P.op("vector", lambda e, f_=fo: e.tensor_tensor(
                out=self.AF[:, f_:f_ + TPC], in0=self.AF[:, f_:f_ + TPC],
                in1=self.AF[:, AF_RS2B:AF_RS2B + TPC], op=ALU.mult), inc=VE)
            v = P.op("vector", lambda e, kc_=kc, f_=fo: e.tensor_scalar(
                out=self.AB[:, AB_H1 + kc_ * TPC:AB_H1 + (kc_ + 1) * TPC], in0=self.AF[:, f_:f_ + TPC],
                scalar1=self.vec_sb["ln1s_v"][:, kc_:kc_ + 1], scalar2=self.vec_sb["ln1b_v"][:, kc_:kc_ + 1],
                op0=ALU.mult, op1=ALU.add), inc=VE)
            self.done(("ilna", p), VE, v)
            self.h1_se = max(self.h1_se, 0)
        self.iln_ve = v

    def ffn_qk(self, wname, is_k):
        P = self.P
        for og in range(4):
            pb = AB_PANEL + (og % 2) * 8192
            dp = self.dma("sync", self.AB[:, pb:pb + 8192],
                          self.wbf[wname][og * 128:(og + 1) * 128, :],
                          waits=self.need(("pan", og % 2)), stream="pan%d" % (og % 2))
            for oi in range(4):
                h = og * 4 + oi
                m = None
                for kc in range(DC):
                    for tg in range(4):
                        waits = [dp, (SE, self.h1_se), (VE, self.iln_ve)]
                        if kc == 0 and tg == 0:
                            waits += self.need("qk_ps")
                        m = P.op("tensor", lambda t, kc_=kc, tg_=tg, oi_=oi, pb_=pb: t.matmul(
                            self.psum[:, tg_ * 512:tg_ * 512 + 512],
                            self.AB[:, pb_ + kc_ * 512 + oi_ * 128:pb_ + kc_ * 512 + oi_ * 128 + 128],
                            self.AB[:, AB_H1 + kc_ * TPC + tg_ * 512:AB_H1 + kc_ * TPC + tg_ * 512 + 512],
                            start=(kc_ == 0), stop=(kc_ == DC - 1)), waits=waits, inc=PE)
                s = P.op("scalar", lambda e: e.copy(out=self.AB[:, AB_EVAC:AB_EVAC + TPC],
                                                    in_=self.psum[:, 0:TPC]),
                         waits=[(PE, m)] + self.need("qk_ev"), inc=SE)
                s2 = P.op("scalar", lambda e: e.activation(out=self.AB[:, AB_EVAC + TPC:AB_EVAC + 2 * TPC],
                                                           in_=self.psum[:, 0:TPC], func=AF_.Square),
                          waits=[(PE, m)], inc=SE)
                self.done("qk_ps", SE, s2)
                nv = None
                for tg in range(4):
                    waits = [(SE, s2)] + ([(VE, nv)] if nv else []) + (self.need("n_ps") if tg == 0 else [])
                    mn = P.op("tensor", lambda t, tg_=tg: t.matmul(
                        self.psum[0:1, 2048:2560], self.ones_b[:, :],
                        self.AB[:, AB_EVAC + TPC + tg_ * 512:AB_EVAC + TPC + tg_ * 512 + 512],
                        start=True, stop=True), waits=waits, inc=PE)
                    if not is_k:
                        nv = P.op("vector", lambda e, tg_=tg: e.reduce_max(
                            out=self.tiny[0:1, 32 + tg_ % 2:33 + tg_ % 2], in_=self.psum[0:1, 2048:2560],
                            axis=AXX), waits=[(PE, mn)], inc=VE)
                        if tg % 2 == 1:
                            nv = P.op("vector", lambda e, b_=tg // 2, h_=h: e.tensor_tensor(
                                out=self.rq2[0:1, b_ * 16 + h_:b_ * 16 + h_ + 1],
                                in0=self.tiny[0:1, 32:33], in1=self.tiny[0:1, 33:34], op=ALU.max), inc=VE)
                    else:
                        nv = P.op("vector", lambda e, tg_=tg: e.tensor_copy(
                            out=self.AF[0:1, AF_MISC + tg_ * 512:AF_MISC + tg_ * 512 + 512],
                            in_=self.psum[0:1, 2048:2560]), waits=[(PE, mn)], inc=VE)
                self.done("n_ps", VE, nv)
                if is_k:
                    dn = self.dma("gpsimd", self.contribF[self.CF_KN + h:self.CF_KN + h + 1, :],
                                  self.AF[0:1, AF_MISC:AF_MISC + TPC], waits=[(VE, nv)], stream="qke")
                    d = self.dma("gpsimd", self.contribF[h * 128:(h + 1) * 128, :],
                                 self.AB[:, AB_EVAC:AB_EVAC + TPC], waits=[(SE, s)], stream="qke")
                else:
                    for bb in range(2):
                        d = self.dma("gpsimd", self.q_scr[(bb * H + h) * 128:(bb * H + h) * 128 + 128, :],
                                     self.AB[:, AB_EVAC + bb * NQB:AB_EVAC + (bb + 1) * NQB],
                                     waits=[(SE, s)], stream="qke")
                self.done("qk_ev", *d)
            self.done(("pan", og % 2), PE, P.counts[PE])

    def ffn_v(self):
        P = self.P
        for vg in range(4):
            pb = AB_PANEL + (vg % 2) * 8192
            dp = self.dma("sync", self.AB[:, pb:pb + 8192],
                          self.wbf["ffn_wvT"][vg * 128:(vg + 1) * 128, :],
                          waits=self.need(("pan", vg % 2)), stream="pan%d" % (vg % 2))
            for t128 in range(DC):
                par = t128 % 2
                pc = par * 512
                m = None
                for kc in range(DC):
                    waits = [dp, (SE, self.h1_se), (VE, self.iln_ve)]
                    if kc == 0:
                        waits += self.need(("v_ps", par))
                    m = P.op("tensor", lambda t, kc_=kc, t_=t128, pc_=pc, pb_=pb: t.matmul(
                        self.psum[:, pc_:pc_ + 512],
                        self.AB[:, AB_H1 + kc_ * TPC + t_ * 128:AB_H1 + kc_ * TPC + t_ * 128 + 128],
                        self.AB[:, pb_ + kc_ * 512:pb_ + (kc_ + 1) * 512],
                        start=(kc_ == 0), stop=(kc_ == DC - 1)), waits=waits, inc=PE)
                s = P.op("scalar", lambda e, par_=par, pc_=pc: e.copy(
                    out=self.AB[:, AB_EVAC + par_ * 512:AB_EVAC + par_ * 512 + 512],
                    in_=self.psum[:, pc_:pc_ + 512]),
                    waits=[(PE, m)] + self.need(("v_ev", par)), inc=SE)
                d = self.dma("gpsimd",
                             self.contribF[self.CF_V + t128 * 128:self.CF_V + (t128 + 1) * 128,
                                           vg * 512:(vg + 1) * 512],
                             self.AB[:, AB_EVAC + par * 512:AB_EVAC + par * 512 + 512],
                             waits=[(SE, s)], stream="ve%d" % par)
                self.done(("v_ps", par), SE, s)
                self.done(("v_ev", par), *d)
            self.done(("pan", vg % 2), PE, P.counts[PE])

    def ffn_attn(self):
        P = self.P
        cc = self.cc_cur
        lcol = 3072
        exp_at_slot = {}
        pv_at_probs = {}
        pres_ve = {}
        pres_pe = {}
        for it in range(B * 8):
            bb, hp = it // 8, it % 8
            buf = it % 2
            kbase = AB_KSB + buf * 16384
            qbase = AB_Q + buf * 2048
            for hl in range(2):
                h = hp * 2 + hl
                for rr in range(NCORE):
                    self.dma("sync",
                             self.AB[:, kbase + hl * 8192 + rr * 1024:kbase + hl * 8192 + (rr + 1) * 1024],
                             self.gathF[rr * self.CF_ROWS + h * 128:rr * self.CF_ROWS + (h + 1) * 128,
                                        bb * NQB:(bb + 1) * NQB],
                             waits=[(cc, 1)] + self.need(("ksb", buf)), stream="ksb%d" % buf)
                dq = self.dma("sync", self.AB[:, qbase + hl * 1024:qbase + (hl + 1) * 1024],
                              self.q_scr[(bb * H + h) * 128:(bb * H + h) * 128 + 128, :],
                              waits=self.need(("ksb", buf)), stream="ksb%d" % buf)
            vlast = None
            for hl in range(2):
                h = hp * 2 + hl
                for rr in range(NCORE):
                    sl = rr % 2
                    dn = self.dma("sync", self.AB[0:1, AB_SB1 + sl * 1024:AB_SB1 + (sl + 1) * 1024],
                                  self.gathF[rr * self.CF_ROWS + self.CF_KN + h:
                                             rr * self.CF_ROWS + self.CF_KN + h + 1,
                                             bb * NQB:(bb + 1) * NQB],
                                  waits=self.need(("knt", sl)), stream="knt%d" % sl)
                    if rr == 0:
                        v = P.op("vector", lambda e, sl_=sl: e.reduce_max(
                            out=self.tiny[0:1, 40:41],
                            in_=self.AB[0:1, AB_SB1 + sl_ * 1024:AB_SB1 + sl_ * 1024 + 1024],
                            axis=AXX), waits=[dn], inc=VE)
                    else:
                        v = P.op("vector", lambda e, sl_=sl: e.reduce_max(
                            out=self.tiny[0:1, 43:44],
                            in_=self.AB[0:1, AB_SB1 + sl_ * 1024:AB_SB1 + sl_ * 1024 + 1024],
                            axis=AXX), waits=[dn], inc=VE)
                        v = P.op("vector", lambda e: e.tensor_tensor(
                            out=self.tiny[0:1, 40:41], in0=self.tiny[0:1, 40:41],
                            in1=self.tiny[0:1, 43:44], op=ALU.max), inc=VE)
                    self.done(("knt", sl), VE, v)
                v = P.op("vector", lambda e, b_=bb, h_=h: e.tensor_tensor(
                    out=self.tiny[0:1, 41:42], in0=self.tiny[0:1, 40:41],
                    in1=self.rq2[0:1, b_ * 16 + h_:b_ * 16 + h_ + 1], op=ALU.add), inc=VE)
                v = P.op("vector", lambda e: e.tensor_scalar(
                    out=self.tiny[0:1, 42:43], in0=self.tiny[0:1, 41:42],
                    scalar1=float(-0.5 * SCALE), scalar2=None, op0=ALU.mult), inc=VE)
                mb = P.op("tensor", lambda t: t.matmul(
                    self.psum[:, 2048:2049], self.ones_f1[:, :], self.tiny[0:1, 42:43],
                    start=True, stop=True),
                    waits=[(VE, v)] + self.need(("sc", 2048)) + self.need(("mhc", buf * 2 + hl)), inc=PE)
                vlast = P.op("vector", lambda e, c_=buf * 2 + hl: e.tensor_copy(
                    out=self.mh[:, c_:c_ + 1], in_=self.psum[:, 2048:2049]),
                    waits=[(PE, mb)], inc=VE)
            mh_ve = vlast
            ring = 0
            l_first = True
            probs_info = {}
            acc_last = None
            for j in range(NKT):
                jo = j % 8
                rr, mm_ = j % 8, j // 8
                vslot = j % 4
                dv = self.dma("sync", self.AB[:, AB_VR + vslot * 256:AB_VR + (vslot + 1) * 256],
                              self.gathF[rr * self.CF_ROWS + self.CF_V + bb * NQB + mm_ * 128:
                                         rr * self.CF_ROWS + self.CF_V + bb * NQB + (mm_ + 1) * 128,
                                         hp * 256:(hp + 1) * 256],
                              waits=self.need(("vr", vslot)), stream="vr%d" % vslot)
                for hl in range(2):
                    for si, (qs, n) in enumerate(slabs_for(j)):
                        scol = 2048 + 512 * (ring % 2)
                        ring += 1
                        pcol = AB_PROBS + ((hl * 2 + si) * 2 + (j % 2)) * 512
                        waits = [dq, (VE, mh_ve), (SE, exp_at_slot.get(scol, 0))] + \
                            self.need(("sc", scol))
                        pe_c = P.op("tensor", lambda t, sc=scol, hl_=hl, jj=j, q0=qs, nn=n, kb=kbase, qb=qbase:
                                    t.matmul(self.psum[:, sc:sc + nn],
                                             self.AB[:, kb + hl_ * 8192 + (jj % 8) * 1024 + (jj // 8) * 128:
                                                     kb + hl_ * 8192 + (jj % 8) * 1024 + (jj // 8) * 128 + 128],
                                             self.AB[:, qb + hl_ * 1024 + q0:qb + hl_ * 1024 + q0 + nn],
                                             start=True, stop=True), waits=waits, inc=PE)
                        waits = [(PE, pe_c), (PE, pv_at_probs.get(pcol, 0)), (VE, pres_ve.get(pcol, 0))]
                        se_c = P.op("scalar", lambda s, sc=scol, pc=pcol, c_=buf * 2 + hl, nn=n:
                                    s.activation(out=self.AB[:, pc:pc + nn], in_=self.psum[:, sc:sc + nn],
                                                 func=AF_.Exp, bias=self.mh[:, c_:c_ + 1], scale=SCALE),
                                    waits=waits, inc=SE)
                        exp_at_slot[scol] = se_c
                        ve_c = 0
                        if qs == 128 * (j // 8):
                            ve_c = P.op("vector", lambda v, pc=pcol, jo_=jo: v.tensor_tensor(
                                out=self.AB[:, pc:pc + 128], in0=self.AB[:, pc:pc + 128],
                                in1=self.masks[:, jo_ * 128:(jo_ + 1) * 128], op=ALU.mult),
                                waits=[(SE, se_c)], inc=VE)
                        probs_info[(j, hl, si)] = (se_c, ve_c, pcol, qs, n)
                    for si, (qs, n) in enumerate(slabs_for(j)):
                        se_c, ve_c, pcol, _, _ = probs_info[(j, hl, si)]
                        waits = [(SE, se_c), dv]
                        if ve_c:
                            waits.append((VE, ve_c))
                        if j == 0:
                            waits += self.need(("acc", hl, 0)) + self.need(("acc", hl, 1))
                        if jo == 7 and qs == 128 * (j // 8):
                            pieces = [(qs, 128, True)]
                            if n > 128:
                                pieces.append((qs + 128, n - 128, j == NKT - 1))
                        else:
                            pieces = [(qs, n, j == NKT - 1)]
                        for (pq, pn, pstop) in pieces:
                            acc_last = P.op("tensor",
                                            lambda t, hl_=hl, pc=pcol, q0=pq, nn=pn, vs=vslot, jj=j,
                                            qs_=qs, st=pstop:
                                            t.matmul(self.psum[:, hl_ * 1024 + q0:hl_ * 1024 + q0 + nn],
                                                     self.AB[:, AB_VR + vs * 256 + hl_ * 128:
                                                             AB_VR + vs * 256 + hl_ * 128 + 128],
                                                     self.AB[:, pc + (q0 - qs_):pc + (q0 - qs_) + nn],
                                                     start=(jj == 0), stop=st, skip_group_check=True),
                                            waits=waits, inc=PE)
                        pv_at_probs[pcol] = acc_last
                self.done(("vr", vslot), PE, acc_last)
                if j % 2 == 1:
                    for hl in range(2):
                        for si, (qs, n) in enumerate(slabs_for(j)):
                            _, ve0, pc0, _, _ = probs_info[(j - 1, hl, si)]
                            se1, ve1, pc1, _, _ = probs_info[(j, hl, si)]
                            pscol = AB_PRES + (hl * 2 + si) * 512
                            waits = [(SE, se1), (PE, pres_pe.get(pscol, 0))]
                            if ve0:
                                waits.append((VE, ve0))
                            if ve1:
                                waits.append((VE, ve1))
                            ve_c = P.op("vector", lambda v, a=pc0, b_=pc1, c=pscol, nn=n:
                                        v.tensor_tensor(out=self.AB[:, c:c + nn], in0=self.AB[:, a:a + nn],
                                                        in1=self.AB[:, b_:b_ + nn], op=ALU.add),
                                        waits=waits, inc=VE)
                            pres_ve[pc0] = ve_c
                            pres_ve[pc1] = ve_c
                            last = None
                            for sl in range(n // 128):
                                qc = (qs + sl * 128) // 128
                                lw = [(VE, ve_c)]
                                if l_first:
                                    lw += self.need("l_ps")
                                last = P.op("tensor", lambda t, c=pscol, sl_=sl, hl_=hl, qc_=qc,
                                            first=l_first, st=(j == 8 * qc + 7):
                                            t.matmul(self.psum[:, lcol + hl_ * 8 + qc_:lcol + hl_ * 8 + qc_ + 1],
                                                     self.AB[:, c + sl_ * 128:c + sl_ * 128 + 128],
                                                     self.ones_b[:, :], start=first, stop=st,
                                                     skip_group_check=True),
                                            waits=lw, inc=PE)
                                l_first = False
                            pres_pe[pscol] = last
            pe_done = P.counts[PE]
            last_exp = max(probs_info[(NKT - 1, hl2, si2)][0]
                           for hl2 in range(2) for si2 in range(len(slabs_for(NKT - 1))))
            self.done(("mhc", buf * 2), SE, last_exp)
            self.done(("mhc", buf * 2 + 1), SE, last_exp)
            vrl = P.op("vector", lambda e: e.reciprocal(out=self.AF[:, AF_MISC + 1024:AF_MISC + 1040],
                                                        in_=self.psum[:, lcol:lcol + 16]),
                       waits=[(PE, pe_done)], inc=VE)
            self.done("l_ps", VE, vrl)
            for hl in range(2):
                h = hp * 2 + hl
                tr = None
                for qc in range(8):
                    tw = [(VE, vrl)]
                    if qc == 0:
                        tw += [(SE, exp_at_slot.get(2048, 0)), (SE, exp_at_slot.get(2560, 0))]
                        tw += self.need("ep_ps")
                    tr = P.op("tensor", lambda t, hl_=hl, qc_=qc: t.transpose(
                        self.psum[0:1, 2048 + qc_ * 128:2048 + qc_ * 128 + 128],
                        self.AF[:, AF_MISC + 1024 + hl_ * 8 + qc_:AF_MISC + 1024 + hl_ * 8 + qc_ + 1],
                        self.idf[:, :]), waits=tw, inc=PE)
                vT = P.op("vector", lambda e: e.tensor_copy(out=self.AF[0:1, AF_MISC:AF_MISC + 1024],
                                                            in_=self.psum[0:1, 2048:3072]),
                          waits=[(PE, tr)], inc=VE)
                bc = None
                for i2 in range(2):
                    bc = P.op("tensor", lambda t, i_=i2: t.matmul(
                        self.psum[:, 2048 + i_ * 512:2048 + i_ * 512 + 512], self.ones_f1[:, :],
                        self.AF[0:1, AF_MISC + i_ * 512:AF_MISC + (i_ + 1) * 512],
                        start=True, stop=True), waits=[(VE, vT)], inc=PE)
                vb = P.op("vector", lambda e: e.tensor_copy(out=self.AF[:, AF_MISC:AF_MISC + 1024],
                                                            in_=self.psum[:, 2048:3072]),
                          waits=[(PE, bc)], inc=VE)
                self.done("ep_ps", VE, vb)
                self.done(("sc", 2048), VE, vb)
                self.done(("sc", 2560), VE, vb)
                vy = P.op("vector", lambda e, hl_=hl: e.tensor_tensor(
                    out=self.AB[:, AB_YS:AB_YS + 1024], in0=self.psum[:, hl_ * 1024:(hl_ + 1) * 1024],
                    in1=self.AF[:, AF_MISC:AF_MISC + 1024], op=ALU.mult),
                    waits=[(VE, vb)] + self.need("ys"), inc=VE)
                self.done(("acc", hl, 0), VE, vy)
                self.done(("acc", hl, 1), VE, vy)
                dy = self.dma("gpsimd", self.ctx_scr[h * 128:(h + 1) * 128, bb * NQB:(bb + 1) * NQB],
                              self.AB[:, AB_YS:AB_YS + 1024], waits=[(VE, vy)], stream="ys")
                self.done("ys", *dy)
            self.done(("ksb", buf), PE, acc_last)

    def dense_to_x(self, w_scr, xadd_src, xdst, kind, fuse_stats):
        P = self.P
        dctx = None
        for kc in range(DC):
            dctx = self.dma("sync", self.AB[:, AB_CTX + kc * TPC:AB_CTX + (kc + 1) * TPC],
                            self.ctx_scr[kc * 128:(kc + 1) * 128, :], stream="ctx")
        stev = None
        pidx = 0
        for tg in range(4):
            for og in range(4):
                par = pidx % 2
                pidx += 1
                pb = AB_SLAB + par * 8192
                ds = self.dma("sync", self.AB[:, pb:pb + 8192],
                              w_scr[og * 128:(og + 1) * 128, :],
                              waits=self.need(("dslab", par)), stream="dsl%d" % par)
                mlast = None
                for kc in range(DC):
                    for oi in range(4):
                        waits = [ds, dctx]
                        if kc == 0:
                            waits += self.need(("dps", oi))
                        mlast = P.op("tensor", lambda t, kc_=kc, tg_=tg, oi_=oi, pb_=pb: t.matmul(
                            self.psum[:, oi_ * 512:oi_ * 512 + 512],
                            self.AB[:, pb_ + kc_ * 512 + oi_ * 128:
                                    pb_ + kc_ * 512 + oi_ * 128 + 128],
                            self.AB[:, AB_CTX + kc_ * TPC + tg_ * 512:AB_CTX + kc_ * TPC + tg_ * 512 + 512],
                            start=(kc_ == 0), stop=(kc_ == DC - 1)), waits=waits, inc=PE)
                self.done(("dslab", par), PE, mlast)
                for oi in range(4):
                    oc = og * 4 + oi
                    slot = oc % 4
                    fo = AF_IO + slot * 512
                    fx = AF_IO + 2048 + slot * 512
                    dx = self.dma("sync", self.AF[:, fo:fo + 512],
                                  xadd_src[oc * 128:(oc + 1) * 128, tg * 512:(tg + 1) * 512],
                                  waits=self.need(("dio", slot)) + self.need(("dio2", slot)),
                                  stream="dio%d" % slot)
                    if kind == "wo":
                        v = P.op("vector", lambda e, f_=fo, tg_=tg: e.tensor_tensor(
                            out=self.AF[:, f_:f_ + 512], in0=self.AF[:, f_:f_ + 512],
                            in1=self.AF[:, AF_MUB + tg_ * 512:AF_MUB + (tg_ + 1) * 512], op=ALU.subtract),
                            waits=[dx, (PE, mlast)], inc=VE)
                        v = P.op("vector", lambda e, f_=fo, tg_=tg: e.tensor_tensor(
                            out=self.AF[:, f_:f_ + 512], in0=self.AF[:, f_:f_ + 512],
                            in1=self.AF[:, AF_RSB + tg_ * 512:AF_RSB + (tg_ + 1) * 512], op=ALU.mult), inc=VE)
                        v = P.op("vector", lambda e, f_=fo, fx_=fx, oi_=oi: e.tensor_tensor(
                            out=self.AF[:, fx_:fx_ + 512], in0=self.AF[:, f_:f_ + 512],
                            in1=self.psum[:, oi_ * 512:oi_ * 512 + 512], op=ALU.add), inc=VE)
                    else:
                        v = P.op("vector", lambda e, f_=fo, fx_=fx, oi_=oi: e.tensor_tensor(
                            out=self.AF[:, fx_:fx_ + 512], in0=self.AF[:, f_:f_ + 512],
                            in1=self.psum[:, oi_ * 512:oi_ * 512 + 512], op=ALU.add),
                            waits=[dx, (PE, mlast)], inc=VE)
                    self.done(("dps", oi), VE, v)
                    d = self.dma("gpsimd", xdst[oc * 128:(oc + 1) * 128, tg * 512:(tg + 1) * 512],
                                 self.AF[:, fx:fx + 512], waits=[(VE, v)], stream="dxo%d" % slot)
                    if fuse_stats:
                        sq = P.op("scalar", lambda e, fx_=fx, f_=fo: e.activation(
                            out=self.AF[:, f_:f_ + 512], in_=self.AF[:, fx_:fx_ + 512], func=AF_.Square),
                            waits=[(VE, v)], inc=SE)
                        P.op("tensor", lambda t, fx_=fx, first=(oc == 0), last=(oc == DC - 1): t.matmul(
                            self.psum[0:1, 2048:2560], self.ones_f128[:, :], self.AF[:, fx_:fx_ + 512],
                            start=first, stop=last), waits=[(VE, v)] + (self.need("sst") if oc == 0 else []),
                            inc=PE)
                        m2 = P.op("tensor", lambda t, f_=fo, first=(oc == 0), last=(oc == DC - 1): t.matmul(
                            self.psum[0:1, 2560:3072], self.ones_f128[:, :], self.AF[:, f_:f_ + 512],
                            start=first, stop=last), waits=[(SE, sq)], inc=PE)
                        self.done(("dio", slot), PE, m2)
                        self.done(("dio2", slot), *d)
                    else:
                        self.done(("dio", slot), *d)
            if fuse_stats:
                v = P.op("vector", lambda e, tg_=tg: e.tensor_copy(
                    out=self.statn[:, tg_ * 512:tg_ * 512 + 512], in_=self.psum[0:1, 2048:2560]),
                    waits=[(PE, P.counts[PE])], inc=VE)
                stev = P.op("vector", lambda e, tg_=tg: e.tensor_copy(
                    out=self.statn[:, TPC + tg_ * 512:TPC + tg_ * 512 + 512], in_=self.psum[0:1, 2560:3072]),
                    waits=[(PE, P.counts[PE])], inc=VE)
                self.done("sst", VE, stev)

    def ffn_ff_select(self, xsrc, xdst, final):
        P = self.P
        for tg in range(4):
            sb0 = P.op("tensor", lambda t, tg_=tg: t.matmul(
                self.psum[:, 3072:3584], self.ones_b1[:, :], self.smask[:, tg_ * 512:tg_ * 512 + 512],
                start=True, stop=True), waits=self.need("sbc_ps"), inc=PE)
            v0 = P.op("vector", lambda e: e.tensor_copy(out=self.AB[:, AB_SBC:AB_SBC + 512],
                                                        in_=self.psum[:, 3072:3584]),
                      waits=[(PE, sb0)] + self.need("sbc"), inc=VE)
            sb1 = P.op("tensor", lambda t, tg_=tg: t.matmul(
                self.psum[:, 3072:3584], self.ones_b1[:, :],
                self.smask[:, TPC + tg_ * 512:TPC + tg_ * 512 + 512],
                start=True, stop=True), waits=[(VE, v0)], inc=PE)
            v1 = P.op("vector", lambda e: e.tensor_copy(out=self.AB[:, AB_SBC + 512:AB_SBC + 1024],
                                                        in_=self.psum[:, 3072:3584]),
                      waits=[(PE, sb1)], inc=VE)
            self.done("sbc_ps", VE, v1)
            sbc_ve = v1
            xnlast = None
            for kc in range(DC):
                p = kc % 2
                fo = AF_IO + p * 512
                d = self.dma("sync", self.AF[:, fo:fo + 512],
                             self.xt_scr[kc * 128:(kc + 1) * 128, tg * 512:(tg + 1) * 512],
                             waits=self.need(("xn", p)), stream="xn%d" % p)
                v = P.op("vector", lambda e, f_=fo, tg_=tg: e.tensor_tensor(
                    out=self.AF[:, f_:f_ + 512], in0=self.AF[:, f_:f_ + 512],
                    in1=self.AF[:, AF_MU2B + tg_ * 512:AF_MU2B + (tg_ + 1) * 512], op=ALU.subtract),
                    waits=[d], inc=VE)
                v = P.op("vector", lambda e, f_=fo, tg_=tg: e.tensor_tensor(
                    out=self.AF[:, f_:f_ + 512], in0=self.AF[:, f_:f_ + 512],
                    in1=self.AF[:, AF_RS2B + tg_ * 512:AF_RS2B + (tg_ + 1) * 512], op=ALU.mult), inc=VE)
                xnlast = P.op("vector", lambda e, f_=fo, kc_=kc: e.tensor_scalar(
                    out=self.AB[:, AB_XN2 + kc_ * 512:AB_XN2 + (kc_ + 1) * 512], in0=self.AF[:, f_:f_ + 512],
                    scalar1=self.vec_sb["ln2s_v"][:, kc_:kc_ + 1], scalar2=self.vec_sb["ln2b_v"][:, kc_:kc_ + 1],
                    op0=ALU.mult, op1=ALU.add), inc=VE)
                self.done(("xn", p), VE, xnlast)
            gelast = None
            for ffq in range(16):
                mlast = None
                for half in range(2):
                    sl = half
                    ds = self.dma("sync", self.AB[:, AB_FSLAB + sl * 4096:AB_FSLAB + (sl + 1) * 4096],
                                  self.wbf["ff1T"][ffq * 128:(ffq + 1) * 128,
                                                   half * 4096:(half + 1) * 4096],
                                  waits=self.need(("fslab", sl)), stream="fsl%d" % sl)
                    for kc8 in range(8):
                        kc = half * 8 + kc8
                        for f4 in range(4):
                            waits = [ds, (VE, xnlast)]
                            if kc == 0:
                                waits += self.need(("fps", f4))
                            mlast = P.op("tensor", lambda t, kc_=kc, kc8_=kc8, f4_=f4, sl_=sl: t.matmul(
                                self.psum[:, f4_ * 512:f4_ * 512 + 512],
                                self.AB[:, AB_FSLAB + sl_ * 4096 + kc8_ * 512 + f4_ * 128:
                                        AB_FSLAB + sl_ * 4096 + kc8_ * 512 + f4_ * 128 + 128],
                                self.AB[:, AB_XN2 + kc_ * 512:AB_XN2 + (kc_ + 1) * 512],
                                start=(kc_ == 0), stop=(kc_ == DC - 1)), waits=waits, inc=PE)
                    self.done(("fslab", sl), PE, mlast)
                for f4 in range(4):
                    ffc = ffq * 4 + f4
                    gelast = P.op("scalar", lambda e, f4_=f4, ffc_=ffc: e.activation(
                        out=self.AB[:, AB_G + ffc_ * 512:AB_G + (ffc_ + 1) * 512],
                        in_=self.psum[:, f4_ * 512:f4_ * 512 + 512], func=AF_.Gelu_apprx_tanh),
                        waits=[(PE, mlast)], inc=SE)
                    self.done(("fps", f4), SE, gelast)
            for doq in range(4):
                mlast = None
                for run in range(8):
                    sl = run % 2
                    ds = self.dma("sync", self.AB[:, AB_FSLAB + sl * 4096:AB_FSLAB + (sl + 1) * 4096],
                                  self.wbf["ff2T"][doq * 128:(doq + 1) * 128,
                                                   run * 4096:(run + 1) * 4096],
                                  waits=self.need(("fslab", sl)), stream="fsl%d" % sl)
                    for f8 in range(8):
                        ffc = run * 8 + f8
                        for d4 in range(4):
                            waits = [ds, (SE, gelast)]
                            if ffc == 0:
                                waits += self.need(("fps", d4))
                            mlast = P.op("tensor", lambda t, ffc_=ffc, f8_=f8, d4_=d4, sl_=sl: t.matmul(
                                self.psum[:, d4_ * 512:d4_ * 512 + 512],
                                self.AB[:, AB_FSLAB + sl_ * 4096 + f8_ * 512 + d4_ * 128:
                                        AB_FSLAB + sl_ * 4096 + f8_ * 512 + d4_ * 128 + 128],
                                self.AB[:, AB_G + ffc_ * 512:AB_G + (ffc_ + 1) * 512],
                                start=(ffc_ == 0), stop=(ffc_ == FFD // 128 - 1)), waits=waits, inc=PE)
                    self.done(("fslab", sl), PE, mlast)
                for d4 in range(4):
                    oc = doq * 4 + d4
                    self.select_tile(oc, d4, tg, mlast, xsrc, xdst, final, sbc_ve)
            if not final:
                v = P.op("vector", lambda e, tg_=tg: e.tensor_copy(
                    out=self.statn[:, tg_ * 512:tg_ * 512 + 512], in_=self.psum[0:1, 2048:2560]),
                    waits=[(PE, P.counts[PE])], inc=VE)
                v = P.op("vector", lambda e, tg_=tg: e.tensor_copy(
                    out=self.statn[:, TPC + tg_ * 512:TPC + tg_ * 512 + 512],
                    in_=self.psum[0:1, 2560:3072]),
                    waits=[(PE, P.counts[PE])], inc=VE)
                self.done("sst", VE, v)

    def select_tile(self, oc, d4, tg, mlast, xsrc, xdst, final, sbc_ve):
        P = self.P
        slot = oc % 2
        fXT = AF_IO + 1024 + slot * 512
        fXO = AF_IO + 2048 + slot * 512
        fO2 = AF_IO + 3072 + (slot % 1) * 512
        fE1 = AF_MISC + 1024
        fA = AF_MISC + 1536
        lr = AB_LR2T + slot * 512
        dxt = self.dma("sync", self.AF[:, fXT:fXT + 512],
                       self.xt_scr[oc * 128:(oc + 1) * 128, tg * 512:(tg + 1) * 512],
                       waits=self.need(("se_io", slot)) + self.need(("se_io2", slot)),
                       stream="sio%d" % slot)
        dxo = self.dma("sync", self.AF[:, fXO:fXO + 512],
                       xsrc[oc * 128:(oc + 1) * 128, tg * 512:(tg + 1) * 512], stream="sio%d" % slot)
        dlr = self.dma("sync", self.AB[:, lr:lr + 512],
                       self.lr2_scr[oc * 128:(oc + 1) * 128, tg * 512:(tg + 1) * 512],
                       stream="sio%d" % slot)
        v = P.op("vector", lambda e: e.tensor_tensor(
            out=self.AF[:, fO2:fO2 + 512], in0=self.AF[:, fXT:fXT + 512],
            in1=self.psum[:, d4 * 512:d4 * 512 + 512], op=ALU.add),
            waits=[dlr, (PE, mlast), (VE, sbc_ve)] + self.need("sel_o2"), inc=VE)
        v = P.op("vector", lambda e, tg_=tg: e.tensor_tensor(
            out=self.AF[:, fXO:fXO + 512], in0=self.AF[:, fXO:fXO + 512],
            in1=self.AF[:, AF_MUB + tg_ * 512:AF_MUB + (tg_ + 1) * 512], op=ALU.subtract), inc=VE)
        v = P.op("vector", lambda e, tg_=tg: e.tensor_tensor(
            out=self.AF[:, fXO:fXO + 512], in0=self.AF[:, fXO:fXO + 512],
            in1=self.AF[:, AF_RSB + tg_ * 512:AF_RSB + (tg_ + 1) * 512], op=ALU.mult), inc=VE)
        v = P.op("vector", lambda e: e.tensor_tensor(
            out=self.AF[:, fE1:fE1 + 512], in0=self.AF[:, fXO:fXO + 512],
            in1=self.AF[:, fO2:fO2 + 512], op=ALU.subtract),
            waits=self.need("st_sq2"), inc=VE)
        v = P.op("vector", lambda e, oc_=oc: e.tensor_scalar(
            out=self.AF[:, fXO:fXO + 512], in0=self.AF[:, fE1:fE1 + 512],
            scalar1=self.vec_sb["bias_v"][:, oc_:oc_ + 1], scalar2=None, op0=ALU.add), inc=VE)
        v = P.op("vector", lambda e: e.tensor_tensor(
            out=self.AF[:, fXO:fXO + 512], in0=self.AF[:, fXO:fXO + 512],
            in1=self.AB[:, AB_SBC:AB_SBC + 512], op=ALU.mult), inc=VE)
        v = P.op("vector", lambda e, lr_=lr: e.tensor_tensor(
            out=self.AF[:, fE1:fE1 + 512], in0=self.AF[:, fE1:fE1 + 512],
            in1=self.AB[:, lr_:lr_ + 512], op=ALU.add), inc=VE)
        v = P.op("vector", lambda e: e.tensor_tensor(
            out=self.AF[:, fE1:fE1 + 512], in0=self.AF[:, fE1:fE1 + 512],
            in1=self.AB[:, AB_SBC + 512:AB_SBC + 1024], op=ALU.mult), inc=VE)
        v = P.op("vector", lambda e: e.tensor_tensor(
            out=self.AF[:, fA:fA + 512], in0=self.AF[:, fXO:fXO + 512],
            in1=self.AF[:, fE1:fE1 + 512], op=ALU.add), inc=VE)
        dxo2 = self.dma("sync", self.AF[:, fXO:fXO + 512],
                        xsrc[oc * 128:(oc + 1) * 128, tg * 512:(tg + 1) * 512], waits=[(VE, v)],
                        stream="sio%d" % slot)
        v = P.op("vector", lambda e: e.tensor_tensor(
            out=self.AF[:, fA:fA + 512], in0=self.AF[:, fA:fA + 512],
            in1=self.AF[:, fXO:fXO + 512], op=ALU.add), waits=[dxo2], inc=VE)
        v = P.op("vector", lambda e: e.scalar_tensor_tensor(
            out=self.AF[:, fXT:fXT + 512], in0=self.AF[:, fO2:fO2 + 512], scalar=0.5,
            in1=self.AF[:, fA:fA + 512], op0=ALU.mult, op1=ALU.add), inc=VE)
        self.done("sel_o2", VE, v)
        self.done(("fps", d4), VE, v)
        d = self.dma("gpsimd", xdst[oc * 128:(oc + 1) * 128, tg * 512:(tg + 1) * 512],
                     self.AF[:, fXT:fXT + 512], waits=[(VE, v)], stream="sxn%d" % slot)
        self.done(("se_io", slot), *d)
        if not final:
            sq = P.op("scalar", lambda e: e.activation(
                out=self.AF[:, fE1:fE1 + 512], in_=self.AF[:, fXT:fXT + 512], func=AF_.Square),
                waits=[(VE, v)] + self.need("st_sq"), inc=SE)
            P.op("tensor", lambda t, first=(oc == 0), last=(oc == DC - 1): t.matmul(
                self.psum[0:1, 2048:2560], self.ones_f128[:, :], self.AF[:, fXT:fXT + 512],
                start=first, stop=last), waits=[(VE, v)] + (self.need("sst") if oc == 0 else []), inc=PE)
            m2 = P.op("tensor", lambda t, first=(oc == 0), last=(oc == DC - 1): t.matmul(
                self.psum[0:1, 2560:3072], self.ones_f128[:, :], self.AF[:, fE1:fE1 + 512],
                start=first, stop=last), waits=[(SE, sq)], inc=PE)
            self.done("st_sq", PE, m2)
            self.done("st_sq2", PE, m2)
            self.done(("se_io2", slot), PE, m2)

    def mla_sublayer(self, xsrc, xdst):
        P = self.P
        self.h_pass(xsrc, router=False)
        P.barrier()
        self.ck()
        self.mla_latents()
        P.barrier()
        self.ck()
        self.mla_q()
        P.barrier()
        self.ck()
        self.tap("qm", self.qm_scr)
        self.tap("contribm", self.contribM)
        cc = self.ccs.pop(0)
        P.barrier()
        P.op("gpsimd", lambda g: g.collective_compute(
            "AllGather", ALU.bypass, replica_groups=[list(range(NCORE))],
            ins=[self.contribM.opt()], outs=[self.gathM.opt()]), inc=cc, amt=1)
        self.cc_cur = cc
        P.barrier()
        self.ck()
        self.mla_attn()
        P.barrier()
        self.tap("ctxm", self.ctx_scr)
        self.ck()
        self.dense_to_x(self.wbf["attn_proj"], xsrc, xdst, kind="proj", fuse_stats=True)

    def mla_latents(self):
        P = self.P
        sub = int(os.environ.get("KSUB", "9"))
        dp = None
        for kc in range(DC):
            dp = self.dma("sync", self.AB[:, AB_KVPAN + kc * 128:AB_KVPAN + (kc + 1) * 128],
                          self.wbf["wkvlat"][kc * 128:(kc + 1) * 128, :], stream="kvp")
        for t128 in range(DC):
            if int(os.environ.get("KSUB", "9")) < 1:
                continue
            par = t128 % 2
            pc = par * 512
            ring = AB_RING + par * 512
            m = None
            for kc in range(DC):
                waits = [dp, (SE, self.h1_se)]
                if kc == 0:
                    waits += self.need(("lat_ps", par))
                m = P.op("tensor", lambda t, kc_=kc, t_=t128, pc_=pc: t.matmul(
                    self.psum[:, pc_:pc_ + 128],
                    self.AB[:, AB_H1 + kc_ * TPC + t_ * 128:AB_H1 + kc_ * TPC + t_ * 128 + 128],
                    self.AB[:, AB_KVPAN + kc_ * 128:AB_KVPAN + (kc_ + 1) * 128],
                    start=(kc_ == 0), stop=(kc_ == DC - 1)), waits=waits, inc=PE)
            s = P.op("scalar", lambda e, pc_=pc, r_=ring: e.copy(
                out=self.AB[:, r_:r_ + 128], in_=self.psum[:, pc_:pc_ + 128]),
                waits=[(PE, m)] + self.need(("lat_ev", par)), inc=SE)
            self.done(("lat_ps", par), SE, s)
            if sub < 2:
                self.done(("lat_cs", par), SE, s)
                self.done(("lat_ev", par), SE, s)
                self.done(("lat_ps", par), SE, s)
                continue
            fc = AF_MISC + par * 1024
            dc_ = self.dma("sync", self.AF[:, fc:fc + 32],
                           self.rk_cos_in[t128 * 128:(t128 + 1) * 128, :],
                           waits=self.need(("lat_cs", par)), stream="lcs%d" % par)
            ds_ = self.dma("sync", self.AF[:, fc + 32:fc + 64],
                           self.rk_sin_in[t128 * 128:(t128 + 1) * 128, :], stream="lcs%d" % par)
            ta, tb, tc, td = fc + 64, fc + 96, fc + 128, fc + 160
            v = P.op("vector", lambda e, r_=ring, a=ta, fc_=fc: e.tensor_tensor(
                out=self.AF[:, a:a + 32], in0=self.AB[:, r_:r_ + 64:2], in1=self.AF[:, fc_:fc_ + 32],
                op=ALU.mult), waits=[(SE, s), ds_], inc=VE)
            v = P.op("vector", lambda e, r_=ring, b_=tb, fc_=fc: e.tensor_tensor(
                out=self.AF[:, b_:b_ + 32], in0=self.AB[:, r_ + 1:r_ + 64:2],
                in1=self.AF[:, fc_ + 32:fc_ + 64], op=ALU.mult), inc=VE)
            v = P.op("vector", lambda e, r_=ring, c_=tc, fc_=fc: e.tensor_tensor(
                out=self.AF[:, c_:c_ + 32], in0=self.AB[:, r_:r_ + 64:2],
                in1=self.AF[:, fc_ + 32:fc_ + 64], op=ALU.mult), inc=VE)
            v = P.op("vector", lambda e, r_=ring, d_=td, fc_=fc: e.tensor_tensor(
                out=self.AF[:, d_:d_ + 32], in0=self.AB[:, r_ + 1:r_ + 64:2],
                in1=self.AF[:, fc_:fc_ + 32], op=ALU.mult), inc=VE)
            v = P.op("vector", lambda e, r_=ring, a=ta, b_=tb: e.tensor_tensor(
                out=self.AB[:, r_:r_ + 64:2], in0=self.AF[:, a:a + 32], in1=self.AF[:, b_:b_ + 32],
                op=ALU.subtract), inc=VE)
            v = P.op("vector", lambda e, r_=ring, c_=tc, d_=td: e.tensor_tensor(
                out=self.AB[:, r_ + 1:r_ + 64:2], in0=self.AF[:, c_:c_ + 32], in1=self.AF[:, d_:d_ + 32],
                op=ALU.add), inc=VE)
            self.done(("lat_cs", par), VE, v)
            if sub < 3:
                self.done(("lat_ev", par), VE, v)
                self.done(("lat_ps", par), VE, v)
                continue
            trp = P.op("tensor", lambda t, r_=ring, par_=par: t.transpose(
                self.psb[:, par_ * 128:par_ * 128 + 128], self.AB[:, r_:r_ + 128], self.idb[:, :]),
                waits=[(VE, v)] + self.need(("psb", par)) + self.need("psb_bank"), inc=PE)
            v2 = P.op("vector", lambda e, par_=par: e.tensor_copy(
                out=self.AB[:, AB_QME + par_ * 512:AB_QME + par_ * 512 + 128],
                in_=self.psb[:, par_ * 128:par_ * 128 + 128]),
                waits=[(PE, trp)] + self.need(("lat_e2", par)), inc=VE)
            self.done(("psb", par), VE, v2)
            self.done("psb_bank", VE, v2)
            self.done(("lat_ev", par), VE, v2)
            d = self.dma("gpsimd", self.contribM[:, t128 * 128:(t128 + 1) * 128],
                         self.AB[:, AB_QME + par * 512:AB_QME + par * 512 + 128], waits=[(VE, v2)],
                         stream="lco%d" % par)
            self.done(("lat_e2", par), *d)

    def mla_q(self):
        P = self.P
        dke = None
        for hh in range(H):
            dke = self.dma("sync", self.AB[:, AB_KEXP + hh * 64:AB_KEXP + (hh + 1) * 64],
                           self.wbf["kexpT"][hh * 128:(hh + 1) * 128, :], stream="kexp")
        for vg in range(4):
            pb = AB_PANEL
            dp = self.dma("sync", self.AB[:, AB_PANEL:AB_PANEL + 8192],
                          self.wbf["attn_wq"][vg * 128:(vg + 1) * 128, :],
                          waits=self.need(("pan", 0)) + self.need(("pan", 1)), stream="pan")
            for t128 in range(DC):
                par = t128 % 2
                pc = par * 512
                ring = AB_RING + par * 1024
                m = None
                for kc in range(DC):
                    waits = [dp, (SE, self.h1_se)]
                    if kc == 0:
                        waits += self.need(("q_ps", par))
                    m = P.op("tensor", lambda t, kc_=kc, t_=t128, pc_=pc, pb_=pb: t.matmul(
                        self.psum[:, pc_:pc_ + 512],
                        self.AB[:, AB_H1 + kc_ * TPC + t_ * 128:AB_H1 + kc_ * TPC + t_ * 128 + 128],
                        self.AB[:, pb_ + kc_ * 512:pb_ + (kc_ + 1) * 512],
                        start=(kc_ == 0), stop=(kc_ == DC - 1)), waits=waits, inc=PE)
                s = P.op("scalar", lambda e, pc_=pc, r_=ring: e.copy(
                    out=self.AB[:, r_:r_ + 512], in_=self.psum[:, pc_:pc_ + 512]),
                    waits=[(PE, m)] + self.need(("q_ev", par)), inc=SE)
                self.done(("q_ps", par), SE, s)
                if int(os.environ.get("KSUB2", "9")) < 2:
                    self.done(("q_ev", par), SE, s)
                    self.done(("q_cs", par), SE, s)
                    self.done(("q_t", par), SE, s)
                    continue
                fc = AF_MISC + par * 1024
                dc_ = self.dma("sync", self.AF[:, fc:fc + 256],
                               self.rq_cos_in[t128 * 128:(t128 + 1) * 128, :],
                               waits=self.need(("q_cs", par)), stream="qcs%d" % par)
                ds_ = self.dma("sync", self.AF[:, fc + 256:fc + 512],
                               self.rq_sin_in[t128 * 128:(t128 + 1) * 128, :], stream="qcs%d" % par)
                ta, tb = AF_IO + 2048 + par * 1024, AF_IO + 2048 + par * 1024 + 256
                tc, td = AF_IO + 2048 + par * 1024 + 512, AF_IO + 2048 + par * 1024 + 768
                v = P.op("vector", lambda e, r_=ring, a=ta, fc_=fc: e.tensor_tensor(
                    out=self.AF[:, a:a + 256], in0=self.AB[:, r_:r_ + 512:2], in1=self.AF[:, fc_:fc_ + 256],
                    op=ALU.mult), waits=[(SE, s), ds_] + self.need(("q_t", par)), inc=VE)
                v = P.op("vector", lambda e, r_=ring, b_=tb, fc_=fc: e.tensor_tensor(
                    out=self.AF[:, b_:b_ + 256], in0=self.AB[:, r_ + 1:r_ + 512:2],
                    in1=self.AF[:, fc_ + 256:fc_ + 512], op=ALU.mult), inc=VE)
                v = P.op("vector", lambda e, r_=ring, c_=tc, fc_=fc: e.tensor_tensor(
                    out=self.AF[:, c_:c_ + 256], in0=self.AB[:, r_:r_ + 512:2],
                    in1=self.AF[:, fc_ + 256:fc_ + 512], op=ALU.mult), inc=VE)
                v = P.op("vector", lambda e, r_=ring, d_=td, fc_=fc: e.tensor_tensor(
                    out=self.AF[:, d_:d_ + 256], in0=self.AB[:, r_ + 1:r_ + 512:2],
                    in1=self.AF[:, fc_:fc_ + 256], op=ALU.mult), inc=VE)
                v = P.op("vector", lambda e, r_=ring, a=ta, b_=tb: e.tensor_tensor(
                    out=self.AB[:, r_:r_ + 512:2], in0=self.AF[:, a:a + 256], in1=self.AF[:, b_:b_ + 256],
                    op=ALU.subtract), inc=VE)
                v = P.op("vector", lambda e, r_=ring, c_=tc, d_=td: e.tensor_tensor(
                    out=self.AB[:, r_ + 1:r_ + 512:2], in0=self.AF[:, c_:c_ + 256], in1=self.AF[:, d_:d_ + 256],
                    op=ALU.add), inc=VE)
                self.done(("q_cs", par), VE, v)
                self.done(("q_t", par), VE, v)
                if int(os.environ.get("KSUB2", "9")) < 3:
                    self.done(("q_ev", par), VE, v)
                    continue
                for hl in range(4):
                    slot = (t128 * 4 + hl) % 8
                    trp = P.op("tensor", lambda t, r_=ring, hl_=hl, sl=slot: t.transpose(
                        self.psb[:, sl * 128:(sl + 1) * 128], self.AB[:, r_ + hl_ * 128:r_ + hl_ * 128 + 128],
                        self.idb[:, :]),
                        waits=[(VE, v)] + self.need(("psb", slot)) + self.need("psb_bank"), inc=PE)
                    v2 = P.op("vector", lambda e, hl_=hl, t_=t128, sl=slot: e.tensor_copy(
                        out=self.AB[:, AB_RQT + hl_ * 512 + (t_ % 4) * 128:
                                    AB_RQT + hl_ * 512 + (t_ % 4) * 128 + 128],
                        in_=self.psb[:, sl * 128:(sl + 1) * 128]),
                        waits=[(PE, trp)] + self.need(("rqt", hl, t128 % 4)), inc=VE)
                    self.done(("psb", slot), VE, v2)
                    self.done("psb_bank", VE, v2)
                    self.done(("q_t", par), VE, v2)
                    self.done(("rqt_v", hl, t128 % 4), VE, v2)
                if t128 % 4 == 3 and int(os.environ.get("KSUB2", "9")) >= 4:
                    tq = t128 // 4
                    bb = tq // 2
                    for hl in range(4):
                        h = vg * 4 + hl
                        wv = self.need(("rqt_v", hl, 0)) + self.need(("rqt_v", hl, 1)) + \
                            self.need(("rqt_v", hl, 2)) + self.need(("rqt_v", hl, 3)) + [dke]
                        mq = P.op("tensor", lambda t, h_=h, hl_=hl: t.matmul(
                            self.psum[0:DL, 1024:1536],
                            self.AB[:, AB_KEXP + h_ * 64:AB_KEXP + (h_ + 1) * 64],
                            self.AB[:, AB_RQT + hl_ * 512:AB_RQT + (hl_ + 1) * 512],
                            start=True, stop=True), waits=wv + self.need("qm_ps") + self.need("qn_rd"), inc=PE)
                        s2 = P.op("scalar", lambda e: e.copy(
                            out=self.AB[0:DL, AB_QME:AB_QME + 512], in_=self.psum[0:DL, 1024:1536]),
                            waits=[(PE, mq)] + self.need("qme"), inc=SE)
                        sq = P.op("scalar", lambda e: e.activation(
                            out=self.AB[0:DL, AB_SQ2:AB_SQ2 + 512], in_=self.psum[0:DL, 1024:1536],
                            func=AF_.Square), waits=[(PE, mq)], inc=SE)
                        self.done("qm_ps", SE, sq)
                        mn = P.op("tensor", lambda t: t.matmul(
                            self.psum[0:1, 1536:2048], self.ones_b[0:DL, :], self.AB[0:DL, AB_SQ2:AB_SQ2 + 512],
                            start=True, stop=True), waits=[(SE, sq)] + self.need("qn_ps"), inc=PE)
                        if tq % 2 == 0:
                            nv = P.op("vector", lambda e, b_=bb, h_=h: e.reduce_max(
                                out=self.rq2m[0:1, b_ * 16 + h_:b_ * 16 + h_ + 1],
                                in_=self.psum[0:1, 1536:2048], axis=AXX), waits=[(PE, mn)], inc=VE)
                            self.done("qn_rd", VE, nv)
                        else:
                            nv = P.op("vector", lambda e: e.reduce_max(
                                out=self.tiny[0:1, 48:49], in_=self.psum[0:1, 1536:2048], axis=AXX),
                                waits=[(PE, mn)], inc=VE)
                            nv = P.op("vector", lambda e, b_=bb, h_=h: e.tensor_tensor(
                                out=self.rq2m[0:1, b_ * 16 + h_:b_ * 16 + h_ + 1],
                                in0=self.rq2m[0:1, b_ * 16 + h_:b_ * 16 + h_ + 1],
                                in1=self.tiny[0:1, 48:49], op=ALU.max), inc=VE)
                            self.done("qn_rd", VE, nv)
                        self.done("qn_ps", VE, nv)
                        d = self.dma("gpsimd",
                                     self.qm_scr[(bb * H + h) * DL:(bb * H + h) * DL + DL,
                                                 (tq % 2) * 512:(tq % 2) * 512 + 512],
                                     self.AB[0:DL, AB_QME:AB_QME + 512], waits=[(SE, s2)],
                                     stream="qme")
                        self.done("qme", *d)
                    for hl in range(4):
                        for qq in range(4):
                            self.done(("rqt", hl, qq), PE, P.counts[PE])
            self.done(("pan", 0), PE, P.counts[PE])
        self.done(("q_ps", 0), SE, self.P.counts[SE])
        self.done(("q_ps", 1), SE, self.P.counts[SE])

    def mla_attn(self):
        P = self.P
        cc = self.cc_cur
        exp_at_slot = {}
        pv_at_probs = {}
        for bb in range(B):
            dk = None
            for rr in range(NCORE):
                self.dma("sync", self.AB[0:DL, AB_KLAT + rr * 1024:AB_KLAT + (rr + 1) * 1024],
                         self.gathM[rr * 128:rr * 128 + DL, bb * NQB:(bb + 1) * NQB],
                         waits=[(cc, 1)] + self.need("mlakv"), stream="mkv")
                dk = self.dma("sync", self.AB[0:DL, AB_VLT + rr * 1024:AB_VLT + (rr + 1) * 1024],
                              self.gathM[rr * 128 + DL:(rr + 1) * 128, bb * NQB:(bb + 1) * NQB],
                              stream="mkv")
            sqk = P.op("scalar", lambda e: e.activation(out=self.AB[0:DL, AB_SQK:AB_SQK + 8192],
                                                        in_=self.AB[0:DL, AB_KLAT:AB_KLAT + 8192],
                                                        func=AF_.Square), waits=[dk], inc=SE)
            nv = None
            for i in range(16):
                mn = P.op("tensor", lambda t, i_=i: t.matmul(
                    self.psum[0:1, 1536:2048], self.ones_b[0:DL, :],
                    self.AB[0:DL, AB_SQK + i_ * 512:AB_SQK + (i_ + 1) * 512],
                    start=True, stop=True),
                    waits=[(SE, sqk)] + ([(VE, nv)] if nv else []) + self.need("mh_m2"), inc=PE)
                if i == 0:
                    nv = P.op("vector", lambda e, b_=bb: e.reduce_max(
                        out=self.rk2m[0:1, b_:b_ + 1], in_=self.psum[0:1, 1536:2048], axis=AXX),
                        waits=[(PE, mn)], inc=VE)
                else:
                    nv = P.op("vector", lambda e: e.reduce_max(
                        out=self.tiny[0:1, 50:51], in_=self.psum[0:1, 1536:2048], axis=AXX),
                        waits=[(PE, mn)], inc=VE)
                    nv = P.op("vector", lambda e, b_=bb: e.tensor_tensor(
                        out=self.rk2m[0:1, b_:b_ + 1], in0=self.rk2m[0:1, b_:b_ + 1],
                        in1=self.tiny[0:1, 50:51], op=ALU.max), inc=VE)
            vm = P.op("vector", lambda e: e.memset(self.AB[:, AB_VT:AB_VT + NKT * 65], 1.0),
                      waits=self.need("vt_ws"), inc=VE)
            vtv = None
            for j in range(NKT):
                slot = j % 8
                trp = P.op("tensor", lambda t, j_=j, sl=slot: t.transpose(
                    self.psb[:, sl * 128:sl * 128 + 64],
                    self.AB[0:DL, AB_VLT + (j_ % 8) * 1024 + (j_ // 8) * 128:
                            AB_VLT + (j_ % 8) * 1024 + (j_ // 8) * 128 + 128],
                    self.idb[0:DL, 0:DL]),
                    waits=[dk, (VE, vm)] + self.need(("psb", slot)) + self.need("psb_bank"), inc=PE)
                vtv = P.op("vector", lambda e, j_=j, sl=slot: e.tensor_copy(
                    out=self.AB[:, AB_VT + j_ * 65:AB_VT + j_ * 65 + 64],
                    in_=self.psb[:, sl * 128:sl * 128 + 64]), waits=[(PE, trp)], inc=VE)
                self.done(("psb", slot), VE, vtv)
                self.done("psb_bank", VE, vtv)
            for h in range(H):
                buf = h % 2
                dq = self.dma("sync", self.AB[0:DL, AB_QM + buf * 1024:AB_QM + (buf + 1) * 1024],
                              self.qm_scr[(bb * H + h) * DL:(bb * H + h) * DL + DL, :],
                              waits=self.need(("qm_b", buf)), stream="qmb%d" % buf)
                v = P.op("vector", lambda e, b_=bb, h_=h: e.tensor_tensor(
                    out=self.tiny[0:1, 52:53], in0=self.rq2m[0:1, b_ * 16 + h_:b_ * 16 + h_ + 1],
                    in1=self.rk2m[0:1, b_:b_ + 1], op=ALU.add), waits=[(VE, nv)], inc=VE)
                v = P.op("vector", lambda e: e.tensor_scalar(
                    out=self.tiny[0:1, 53:54], in0=self.tiny[0:1, 52:53],
                    scalar1=float(-0.5 * SCALE), scalar2=None, op0=ALU.mult), inc=VE)
                mb = P.op("tensor", lambda t: t.matmul(
                    self.psum[:, 1536:1537], self.ones_f1[:, :], self.tiny[0:1, 53:54],
                    start=True, stop=True), waits=[(VE, v)] + self.need("mh_m") + self.need("mh_m2"),
                    inc=PE)
                mhv = P.op("vector", lambda e, c_=buf: e.tensor_copy(
                    out=self.mh[:, c_:c_ + 1], in_=self.psum[:, 1536:1537]), waits=[(PE, mb)], inc=VE)
                self.done("mh_m", VE, mhv)
                ring = 0
                probs_info = {}
                acc_last = None
                for j in range(NKT):
                    jo = j % 8
                    for si, (qs, n) in enumerate(slabs_for(j)):
                        scol = 2048 + 512 * (ring % 3)
                        ring += 1
                        pcol = AB_PROBS + ((si * 2 + (j % 2)) % 4) * 512
                        waits = [dq, (VE, mhv), (SE, exp_at_slot.get(scol, 0))]
                        pe_c = P.op("tensor", lambda t, sc=scol, jj=j, q0=qs, nn=n, b2=buf:
                                    t.matmul(self.psum[:, sc:sc + nn],
                                             self.AB[0:DL, AB_KLAT + (jj % 8) * 1024 + (jj // 8) * 128:
                                                     AB_KLAT + (jj % 8) * 1024 + (jj // 8) * 128 + 128],
                                             self.AB[0:DL, AB_QM + b2 * 1024 + q0:AB_QM + b2 * 1024 + q0 + nn],
                                             start=True, stop=True), waits=waits, inc=PE)
                        waits = [(PE, pe_c), (PE, pv_at_probs.get(pcol, 0))]
                        se_c = P.op("scalar", lambda s, sc=scol, pc=pcol, c_=buf, nn=n:
                                    s.activation(out=self.AB[:, pc:pc + nn], in_=self.psum[:, sc:sc + nn],
                                                 func=AF_.Exp, bias=self.mh[:, c_:c_ + 1], scale=SCALE),
                                    waits=waits, inc=SE)
                        exp_at_slot[scol] = se_c
                        ve_c = 0
                        if qs == 128 * (j // 8):
                            ve_c = P.op("vector", lambda v2, pc=pcol, jo_=jo: v2.tensor_tensor(
                                out=self.AB[:, pc:pc + 128], in0=self.AB[:, pc:pc + 128],
                                in1=self.masks[:, jo_ * 128:(jo_ + 1) * 128], op=ALU.mult),
                                waits=[(SE, se_c)], inc=VE)
                        probs_info[(j, si)] = (se_c, ve_c, pcol, qs, n)
                    for si, (qs, n) in enumerate(slabs_for(j)):
                        se_c, ve_c, pcol, _, _ = probs_info[(j, si)]
                        waits = [(SE, se_c), (VE, vtv)]
                        if ve_c:
                            waits.append((VE, ve_c))
                        if j == 0:
                            waits += self.need("macc")
                        if jo == 7 and qs == 128 * (j // 8):
                            pieces = [(qs, 128, True)]
                            if n > 128:
                                pieces.append((qs + 128, n - 128, j == NKT - 1))
                        else:
                            pieces = [(qs, n, j == NKT - 1)]
                        for (pq, pn, pstop) in pieces:
                            acc_last = P.op("tensor", lambda t, jj=j, pc=pcol, q0=pq, nn=pn, qs_=qs, st=pstop:
                                            t.matmul(self.psum[0:65, q0:q0 + nn],
                                                     self.AB[:, AB_VT + jj * 65:AB_VT + (jj + 1) * 65],
                                                     self.AB[:, pc + (q0 - qs_):pc + (q0 - qs_) + nn],
                                                     start=(jj == 0), stop=st, skip_group_check=True),
                                            waits=waits, inc=PE)
                        pv_at_probs[pcol] = acc_last
                vrl = P.op("vector", lambda e: e.reciprocal(out=self.AF[0:1, AF_MISC:AF_MISC + 1024],
                                                            in_=self.psum[64:65, 0:1024]),
                           waits=[(PE, acc_last)], inc=VE)
                bc = None
                for i2 in range(2):
                    bc = P.op("tensor", lambda t, i_=i2: t.matmul(
                        self.psum[0:DL, 1024 + i_ * 512:1024 + i_ * 512 + 512], self.ones_f1[:, 0:DL],
                        self.AF[0:1, AF_MISC + i_ * 512:AF_MISC + (i_ + 1) * 512],
                        start=True, stop=True), waits=[(VE, vrl)] + (self.need("mrb") if i2 == 0 else []),
                        inc=PE)
                vb = P.op("vector", lambda e: e.tensor_copy(out=self.AF[0:DL, AF_MISC:AF_MISC + 1024],
                                                            in_=self.psum[0:DL, 1024:2048]),
                          waits=[(PE, bc)], inc=VE)
                vy = P.op("vector", lambda e: e.tensor_tensor(
                    out=self.AB[0:DL, AB_YN:AB_YN + 1024], in0=self.psum[0:DL, 0:1024],
                    in1=self.AF[0:DL, AF_MISC:AF_MISC + 1024], op=ALU.mult),
                    waits=[(VE, vb)] + self.need("yn"), inc=VE)
                self.done("macc", VE, vy)
                dv = self.dma("sync", self.AB[0:DL, AB_VEXP:AB_VEXP + 128],
                              self.wbf["vexp"][h * DL:(h + 1) * DL, :], waits=self.need("vexp"),
                              stream="vex")
                my = P.op("tensor", lambda t: t.matmul(
                    self.psum[:, 1024:1536], self.AB[0:DL, AB_VEXP:AB_VEXP + 128],
                    self.AB[0:DL, AB_YN:AB_YN + 512], start=True, stop=True),
                    waits=[(VE, vy), dv, (VE, vb)], inc=PE)
                my2 = P.op("tensor", lambda t: t.matmul(
                    self.psum[:, 1536:2048], self.AB[0:DL, AB_VEXP:AB_VEXP + 128],
                    self.AB[0:DL, AB_YN + 512:AB_YN + 1024], start=True, stop=True),
                    waits=self.need("mh_m2"), inc=PE)
                self.done("mrb", PE, my2)
                self.done("vexp", PE, my2)
                sy = P.op("scalar", lambda e: e.copy(out=self.AB[:, AB_MYE:AB_MYE + 1024],
                                                     in_=self.psum[:, 1024:2048]),
                          waits=[(PE, my2)] + self.need("ydma"), inc=SE)
                self.done("yn", PE, my2)
                self.done("mh_m2", SE, sy)
                dy = self.dma("gpsimd", self.ctx_scr[h * 128:(h + 1) * 128, bb * NQB:(bb + 1) * NQB],
                              self.AB[:, AB_MYE:AB_MYE + 1024], waits=[(SE, sy)], stream="ymy")
                self.done("ydma", *dy)
                self.done(("qm_b", buf), PE, acc_last)
            self.done("mlakv", PE, self.P.counts[PE])
            self.done("vt_ws", PE, self.P.counts[PE])


def build_nc():
    nc = bass.Bass("TRN2", target_bir_lowering=False, debug=False)
    Builder(nc).build()
    return nc


def _host_inputs(inputs, r):
    asc = np.ascontiguousarray
    f32 = np.float32
    d = {}
    x = inputs["L0_attn_x"]
    d["x0"] = asc(x.reshape(2, 64, 128, 2048)[:, r::8].transpose(3, 0, 1, 2).reshape(2048, 2048))
    return d


_SHARED_CACHE = {}


def _host_shared(inputs):
    if _SHARED_CACHE.get("done"):
        return _SHARED_CACHE
    asc = np.ascontiguousarray
    f32 = np.float32
    s = _SHARED_CACHE
    s["ffn_wqT"] = asc(inputs["L0_ffn_wq"].T)
    s["ffn_wkT"] = asc(inputs["L0_ffn_wk"].T)
    s["ffn_wvT"] = asc(inputs["L0_ffn_wv"].T)
    s["ffn_woT"] = asc(inputs["L0_ffn_wo"].T)
    s["ff1T"] = asc(inputs["L0_ffn_w_ff1"].T)
    s["ff2T"] = asc(inputs["L0_ffn_w_ff2"].T)
    s["wdownT"] = asc(inputs["L0_ffn_w_down"].T)
    s["wupT"] = asc(inputs["L0_ffn_w_up"].T)
    s["attn_wq"] = asc(inputs["L0_attn_wq"])
    s["attn_proj"] = asc(inputs["L0_attn_proj_w"])
    s["wkvlat"] = asc(np.concatenate([inputs["L0_attn_wk_lat"], inputs["L0_attn_wv_lat"]], axis=1))
    s["kexpT"] = asc(inputs["L0_attn_k_expand"].transpose(0, 2, 1).reshape(H * DH, DL))
    s["vexp"] = asc(inputs["L0_attn_v_expand"].reshape(H * DL, DH))
    wrT = inputs["L0_ffn_w_router"].T
    s["wrouter_sb"] = asc(wrT.reshape(16, 128, 3).transpose(1, 0, 2).reshape(128, 48))
    for name, key in [("bias_v", "L0_ffn_simple_bias"), ("ln1s_v", "L0_ffn_ln1_scale"),
                      ("ln1b_v", "L0_ffn_ln1_bias"), ("ln2s_v", "L0_ffn_ln2_scale"),
                      ("ln2b_v", "L0_ffn_ln2_bias")]:
        s[name] = asc(inputs[key].reshape(16, 128).T)
    s["ident_f"] = np.eye(128, dtype=f32)
    s["ident_b"] = np.eye(128, dtype=f32).astype(BFNP)
    inv_q = (1.0 / 10000.0 ** (np.arange(0, DH, 2, dtype=f32) / DH)).astype(f32)
    inv_k = (1.0 / 10000.0 ** (np.arange(0, DL, 2, dtype=f32) / DL)).astype(f32)
    tpos = np.arange(T, dtype=f32)
    angq = tpos[:, None] * inv_q[None, :]
    angk = tpos[:, None] * inv_k[None, :]
    s["cos_q"] = np.cos(angq, dtype=f32)
    s["sin_q"] = np.sin(angq, dtype=f32)
    s["cos_k"] = np.cos(angk, dtype=f32)
    s["sin_k"] = np.sin(angk, dtype=f32)
    s["done"] = True
    return s


def _core_inputs(inputs, s, r):
    asc = np.ascontiguousarray
    f32 = np.float32
    d = {k: s[k] for k in ("ffn_wqT", "ffn_wkT", "ffn_wvT", "ffn_woT", "ff1T", "ff2T",
                           "wdownT", "wupT", "attn_wq", "attn_proj", "wkvlat", "kexpT",
                           "vexp", "wrouter_sb", "bias_v", "ln1s_v", "ln1b_v", "ln2s_v",
                           "ln2b_v", "ident_f", "ident_b")}
    x = inputs["L0_attn_x"]
    d["x0"] = asc(x.reshape(2, 64, 128, 2048)[:, r::8].transpose(3, 0, 1, 2).reshape(2048, 2048))
    kk = np.arange(128)[:, None]
    qq = np.arange(128)[None, :]
    m = np.zeros((8, 128, 128), f32)
    for jo in range(8):
        m[jo] = (qq + (r - jo) * 128 >= kk)
    d["masks_i"] = m.astype(BFNP)
    mm, ii = np.meshgrid(np.arange(8), np.arange(128), indexing="ij")
    pos1 = ((8 * mm + r) * 128 + ii).reshape(-1)
    pos = np.concatenate([pos1, pos1])
    d["rq_cos"] = asc(np.tile(s["cos_q"][pos], (1, 4)))
    d["rq_sin"] = asc(np.tile(s["sin_q"][pos], (1, 4)))
    d["rk_cos"] = asc(s["cos_k"][pos])
    d["rk_sin"] = asc(s["sin_k"][pos])
    return d


def kernel(**inputs):
    s = dict()
    _SHARED_CACHE.clear()
    sh = _host_shared(inputs)
    in_maps = [_core_inputs(inputs, sh, r) for r in range(NCORE)]
    nc = build_nc()
    res = run_bass_kernel_spmd(nc, in_maps, core_ids=list(range(NCORE)))
    y = np.empty((2, 8192, 2048), np.float32)
    yv = y.reshape(2, 64, 128, 2048)
    for r in range(NCORE):
        o = res.results[r]["out"]
        yv[:, r::8] = o.reshape(2048, 2, 8, 128).transpose(1, 2, 3, 0)
    if DEBUG:
        kernel.last_results = res
    return y


# version 05
# speedup vs baseline: 1.4013x; 1.1880x over previous
import os
import numpy as np
import ml_dtypes
import concourse.bass as bass
import concourse.mybir as mybir
from concourse.bass_utils import run_bass_kernel_spmd
from contextlib import ExitStack

F32 = mybir.dt.float32
BF16 = mybir.dt.bfloat16
AF_ = mybir.ActivationFunctionType
ALU = mybir.AluOpType
AXX = mybir.AxisListType.X
BFNP = ml_dtypes.bfloat16

B, T, D, H, DH, DL, FFD, RLOW = 2, 8192, 2048, 16, 128, 64, 8192, 32
NCORE = 8
TPC = 2048
NQB = 1024
DC = 16
NKT = 64
SCALE = float(1.0 / np.sqrt(128.0))
EPS = 1e-5

DEBUG = False
ENGINES = ("sync", "tensor", "vector", "scalar", "gpsimd")
DMA, DMB, PE, SE, VE = "dma", "dmb", "pe", "se", "ve"


class Plan:
    def __init__(self):
        self.ops = {e: [] for e in ENGINES}
        self.counts = {PE: 0, SE: 0, VE: 0}
        self._barrier = {}

    def op(self, engine, fn, waits=(), inc=None, amt=1):
        w = list(waits)
        if self._barrier.get(engine):
            w = list(self._barrier[engine]) + w
            self._barrier[engine] = None
        if engine in ("vector", "scalar") and inc is not None:
            prev = self.counts.get(inc, 0)
            if prev:
                w.append((inc, prev))
        if inc is not None:
            self.counts[inc] = self.counts.get(inc, 0) + amt
        self.ops[engine].append((fn, tuple(w), inc, amt))
        return self.counts.get(inc) if inc is not None else None

    def barrier(self):
        snap = [(s, c) for s, c in self.counts.items() if c > 0]
        for e in ENGINES:
            self._barrier[e] = snap

    def emit(self, block, sems):
        for ename in ENGINES:
            ops = self.ops[ename]
            if not ops:
                continue
            dec = getattr(block, ename)

            def section(eng, _ops=ops):
                waited = {}
                for fn, waits, inc, amt in _ops:
                    for sname, val in waits:
                        if val and val > 0 and waited.get(sname, 0) < val:
                            eng.wait_ge(sems[sname], val)
                            waited[sname] = val
                    ins = fn(eng)
                    if inc is not None:
                        ins.then_inc(sems[inc], amt)
            dec(section)


def slabs_for(j):
    qs = 128 * (j // 8)
    out = []
    if qs < 512:
        out.append((qs, 512 - qs))
    if 1024 - max(qs, 512) > 0:
        out.append((max(qs, 512), 1024 - max(qs, 512)))
    return out


AB_H1 = 0
AB_PANEL = 32768
AB_EVAC = 49152
AB_WDN = 32768
AB_WUP = 33280
AB_LR2E = 35328
AB_KSB = 0
AB_PROBS = 16384
AB_PRES = 19456
AB_VR = 20480
AB_Q = 21504
AB_SB1 = 23552
AB_YS = 25600
AB_CTX = 0
AB_SLAB = 32768
AB_G = 0
AB_XN2 = 32768
AB_FSLAB = 40960
AB_SBC = 49152
AB_LR2T = 50176
AB_KVPAN = 32768
AB_KEXP = 40960
AB_RQT = 41984
AB_RING = 44032
AB_QME = 46080
AB_SQ2 = 47104
AB_MYE = 45056
AB_KLAT = 0
AB_VLT = 8192
AB_SQK = 16384
AB_VT = 24576
AB_QM = 28736
AB_YN = 30784
AB_VEXP = 31808

AF_IO = 0
AF_MUB = 4096
AF_RSB = 6144
AF_MU2B = 8192
AF_RS2B = 10240
AF_MISC = 12288


class Builder:
    def __init__(self, nc):
        self.nc = nc
        self.P = Plan()
        self.war = {}
        self.ccs = ["cc0", "cc1", "cc2"]

    def need(self, key):
        v = self.war.get(key)
        return list(v.items()) if v else []

    def done(self, key, sem, count):
        d = self.war.setdefault(key, {})
        if count and count > d.get(sem, 0):
            d[sem] = count

    def dma(self, q, dst, src, waits=(), stream="misc"):
        sem = "D" + stream
        c = self.P.op(q, lambda e, d=dst, s=src: e.dma_start(out=d, in_=s),
                      waits=waits, inc=sem, amt=16)
        return (sem, c)

    def build(self):
        nc = self.nc
        ap = lambda name, shape, dt, kind="ExternalInput": nc.dram_tensor(name, shape, dt, kind=kind).ap()
        self.x0 = ap("x0", [D, TPC], F32)
        self.out = ap("out", [D, TPC], F32, "ExternalOutput")
        self.w_in = {}
        for name, shape in [
            ("ffn_wqT", [D, D]), ("ffn_wkT", [D, D]), ("ffn_wvT", [D, D]), ("ffn_woT", [D, D]),
            ("ff1T", [D, FFD]), ("ff2T", [FFD, D]),
            ("wdownT", [D, RLOW]), ("wupT", [RLOW, D]),
            ("attn_wq", [D, D]), ("attn_proj", [D, D]), ("wkvlat", [D, 2 * DL]),
            ("kexpT", [H * DH, DL]), ("vexp", [H * DL, DH]),
        ]:
            self.w_in[name] = ap(name, shape, F32)
        self.wrouter_in = ap("wrouter_sb", [128, DC * 3], F32)
        self.vec_in = {n: ap(n, [128, DC], F32)
                       for n in ("bias_v", "ln1s_v", "ln1b_v", "ln2s_v", "ln2b_v")}
        self.ident_f_in = ap("ident_f", [128, 128], F32)
        self.ident_b_in = ap("ident_b", [128, 128], BF16)
        self.masks_in = ap("masks_i", [8, 128, 128], BF16)
        self.rq_cos_in = ap("rq_cos", [TPC, 256], F32)
        self.rq_sin_in = ap("rq_sin", [TPC, 256], F32)
        self.rk_cos_in = ap("rk_cos", [TPC, 32], F32)
        self.rk_sin_in = ap("rk_sin", [TPC, 32], F32)

        dt_ = lambda name, shape, dt, **kw: nc.dram_tensor(name, shape, dt, kind="Internal", **kw).ap()
        self.tiled = {"ffn_wqT", "ffn_wkT", "ffn_wvT", "ffn_woT", "attn_wq", "attn_proj",
                      "ff1T", "ff2T"}
        def bf_shape(n, t):
            if n in self.tiled:
                rows, cols = t.shape
                return [(cols // 512) * 128, (rows // 128) * 512]
            return list(t.shape)
        self.wbf = {n: dt_("bf_" + n, bf_shape(n, t), BF16) for n, t in self.w_in.items()}
        self.x_a = dt_("x_a", [D, TPC], F32)
        self.x_b = dt_("x_b", [D, TPC], F32)
        self.xt_scr = dt_("xt_scr", [D, TPC], F32)
        self.lr2_scr = dt_("lr2_scr", [D, TPC], BF16)
        self.q_scr = dt_("q_scr", [B * H * 128, NQB], BF16)
        self.qm_scr = dt_("qm_scr", [B * H * DL, NQB], BF16)
        self.ctx_scr = dt_("ctx_scr", [D, TPC], BF16)
        self.CF_K, self.CF_KN, self.CF_V, self.CF_ROWS = 0, 2048, 2064, 4112
        self.contribF = dt_("contribF", [self.CF_ROWS, TPC], BF16)
        self.gathF = dt_("gathF", [NCORE * self.CF_ROWS, TPC], BF16, addr_space="Shared")
        self.contribM = dt_("contribM", [128, TPC], BF16)
        self.gathM = dt_("gathM", [NCORE * 128, TPC], BF16, addr_space="Shared")
        self.dbg = {}
        self.dbg_out = {}
        if DEBUG:
            for n, shape, dt2 in [("x1", [D, TPC], F32), ("xt", [D, TPC], F32),
                                  ("ctx", [D, TPC], BF16), ("q", [B * H * 128, NQB], BF16),
                                  ("contrib", [self.CF_ROWS, TPC], BF16),
                                  ("lr2", [D, TPC], BF16), ("smask", [1, 2 * TPC], BF16),
                                  ("x2", [D, TPC], F32), ("ctxm", [D, TPC], BF16),
                                  ("qm", [B * H * DL, NQB], BF16), ("contribm", [128, TPC], BF16)]:
                self.dbg[n] = dt_("dbg_" + n, shape, dt2)
                self.dbg_out[n] = ap("dout_" + n, shape, dt2, "ExternalOutput")

        with ExitStack() as st:
            E = st.enter_context
            sb = lambda name, shape, dt: E(nc.sbuf_tensor(name, shape, dt))
            self.masks = sb("masks", [128, 8 * 128], BF16)
            self.idf = sb("idf", [128, 128], F32)
            self.idb = sb("idb", [128, 128], BF16)
            self.ones_b = sb("ones_b", [128, 1], BF16)
            self.ones_b1 = sb("ones_b1", [1, 128], BF16)
            self.ones_f1 = sb("ones_f1", [1, 128], F32)
            self.ones_f128 = sb("ones_f128", [128, 1], F32)
            self.vec_sb = {n: sb("sb_" + n, [128, DC], F32) for n in self.vec_in}
            self.wrouter = sb("wrouter", [128, DC * 3], F32)
            self.statn = sb("statn", [1, 2 * TPC], F32)
            self.smask = sb("smask", [1, 2 * TPC], BF16)
            self.rq2 = sb("rq2", [1, 64], F32)
            self.rq2m = sb("rq2m", [1, 64], F32)
            self.rk2m = sb("rk2m", [1, 4], F32)
            self.mh = sb("mh", [128, 4], F32)
            self.tiny = sb("tiny", [1, 64], F32)
            self.epsv = sb("epsv", [1, 1], F32)
            self.AB = sb("AB", [128, 53248], BF16)
            self.AF = sb("AF", [128, 14336], F32)
            self.psum = E(nc.psum_tensor("psum", [128, 3584], F32))
            self.psb = E(nc.psum_tensor("psb", [128, 1024], BF16))

            self.emit_all()

            sems = {}
            for name in sorted(self.P.counts):
                sems[name] = E(nc.semaphore(name.replace(":", "_")))
            block = E(nc.Block())
            self.P.emit(block, sems)
        return nc

    def tap(self, name, src):
        if not DEBUG or name not in self.dbg:
            return
        w = [(s, c) for s, c in self.P.counts.items() if c > 0]
        d = self.dma("gpsimd", self.dbg[name], src, waits=w, stream="tap")
        self.dma("gpsimd", self.dbg_out[name], self.dbg[name], waits=[d], stream="tap")

    class _Stop(Exception):
        pass

    def ck(self):
        self._phase += 1
        if self._phase > int(os.environ.get("KSTOP", "99")):
            raise Builder._Stop()

    def emit_all(self):
        self._phase = 0
        try:
            if os.environ.get("KONLY") == "lat":
                P = self.P
                self.load_consts()
                self.prep_weights()
                P.barrier()
                self.stats_pass(self.x0)
                P.barrier()
                self.finalize_stats(AF_MUB, AF_RSB)
                P.barrier()
                self.h_pass(self.x0, router=False)
                P.barrier()
                self.mla_latents()
                raise Builder._Stop()
            self._emit_all()
        except Builder._Stop:
            pass
        fin = [(s, c) for s, c in self.P.counts.items() if c > 0]
        self.P.op("gpsimd", lambda g: g.engine_nop(), waits=fin)

    def _emit_all(self):
        P = self.P
        self.load_consts()
        self.ck()
        self.prep_weights()
        self.ck()
        self.stats_pass(self.x0)
        P.barrier()
        self.finalize_stats(AF_MUB, AF_RSB)
        P.barrier()
        self.ck()
        self.ffn_sublayer(self.x0, self.x_a, final=False)
        self.tap("x1", self.x_a)
        P.barrier()
        self.finalize_stats(AF_MUB, AF_RSB)
        P.barrier()
        self.ck()
        self.mla_sublayer(self.x_a, self.x_b)
        self.tap("x2", self.x_b)
        P.barrier()
        self.finalize_stats(AF_MUB, AF_RSB)
        P.barrier()
        self.ck()
        self.ffn_sublayer(self.x_b, self.out, final=True)

    def load_consts(self):
        for jo in range(8):
            self.dma("sync", self.masks[:, jo * 128:(jo + 1) * 128], self.masks_in[jo], stream="consts")
        self.dma("sync", self.idf[:, :], self.ident_f_in, stream="consts")
        self.dma("sync", self.idb[:, :], self.ident_b_in, stream="consts")
        for n in self.vec_in:
            self.dma("sync", self.vec_sb[n][:, :], self.vec_in[n], stream="consts")
        self.dma("sync", self.wrouter[:, :], self.wrouter_in, stream="consts")
        P = self.P
        P.op("vector", lambda v: v.memset(self.ones_b[:], 1.0), inc=VE)
        P.op("vector", lambda v: v.memset(self.ones_b1[:], 1.0), inc=VE)
        P.op("vector", lambda v: v.memset(self.ones_f1[:], 1.0), inc=VE)
        P.op("vector", lambda v: v.memset(self.ones_f128[:], 1.0), inc=VE)
        P.op("vector", lambda v: v.memset(self.epsv[:], float(EPS)), inc=VE)

    def prep_weights(self):
        for name, src in self.w_in.items():
            dst = self.wbf[name]
            rows, cols = src.shape
            d = None
            if name in self.tiled:
                ncol = cols // 512
                re = src.rearrange("(k p) (o c) -> o p k c", p=128, c=512)
                for og in range(ncol):
                    d = self.dma("gpsimd", dst[og * 128:(og + 1) * 128, :], re[og],
                                 stream="wp%d" % (og % 2))
            else:
                d = self.dma("gpsimd", dst[:, :], src, stream="wp0")
            self.done(("wrdy", name), *d)
            if name in self.tiled and ncol > 1:
                self.done(("wrdy", name), "Dwp%d" % ((ncol - 2) % 2), self.P.counts.get("Dwp%d" % ((ncol - 2) % 2), 0))

    def stats_pass(self, xsrc):
        P = self.P
        for tg in range(4):
            m2 = None
            for kc in range(DC):
                p = kc % 2
                fo = AF_IO + p * 512
                d = self.dma("sync", self.AF[:, fo:fo + 512],
                             xsrc[kc * 128:(kc + 1) * 128, tg * 512:(tg + 1) * 512],
                             waits=self.need(("st", p)), stream="stx%d" % p)
                s = P.op("scalar", lambda e, f_=fo: e.activation(
                    out=self.AF[:, f_ + 1024:f_ + 1536], in_=self.AF[:, f_:f_ + 512], func=AF_.Square),
                    waits=[d], inc=SE)
                P.op("tensor", lambda t, f_=fo, first=(kc == 0), last=(kc == DC - 1): t.matmul(
                    self.psum[0:1, 0:512], self.ones_f128[:, :], self.AF[:, f_:f_ + 512],
                    start=first, stop=last),
                    waits=[d] + (self.need("st_ev") if kc == 0 else []), inc=PE)
                m2 = P.op("tensor", lambda t, f_=fo, first=(kc == 0), last=(kc == DC - 1): t.matmul(
                    self.psum[0:1, 512:1024], self.ones_f128[:, :], self.AF[:, f_ + 1024:f_ + 1536],
                    start=first, stop=last), waits=[(SE, s)], inc=PE)
                self.done(("st", p), PE, m2)
            v = P.op("vector", lambda e, tg_=tg: e.tensor_copy(
                out=self.statn[:, tg_ * 512:tg_ * 512 + 512], in_=self.psum[0:1, 0:512]),
                waits=[(PE, m2)], inc=VE)
            v = P.op("vector", lambda e, tg_=tg: e.tensor_copy(
                out=self.statn[:, TPC + tg_ * 512:TPC + tg_ * 512 + 512], in_=self.psum[0:1, 512:1024]),
                waits=[(PE, m2)], inc=VE)
            self.done(("st", 0), VE, v)
            self.done(("st", 1), VE, v)
            self.done("st_ev", VE, v)

    def finalize_stats(self, mub_col, rsb_col):
        P = self.P
        v = P.op("vector", lambda e: e.tensor_scalar(out=self.statn[:, :], in0=self.statn[:, :],
                                                     scalar1=float(1.0 / D), scalar2=None, op0=ALU.mult),
                 inc=VE)
        v = P.op("vector", lambda e: e.tensor_tensor(out=self.AF[0:1, AF_MISC:AF_MISC + TPC],
                                                     in0=self.statn[:, 0:TPC], in1=self.statn[:, 0:TPC],
                                                     op=ALU.mult), waits=[(VE, v)], inc=VE)
        v = P.op("vector", lambda e: e.tensor_tensor(out=self.statn[:, TPC:2 * TPC],
                                                     in0=self.statn[:, TPC:2 * TPC],
                                                     in1=self.AF[0:1, AF_MISC:AF_MISC + TPC],
                                                     op=ALU.subtract), waits=[(VE, v)], inc=VE)
        s = P.op("scalar", lambda e: e.activation(out=self.AF[0:1, AF_MISC:AF_MISC + TPC],
                                                  in_=self.statn[:, TPC:2 * TPC],
                                                  func=AF_.Sqrt, bias=self.epsv[:, :]),
                 waits=[(VE, v)], inc=SE)
        v = P.op("vector", lambda e: e.reciprocal(out=self.statn[:, TPC:2 * TPC],
                                                  in_=self.AF[0:1, AF_MISC:AF_MISC + TPC]),
                 waits=[(SE, s)], inc=VE)
        evs = {}
        for half, col in ((0, mub_col), (1, rsb_col)):
            for i in range(4):
                w = [(VE, v)] + ([(VE, evs[i])] if i in evs else [])
                m = P.op("tensor", lambda t, h_=half, i_=i: t.matmul(
                    self.psum[:, i_ * 512:i_ * 512 + 512], self.ones_f1[:, :],
                    self.statn[:, h_ * TPC + i_ * 512:h_ * TPC + (i_ + 1) * 512],
                    start=True, stop=True), waits=w, inc=PE)
                evs[i] = P.op("vector", lambda e, c_=col, i_=i: e.tensor_copy(
                    out=self.AF[:, c_ + i_ * 512:c_ + i_ * 512 + 512],
                    in_=self.psum[:, i_ * 512:i_ * 512 + 512]), waits=[(PE, m)], inc=VE)

    def ffn_sublayer(self, xsrc, xdst, final):
        P = self.P
        self.h_pass(xsrc, router=True)
        P.barrier()
        self.ck()
        self.ffn_lowrank()
        P.barrier()
        self.tap("lr2", self.lr2_scr)
        self.tap("smask", self.smask[:, :])
        self.ffn_inner_ln()
        P.barrier()
        self.ck()
        self.ffn_qk("ffn_wkT", True)
        P.barrier()
        self.ck()
        self.ffn_v()
        P.barrier()
        self.tap("contrib", self.contribF)
        self.ck()
        cc = self.ccs.pop(0)
        P.barrier()
        P.op("gpsimd", lambda g: g.collective_compute(
            "AllGather", ALU.bypass, replica_groups=[list(range(NCORE))],
            ins=[self.contribF.opt()], outs=[self.gathF.opt()]), inc=cc, amt=1)
        self.cc_cur = cc
        self.ck()
        self.ffn_qk("ffn_wqT", False)
        P.barrier()
        self.tap("q", self.q_scr)
        self.ck()
        self.ffn_attn()
        P.barrier()
        self.tap("ctx", self.ctx_scr)
        self.ck()
        self.dense_to_x(self.wbf["ffn_woT"], xsrc, self.xt_scr, kind="wo", fuse_stats=True,
                        wname_="ffn_woT")
        P.barrier()
        self.tap("xt", self.xt_scr)
        self.finalize_stats(AF_MU2B, AF_RS2B)
        P.barrier()
        self.ck()
        self.ffn_ff_select(xsrc, xdst, final)

    def h_pass(self, xsrc, router):
        P = self.P
        mlast = None
        selast = None
        for kc in range(DC):
            p = kc % 2
            fo = AF_IO + p * 2048
            d = self.dma("sync", self.AF[:, fo:fo + TPC], xsrc[kc * 128:(kc + 1) * 128, :],
                         waits=self.need(("h1", p)), stream="h1x%d" % p)
            v = P.op("vector", lambda e, f_=fo: e.tensor_tensor(
                out=self.AF[:, f_:f_ + TPC], in0=self.AF[:, f_:f_ + TPC],
                in1=self.AF[:, AF_MUB:AF_MUB + TPC], op=ALU.subtract), waits=[d], inc=VE)
            v = P.op("vector", lambda e, f_=fo: e.tensor_tensor(
                out=self.AF[:, f_:f_ + TPC], in0=self.AF[:, f_:f_ + TPC],
                in1=self.AF[:, AF_RSB:AF_RSB + TPC], op=ALU.mult), inc=VE)
            selast = P.op("scalar", lambda e, f_=fo, kc_=kc: e.copy(
                out=self.AB[:, AB_H1 + kc_ * TPC:AB_H1 + (kc_ + 1) * TPC], in_=self.AF[:, f_:f_ + TPC]),
                waits=[(VE, v)], inc=SE)
            if router:
                for tg in range(4):
                    mlast = P.op("tensor", lambda t, f_=fo, kc_=kc, tg_=tg: t.matmul(
                        self.psum[0:3, tg_ * 512:tg_ * 512 + 512],
                        self.wrouter[:, kc_ * 3:(kc_ + 1) * 3],
                        self.AF[:, f_ + tg_ * 512:f_ + tg_ * 512 + 512],
                        start=(kc_ == 0), stop=(kc_ == DC - 1)), waits=[(VE, v)], inc=PE)
                self.done(("h1", p), PE, mlast)
                self.done(("h1", p), SE, selast)
            else:
                self.done(("h1", p), SE, selast)
        self.h1_se = selast
        if not router:
            return
        v = P.op("vector", lambda e: e.tensor_copy(out=self.AF[0:3, AF_IO:AF_IO + TPC],
                                                   in_=self.psum[0:3, 0:TPC]),
                 waits=[(PE, mlast), (SE, selast)], inc=VE)
        d1 = self.dma("gpsimd", self.AF[0:1, AF_IO + TPC:AF_IO + 2 * TPC],
                      self.AF[1:2, AF_IO:AF_IO + TPC], waits=[(VE, v)], stream="rrow")
        d2 = self.dma("gpsimd", self.AF[0:1, AF_MU2B:AF_MU2B + TPC],
                      self.AF[2:3, AF_IO:AF_IO + TPC], waits=[(VE, v)], stream="rrow")
        L0 = self.AF[0:1, AF_IO:AF_IO + TPC]
        L1 = self.AF[0:1, AF_IO + TPC:AF_IO + 2 * TPC]
        L2 = self.AF[0:1, AF_MU2B:AF_MU2B + TPC]
        TA = self.AF[0:1, AF_RS2B:AF_RS2B + TPC]
        TB = self.AF[0:1, AF_MISC:AF_MISC + TPC]
        v = P.op("vector", lambda e: e.tensor_tensor(out=TA, in0=L0, in1=L1, op=ALU.is_ge),
                 waits=[d2], inc=VE)
        v = P.op("vector", lambda e: e.tensor_tensor(out=TB, in0=L0, in1=L2, op=ALU.is_ge), inc=VE)
        v = P.op("vector", lambda e: e.tensor_tensor(out=self.smask[:, 0:TPC], in0=TA, in1=TB,
                                                     op=ALU.mult), inc=VE)
        v = P.op("vector", lambda e: e.tensor_tensor(out=TA, in0=L1, in1=L0, op=ALU.is_gt), inc=VE)
        v = P.op("vector", lambda e: e.tensor_tensor(out=TB, in0=L1, in1=L2, op=ALU.is_ge), inc=VE)
        v = P.op("vector", lambda e: e.tensor_tensor(out=self.smask[:, TPC:2 * TPC], in0=TA, in1=TB,
                                                     op=ALU.mult), inc=VE)
        v = P.op("vector", lambda e: e.tensor_scalar(out=self.smask[:, :], in0=self.smask[:, :],
                                                     scalar1=0.5, scalar2=None, op0=ALU.mult), inc=VE)

    def ffn_lowrank(self):
        P = self.P
        for kc in range(DC):
            self.dma("sync", self.AB[:, AB_WDN + kc * 32:AB_WDN + (kc + 1) * 32],
                     self.wbf["wdownT"][kc * 128:(kc + 1) * 128, :],
                     waits=self.need(("wrdy", "wdownT")), stream="lrw")
        du = self.dma("sync", self.AB[0:RLOW, AB_WUP:AB_WUP + TPC], self.wbf["wupT"][:, :],
                      waits=self.need(("wrdy", "wupT")), stream="lrw")
        m = None
        for tg in range(4):
            for kc in range(DC):
                m = P.op("tensor", lambda t, kc_=kc, tg_=tg: t.matmul(
                    self.psum[0:RLOW, tg_ * 512:tg_ * 512 + 512],
                    self.AB[:, AB_WDN + kc_ * 32:AB_WDN + (kc_ + 1) * 32],
                    self.AB[:, AB_H1 + kc_ * TPC + tg_ * 512:AB_H1 + kc_ * TPC + tg_ * 512 + 512],
                    start=(kc_ == 0), stop=(kc_ == DC - 1)),
                    waits=[du, (SE, self.h1_se)], inc=PE)
        v = P.op("vector", lambda e: e.tensor_copy(out=self.AB[0:RLOW, AB_LR2E:AB_LR2E + TPC],
                                                   in_=self.psum[0:RLOW, 0:TPC]),
                 waits=[(PE, m)], inc=VE)
        for oc in range(DC):
            for tg in range(4):
                i = oc * 4 + tg
                pcol = 2048 + (i % 2) * 512
                ecol = AB_EVAC + (i % 2) * 512
                waits = [(VE, v), du] + self.need(("lrp", i % 2))
                m = P.op("tensor", lambda t, oc_=oc, tg_=tg, pc=pcol: t.matmul(
                    self.psum[:, pc:pc + 512],
                    self.AB[0:RLOW, AB_WUP + oc_ * 128:AB_WUP + (oc_ + 1) * 128],
                    self.AB[0:RLOW, AB_LR2E + tg_ * 512:AB_LR2E + tg_ * 512 + 512],
                    start=True, stop=True), waits=waits, inc=PE)
                s = P.op("scalar", lambda e, pc=pcol, ec=ecol: e.copy(
                    out=self.AB[:, ec:ec + 512], in_=self.psum[:, pc:pc + 512]),
                    waits=[(PE, m)] + self.need(("lre", i % 2)), inc=SE)
                d = self.dma("gpsimd", self.lr2_scr[oc * 128:(oc + 1) * 128, tg * 512:(tg + 1) * 512],
                             self.AB[:, ecol:ecol + 512], waits=[(SE, s)], stream="lre%d" % (i % 2))
                self.done(("lrp", i % 2), SE, s)
                self.done(("lre", i % 2), *d)

    def ffn_inner_ln(self):
        P = self.P
        P.barrier()
        sq = AB_EVAC
        for tg in range(4):
            m2 = None
            sprev = None
            for kc in range(DC):
                p = kc % 2
                sprev = P.op("scalar", lambda e, kc_=kc, tg_=tg, p_=p: e.activation(
                    out=self.AB[:, sq + p_ * 512:sq + p_ * 512 + 512],
                    in_=self.AB[:, AB_H1 + kc_ * TPC + tg_ * 512:AB_H1 + kc_ * TPC + tg_ * 512 + 512],
                    func=AF_.Square), waits=self.need(("iln", p)), inc=SE)
                P.op("tensor", lambda t, kc_=kc, tg_=tg, first=(kc == 0), last=(kc == DC - 1): t.matmul(
                    self.psum[0:1, 2048:2560], self.ones_b[:, :],
                    self.AB[:, AB_H1 + kc_ * TPC + tg_ * 512:AB_H1 + kc_ * TPC + tg_ * 512 + 512],
                    start=first, stop=last),
                    waits=(self.need("st_ev") if kc == 0 else []), inc=PE)
                m2 = P.op("tensor", lambda t, p_=p, first=(kc == 0), last=(kc == DC - 1): t.matmul(
                    self.psum[0:1, 2560:3072], self.ones_b[:, :],
                    self.AB[:, sq + p_ * 512:sq + p_ * 512 + 512],
                    start=first, stop=last), waits=[(SE, sprev)], inc=PE)
                self.done(("iln", p), PE, m2)
            v = P.op("vector", lambda e, tg_=tg: e.tensor_copy(
                out=self.statn[:, tg_ * 512:tg_ * 512 + 512], in_=self.psum[0:1, 2048:2560]),
                waits=[(PE, m2)], inc=VE)
            v = P.op("vector", lambda e, tg_=tg: e.tensor_copy(
                out=self.statn[:, TPC + tg_ * 512:TPC + tg_ * 512 + 512], in_=self.psum[0:1, 2560:3072]),
                waits=[(PE, m2)], inc=VE)
            self.done(("iln", 0), VE, v)
            self.done(("iln", 1), VE, v)
            self.done("st_ev", VE, v)
        P.barrier()
        self.finalize_stats(AF_MU2B, AF_RS2B)
        P.barrier()
        for kc in range(DC):
            p = kc % 2
            fo = AF_IO + p * 2048
            v = P.op("vector", lambda e, kc_=kc, f_=fo: e.tensor_tensor(
                out=self.AF[:, f_:f_ + TPC], in0=self.AB[:, AB_H1 + kc_ * TPC:AB_H1 + (kc_ + 1) * TPC],
                in1=self.AF[:, AF_MU2B:AF_MU2B + TPC], op=ALU.subtract),
                waits=self.need(("ilna", p)), inc=VE)
            v = P.op("vector", lambda e, f_=fo: e.tensor_tensor(
                out=self.AF[:, f_:f_ + TPC], in0=self.AF[:, f_:f_ + TPC],
                in1=self.AF[:, AF_RS2B:AF_RS2B + TPC], op=ALU.mult), inc=VE)
            v = P.op("vector", lambda e, kc_=kc, f_=fo: e.tensor_scalar(
                out=self.AB[:, AB_H1 + kc_ * TPC:AB_H1 + (kc_ + 1) * TPC], in0=self.AF[:, f_:f_ + TPC],
                scalar1=self.vec_sb["ln1s_v"][:, kc_:kc_ + 1], scalar2=self.vec_sb["ln1b_v"][:, kc_:kc_ + 1],
                op0=ALU.mult, op1=ALU.add), inc=VE)
            self.done(("ilna", p), VE, v)
            self.h1_se = max(self.h1_se, 0)
        self.iln_ve = v

    def ffn_qk(self, wname, is_k):
        P = self.P
        for og in range(4):
            pb = AB_PANEL + (og % 2) * 8192
            dp = self.dma("sync", self.AB[:, pb:pb + 8192],
                          self.wbf[wname][og * 128:(og + 1) * 128, :],
                          waits=self.need(("pan", og % 2)) + self.need(("wrdy", wname)),
                          stream="pan%d" % (og % 2))
            for oi in range(4):
                h = og * 4 + oi
                m = None
                for kc in range(DC):
                    for tg in range(4):
                        waits = [dp, (SE, self.h1_se), (VE, self.iln_ve)]
                        if kc == 0 and tg == 0:
                            waits += self.need("qk_ps")
                        m = P.op("tensor", lambda t, kc_=kc, tg_=tg, oi_=oi, pb_=pb: t.matmul(
                            self.psum[:, tg_ * 512:tg_ * 512 + 512],
                            self.AB[:, pb_ + kc_ * 512 + oi_ * 128:pb_ + kc_ * 512 + oi_ * 128 + 128],
                            self.AB[:, AB_H1 + kc_ * TPC + tg_ * 512:AB_H1 + kc_ * TPC + tg_ * 512 + 512],
                            start=(kc_ == 0), stop=(kc_ == DC - 1)), waits=waits, inc=PE)
                s = P.op("scalar", lambda e: e.copy(out=self.AB[:, AB_EVAC:AB_EVAC + TPC],
                                                    in_=self.psum[:, 0:TPC]),
                         waits=[(PE, m)] + self.need("qk_ev"), inc=SE)
                s2 = P.op("scalar", lambda e: e.activation(out=self.AB[:, AB_EVAC + TPC:AB_EVAC + 2 * TPC],
                                                           in_=self.psum[:, 0:TPC], func=AF_.Square),
                          waits=[(PE, m)], inc=SE)
                self.done("qk_ps", SE, s2)
                nv = None
                for tg in range(4):
                    waits = [(SE, s2)] + ([(VE, nv)] if nv else []) + (self.need("n_ps") if tg == 0 else [])
                    mn = P.op("tensor", lambda t, tg_=tg: t.matmul(
                        self.psum[0:1, 2048:2560], self.ones_b[:, :],
                        self.AB[:, AB_EVAC + TPC + tg_ * 512:AB_EVAC + TPC + tg_ * 512 + 512],
                        start=True, stop=True), waits=waits, inc=PE)
                    if not is_k:
                        nv = P.op("vector", lambda e, tg_=tg: e.reduce_max(
                            out=self.tiny[0:1, 32 + tg_ % 2:33 + tg_ % 2], in_=self.psum[0:1, 2048:2560],
                            axis=AXX), waits=[(PE, mn)], inc=VE)
                        if tg % 2 == 1:
                            nv = P.op("vector", lambda e, b_=tg // 2, h_=h: e.tensor_tensor(
                                out=self.rq2[0:1, b_ * 16 + h_:b_ * 16 + h_ + 1],
                                in0=self.tiny[0:1, 32:33], in1=self.tiny[0:1, 33:34], op=ALU.max), inc=VE)
                    else:
                        nv = P.op("vector", lambda e, tg_=tg: e.tensor_copy(
                            out=self.AF[0:1, AF_MISC + tg_ * 512:AF_MISC + tg_ * 512 + 512],
                            in_=self.psum[0:1, 2048:2560]), waits=[(PE, mn)], inc=VE)
                self.done("n_ps", VE, nv)
                if is_k:
                    dn = self.dma("gpsimd", self.contribF[self.CF_KN + h:self.CF_KN + h + 1, :],
                                  self.AF[0:1, AF_MISC:AF_MISC + TPC], waits=[(VE, nv)], stream="qke")
                    d = self.dma("gpsimd", self.contribF[h * 128:(h + 1) * 128, :],
                                 self.AB[:, AB_EVAC:AB_EVAC + TPC], waits=[(SE, s)], stream="qke")
                else:
                    for bb in range(2):
                        d = self.dma("gpsimd", self.q_scr[(bb * H + h) * 128:(bb * H + h) * 128 + 128, :],
                                     self.AB[:, AB_EVAC + bb * NQB:AB_EVAC + (bb + 1) * NQB],
                                     waits=[(SE, s)], stream="qke")
                self.done("qk_ev", *d)
            self.done(("pan", og % 2), PE, P.counts[PE])

    def ffn_v(self):
        P = self.P
        for vg in range(4):
            pb = AB_PANEL + (vg % 2) * 8192
            dp = self.dma("sync", self.AB[:, pb:pb + 8192],
                          self.wbf["ffn_wvT"][vg * 128:(vg + 1) * 128, :],
                          waits=self.need(("pan", vg % 2)) + self.need(("wrdy", "ffn_wvT")),
                          stream="pan%d" % (vg % 2))
            for t128 in range(DC):
                par = t128 % 2
                pc = par * 512
                m = None
                for kc in range(DC):
                    waits = [dp, (SE, self.h1_se), (VE, self.iln_ve)]
                    if kc == 0:
                        waits += self.need(("v_ps", par))
                    m = P.op("tensor", lambda t, kc_=kc, t_=t128, pc_=pc, pb_=pb: t.matmul(
                        self.psum[:, pc_:pc_ + 512],
                        self.AB[:, AB_H1 + kc_ * TPC + t_ * 128:AB_H1 + kc_ * TPC + t_ * 128 + 128],
                        self.AB[:, pb_ + kc_ * 512:pb_ + (kc_ + 1) * 512],
                        start=(kc_ == 0), stop=(kc_ == DC - 1)), waits=waits, inc=PE)
                s = P.op("scalar", lambda e, par_=par, pc_=pc: e.copy(
                    out=self.AB[:, AB_EVAC + par_ * 512:AB_EVAC + par_ * 512 + 512],
                    in_=self.psum[:, pc_:pc_ + 512]),
                    waits=[(PE, m)] + self.need(("v_ev", par)), inc=SE)
                d = self.dma("gpsimd",
                             self.contribF[self.CF_V + t128 * 128:self.CF_V + (t128 + 1) * 128,
                                           vg * 512:(vg + 1) * 512],
                             self.AB[:, AB_EVAC + par * 512:AB_EVAC + par * 512 + 512],
                             waits=[(SE, s)], stream="ve%d" % par)
                self.done(("v_ps", par), SE, s)
                self.done(("v_ev", par), *d)
            self.done(("pan", vg % 2), PE, P.counts[PE])

    def ffn_attn(self):
        P = self.P
        cc = self.cc_cur
        lcol = 3072
        exp_at_slot = {}
        pv_at_probs = {}
        pres_ve = {}
        pres_pe = {}
        for it in range(B * H):
            bb, h = it // H, it % H
            buf = it % 2
            hp, hl = h // 2, h % 2
            kbase = AB_KSB + buf * 8192
            qbase = AB_Q + buf * 1024
            for rr in range(NCORE):
                self.dma("sync", self.AB[:, kbase + rr * 1024:kbase + (rr + 1) * 1024],
                         self.gathF[rr * self.CF_ROWS + h * 128:rr * self.CF_ROWS + (h + 1) * 128,
                                    bb * NQB:(bb + 1) * NQB],
                         waits=[(cc, 1)] + self.need(("ksb", buf)), stream="ksb%d" % buf)
            dq = self.dma("sync", self.AB[:, qbase:qbase + 1024],
                          self.q_scr[(bb * H + h) * 128:(bb * H + h) * 128 + 128, :],
                          waits=self.need(("ksb", buf)), stream="ksb%d" % buf)
            for rr in range(NCORE):
                sl = rr % 2
                dn = self.dma("sync", self.AB[0:1, AB_SB1 + sl * 1024:AB_SB1 + (sl + 1) * 1024],
                              self.gathF[rr * self.CF_ROWS + self.CF_KN + h:
                                         rr * self.CF_ROWS + self.CF_KN + h + 1,
                                         bb * NQB:(bb + 1) * NQB],
                              waits=self.need(("knt", sl)), stream="knt%d" % sl)
                if rr == 0:
                    v = P.op("vector", lambda e, sl_=sl: e.reduce_max(
                        out=self.tiny[0:1, 40:41],
                        in_=self.AB[0:1, AB_SB1 + sl_ * 1024:AB_SB1 + sl_ * 1024 + 1024],
                        axis=AXX), waits=[dn], inc=VE)
                else:
                    v = P.op("vector", lambda e, sl_=sl: e.reduce_max(
                        out=self.tiny[0:1, 43:44],
                        in_=self.AB[0:1, AB_SB1 + sl_ * 1024:AB_SB1 + sl_ * 1024 + 1024],
                        axis=AXX), waits=[dn], inc=VE)
                    v = P.op("vector", lambda e: e.tensor_tensor(
                        out=self.tiny[0:1, 40:41], in0=self.tiny[0:1, 40:41],
                        in1=self.tiny[0:1, 43:44], op=ALU.max), inc=VE)
                self.done(("knt", sl), VE, v)
            v = P.op("vector", lambda e, b_=bb, h_=h: e.tensor_tensor(
                out=self.tiny[0:1, 41:42], in0=self.tiny[0:1, 40:41],
                in1=self.rq2[0:1, b_ * 16 + h_:b_ * 16 + h_ + 1], op=ALU.add), inc=VE)
            v = P.op("vector", lambda e: e.tensor_scalar(
                out=self.tiny[0:1, 42:43], in0=self.tiny[0:1, 41:42],
                scalar1=float(-0.5 * SCALE), scalar2=None, op0=ALU.mult), inc=VE)
            mb = P.op("tensor", lambda t: t.matmul(
                self.psum[:, 1024:1025], self.ones_f1[:, :], self.tiny[0:1, 42:43],
                start=True, stop=True),
                waits=[(VE, v)] + self.need(("sc", 0)) + self.need(("mhc", buf)), inc=PE)
            mh_ve = P.op("vector", lambda e, c_=buf: e.tensor_copy(
                out=self.mh[:, c_:c_ + 1], in_=self.psum[:, 1024:1025]),
                waits=[(PE, mb)], inc=VE)
            probs_info = {}
            dv_at = {}
            acc_last = None
            l_first = True

            def emit_pv(jj):
                nonlocal acc_last, l_first
                jo2 = jj % 8
                vs = jj % 4
                se_c2, ve_c2, pcol2, qs0 = probs_info[jj]
                for si2, (qs2, n2) in enumerate(slabs_for(jj)):
                    waits2 = [(SE, se_c2), dv_at[jj]]
                    if ve_c2:
                        waits2.append((VE, ve_c2))
                    if jj == 0:
                        waits2 += self.need("acc")
                    if jo2 == 7 and qs2 == 128 * (jj // 8):
                        pieces = [(qs2, 128, True)]
                        if n2 > 128:
                            pieces.append((qs2 + 128, n2 - 128, jj == NKT - 1))
                    else:
                        pieces = [(qs2, n2, jj == NKT - 1)]
                    for (pq, pn, pstop) in pieces:
                        acc_last = P.op("tensor",
                                        lambda t, pc=pcol2, q0=pq, nn=pn, vs_=vs, jx=jj, hl_=hl, st=pstop:
                                        t.matmul(self.psum[:, q0:q0 + nn],
                                                 self.AB[:, AB_VR + vs_ * 256 + hl_ * 128:
                                                         AB_VR + vs_ * 256 + hl_ * 128 + 128],
                                                 self.AB[:, pc + q0:pc + q0 + nn],
                                                 start=(jx == 0), stop=st, skip_group_check=True),
                                        waits=waits2, inc=PE)
                pv_at_probs[pcol2] = acc_last
                self.done(("vr", vs), PE, acc_last)
                if jj % 2 == 1:
                    _, vem1, pcolm1, _ = probs_info[jj - 1]
                    qs0m = 128 * ((jj - 1) // 8)
                    nact = 1024 - qs0
                    waits2 = [(SE, se_c2), (PE, pres_pe.get(0, 0))]
                    if vem1:
                        waits2.append((VE, vem1))
                    if ve_c2:
                        waits2.append((VE, ve_c2))
                    ve_c3 = P.op("vector", lambda v2, a=pcolm1, b_=pcol2, q0=qs0, nn=nact:
                                 v2.tensor_tensor(out=self.AB[:, AB_PRES + q0:AB_PRES + q0 + nn],
                                                  in0=self.AB[:, a + q0:a + q0 + nn],
                                                  in1=self.AB[:, b_ + q0:b_ + q0 + nn], op=ALU.add),
                                 waits=waits2, inc=VE)
                    pres_ve[pcolm1] = ve_c3
                    pres_ve[pcol2] = ve_c3
                    last = None
                    for qc in range(qs0 // 128, 8):
                        lw = [(VE, ve_c3)]
                        if l_first:
                            lw += self.need("l_ps")
                        last = P.op("tensor", lambda t, qc_=qc, first=l_first, st=(jj == 8 * qc + 7):
                                    t.matmul(self.psum[:, lcol + qc_:lcol + qc_ + 1],
                                             self.AB[:, AB_PRES + qc_ * 128:AB_PRES + qc_ * 128 + 128],
                                             self.ones_b[:, :], start=first, stop=st,
                                             skip_group_check=True),
                                    waits=lw, inc=PE)
                        l_first = False
                    pres_pe[0] = last

            for j in range(NKT):
                jo = j % 8
                rr, mm_ = j % 8, j // 8
                vslot = j % 4
                dv_at[j] = self.dma("sync", self.AB[:, AB_VR + vslot * 256:AB_VR + (vslot + 1) * 256],
                                    self.gathF[rr * self.CF_ROWS + self.CF_V + bb * NQB + mm_ * 128:
                                               rr * self.CF_ROWS + self.CF_V + bb * NQB + (mm_ + 1) * 128,
                                               hp * 256:(hp + 1) * 256],
                                    waits=self.need(("vr", vslot)), stream="vr%d" % vslot)
                qs0 = 128 * (j // 8)
                nact = 1024 - qs0
                sbase = 1024 + (j % 2) * 1024
                pcol = AB_PROBS + (j % 3) * 1024
                mlast = None
                for si, (qs, n) in enumerate(slabs_for(j)):
                    waits = [dq, (VE, mh_ve), (SE, exp_at_slot.get(j % 2, 0))] +                         self.need(("sc", j % 2))
                    mlast = P.op("tensor", lambda t, sb=sbase, jj=j, q0=qs, nn=n, kb=kbase, qb=qbase:
                                 t.matmul(self.psum[:, sb + q0:sb + q0 + nn],
                                          self.AB[:, kb + (jj % 8) * 1024 + (jj // 8) * 128:
                                                  kb + (jj % 8) * 1024 + (jj // 8) * 128 + 128],
                                          self.AB[:, qb + q0:qb + q0 + nn],
                                          start=True, stop=True), waits=waits, inc=PE)
                waits = [(PE, mlast), (PE, pv_at_probs.get(pcol, 0)), (VE, pres_ve.get(pcol, 0))]
                se_c = P.op("scalar", lambda s2, sb=sbase, pc=pcol, c_=buf, q0=qs0, nn=nact:
                            s2.activation(out=self.AB[:, pc + q0:pc + q0 + nn],
                                          in_=self.psum[:, sb + q0:sb + q0 + nn],
                                          func=AF_.Exp, bias=self.mh[:, c_:c_ + 1], scale=SCALE),
                            waits=waits, inc=SE)
                exp_at_slot[j % 2] = se_c
                ve_c = 0
                if True:
                    ve_c = P.op("vector", lambda v2, pc=pcol, q0=qs0, jo_=jo: v2.tensor_tensor(
                        out=self.AB[:, pc + q0:pc + q0 + 128], in0=self.AB[:, pc + q0:pc + q0 + 128],
                        in1=self.masks[:, jo_ * 128:(jo_ + 1) * 128], op=ALU.mult),
                        waits=[(SE, se_c)], inc=VE)
                probs_info[j] = (se_c, ve_c, pcol, qs0)
                if j >= 1:
                    emit_pv(j - 1)
                if j == NKT - 1:
                    emit_pv(j)
            pe_done = P.counts[PE]
            last_exp = probs_info[NKT - 1][0]
            self.done(("mhc", buf), SE, last_exp)
            vrl = P.op("vector", lambda e: e.reciprocal(out=self.AF[:, AF_MISC + 1024:AF_MISC + 1032],
                                                        in_=self.psum[:, lcol:lcol + 8]),
                       waits=[(PE, pe_done)], inc=VE)
            self.done("l_ps", VE, vrl)
            tr = None
            for qc in range(8):
                tw = [(VE, vrl)]
                if qc == 0:
                    tw += [(SE, exp_at_slot.get(0, 0)), (SE, exp_at_slot.get(1, 0))]
                    tw += self.need("ep_ps")
                tr = P.op("tensor", lambda t, qc_=qc: t.transpose(
                    self.psum[0:1, 1024 + qc_ * 128:1024 + qc_ * 128 + 128],
                    self.AF[:, AF_MISC + 1024 + qc_:AF_MISC + 1024 + qc_ + 1],
                    self.idf[:, :]), waits=tw, inc=PE)
            vT = P.op("vector", lambda e: e.tensor_copy(out=self.AF[0:1, AF_MISC:AF_MISC + 1024],
                                                        in_=self.psum[0:1, 1024:2048]),
                      waits=[(PE, tr)], inc=VE)
            bc = None
            for i2 in range(2):
                bc = P.op("tensor", lambda t, i_=i2: t.matmul(
                    self.psum[:, 1024 + i_ * 512:1024 + i_ * 512 + 512], self.ones_f1[:, :],
                    self.AF[0:1, AF_MISC + i_ * 512:AF_MISC + (i_ + 1) * 512],
                    start=True, stop=True), waits=[(VE, vT)], inc=PE)
            vb = P.op("vector", lambda e: e.tensor_copy(out=self.AF[:, AF_MISC:AF_MISC + 1024],
                                                        in_=self.psum[:, 1024:2048]),
                      waits=[(PE, bc)], inc=VE)
            self.done("ep_ps", VE, vb)
            self.done(("sc", 0), VE, vb)
            self.done(("sc", 1), VE, vb)
            vy = P.op("vector", lambda e: e.tensor_tensor(
                out=self.AB[:, AB_YS:AB_YS + 1024], in0=self.psum[:, 0:1024],
                in1=self.AF[:, AF_MISC:AF_MISC + 1024], op=ALU.mult),
                waits=[(VE, vb)] + self.need("ys"), inc=VE)
            self.done("acc", VE, vy)
            dy = self.dma("gpsimd", self.ctx_scr[h * 128:(h + 1) * 128, bb * NQB:(bb + 1) * NQB],
                          self.AB[:, AB_YS:AB_YS + 1024], waits=[(VE, vy)], stream="ys")
            self.done("ys", *dy)
            self.done(("ksb", buf), PE, acc_last)

    def dense_to_x(self, w_scr, xadd_src, xdst, kind, fuse_stats, wname_=None):
        P = self.P
        dctx = None
        for kc in range(DC):
            dctx = self.dma("sync", self.AB[:, AB_CTX + kc * TPC:AB_CTX + (kc + 1) * TPC],
                            self.ctx_scr[kc * 128:(kc + 1) * 128, :], stream="ctx")
        stev = None
        pidx = 0
        for tg in range(4):
            for og in range(4):
                par = pidx % 2
                pidx += 1
                pb = AB_SLAB + par * 8192
                ds = self.dma("sync", self.AB[:, pb:pb + 8192],
                              w_scr[og * 128:(og + 1) * 128, :],
                              waits=self.need(("dslab", par)) + self.need(("wrdy", wname_)),
                              stream="dsl%d" % par)
                mlast = None
                for kc in range(DC):
                    for oi in range(4):
                        waits = [ds, dctx]
                        if kc == 0:
                            waits += self.need(("dps", oi))
                        mlast = P.op("tensor", lambda t, kc_=kc, tg_=tg, oi_=oi, pb_=pb: t.matmul(
                            self.psum[:, oi_ * 512:oi_ * 512 + 512],
                            self.AB[:, pb_ + kc_ * 512 + oi_ * 128:
                                    pb_ + kc_ * 512 + oi_ * 128 + 128],
                            self.AB[:, AB_CTX + kc_ * TPC + tg_ * 512:AB_CTX + kc_ * TPC + tg_ * 512 + 512],
                            start=(kc_ == 0), stop=(kc_ == DC - 1)), waits=waits, inc=PE)
                self.done(("dslab", par), PE, mlast)
                for oi in range(4):
                    oc = og * 4 + oi
                    slot = oc % 4
                    fo = AF_IO + slot * 512
                    fx = AF_IO + 2048 + slot * 512
                    dx = self.dma("sync", self.AF[:, fo:fo + 512],
                                  xadd_src[oc * 128:(oc + 1) * 128, tg * 512:(tg + 1) * 512],
                                  waits=self.need(("dio", slot)) + self.need(("dio2", slot)),
                                  stream="dio%d" % slot)
                    if kind == "wo":
                        v = P.op("vector", lambda e, f_=fo, tg_=tg: e.tensor_tensor(
                            out=self.AF[:, f_:f_ + 512], in0=self.AF[:, f_:f_ + 512],
                            in1=self.AF[:, AF_MUB + tg_ * 512:AF_MUB + (tg_ + 1) * 512], op=ALU.subtract),
                            waits=[dx, (PE, mlast)], inc=VE)
                        v = P.op("vector", lambda e, f_=fo, tg_=tg: e.tensor_tensor(
                            out=self.AF[:, f_:f_ + 512], in0=self.AF[:, f_:f_ + 512],
                            in1=self.AF[:, AF_RSB + tg_ * 512:AF_RSB + (tg_ + 1) * 512], op=ALU.mult), inc=VE)
                        v = P.op("vector", lambda e, f_=fo, fx_=fx, oi_=oi: e.tensor_tensor(
                            out=self.AF[:, fx_:fx_ + 512], in0=self.AF[:, f_:f_ + 512],
                            in1=self.psum[:, oi_ * 512:oi_ * 512 + 512], op=ALU.add), inc=VE)
                    else:
                        v = P.op("vector", lambda e, f_=fo, fx_=fx, oi_=oi: e.tensor_tensor(
                            out=self.AF[:, fx_:fx_ + 512], in0=self.AF[:, f_:f_ + 512],
                            in1=self.psum[:, oi_ * 512:oi_ * 512 + 512], op=ALU.add),
                            waits=[dx, (PE, mlast)], inc=VE)
                    self.done(("dps", oi), VE, v)
                    d = self.dma("gpsimd", xdst[oc * 128:(oc + 1) * 128, tg * 512:(tg + 1) * 512],
                                 self.AF[:, fx:fx + 512], waits=[(VE, v)], stream="dxo%d" % slot)
                    if fuse_stats:
                        sq = P.op("scalar", lambda e, fx_=fx, f_=fo: e.activation(
                            out=self.AF[:, f_:f_ + 512], in_=self.AF[:, fx_:fx_ + 512], func=AF_.Square),
                            waits=[(VE, v)], inc=SE)
                        P.op("tensor", lambda t, fx_=fx, first=(oc == 0), last=(oc == DC - 1): t.matmul(
                            self.psum[0:1, 2048:2560], self.ones_f128[:, :], self.AF[:, fx_:fx_ + 512],
                            start=first, stop=last), waits=[(VE, v)] + (self.need("sst") if oc == 0 else []),
                            inc=PE)
                        m2 = P.op("tensor", lambda t, f_=fo, first=(oc == 0), last=(oc == DC - 1): t.matmul(
                            self.psum[0:1, 2560:3072], self.ones_f128[:, :], self.AF[:, f_:f_ + 512],
                            start=first, stop=last), waits=[(SE, sq)], inc=PE)
                        self.done(("dio", slot), PE, m2)
                        self.done(("dio2", slot), *d)
                    else:
                        self.done(("dio", slot), *d)
            if fuse_stats:
                v = P.op("vector", lambda e, tg_=tg: e.tensor_copy(
                    out=self.statn[:, tg_ * 512:tg_ * 512 + 512], in_=self.psum[0:1, 2048:2560]),
                    waits=[(PE, P.counts[PE])], inc=VE)
                stev = P.op("vector", lambda e, tg_=tg: e.tensor_copy(
                    out=self.statn[:, TPC + tg_ * 512:TPC + tg_ * 512 + 512], in_=self.psum[0:1, 2560:3072]),
                    waits=[(PE, P.counts[PE])], inc=VE)
                self.done("sst", VE, stev)

    def ffn_ff_select(self, xsrc, xdst, final):
        P = self.P
        for tg in range(4):
            sb0 = P.op("tensor", lambda t, tg_=tg: t.matmul(
                self.psum[:, 3072:3584], self.ones_b1[:, :], self.smask[:, tg_ * 512:tg_ * 512 + 512],
                start=True, stop=True), waits=self.need("sbc_ps"), inc=PE)
            v0 = P.op("vector", lambda e: e.tensor_copy(out=self.AB[:, AB_SBC:AB_SBC + 512],
                                                        in_=self.psum[:, 3072:3584]),
                      waits=[(PE, sb0)] + self.need("sbc"), inc=VE)
            sb1 = P.op("tensor", lambda t, tg_=tg: t.matmul(
                self.psum[:, 3072:3584], self.ones_b1[:, :],
                self.smask[:, TPC + tg_ * 512:TPC + tg_ * 512 + 512],
                start=True, stop=True), waits=[(VE, v0)], inc=PE)
            v1 = P.op("vector", lambda e: e.tensor_copy(out=self.AB[:, AB_SBC + 512:AB_SBC + 1024],
                                                        in_=self.psum[:, 3072:3584]),
                      waits=[(PE, sb1)], inc=VE)
            self.done("sbc_ps", VE, v1)
            sbc_ve = v1
            xnlast = None
            for kc in range(DC):
                p = kc % 2
                fo = AF_IO + p * 512
                d = self.dma("sync", self.AF[:, fo:fo + 512],
                             self.xt_scr[kc * 128:(kc + 1) * 128, tg * 512:(tg + 1) * 512],
                             waits=self.need(("xn", p)), stream="xn%d" % p)
                v = P.op("vector", lambda e, f_=fo, tg_=tg: e.tensor_tensor(
                    out=self.AF[:, f_:f_ + 512], in0=self.AF[:, f_:f_ + 512],
                    in1=self.AF[:, AF_MU2B + tg_ * 512:AF_MU2B + (tg_ + 1) * 512], op=ALU.subtract),
                    waits=[d], inc=VE)
                v = P.op("vector", lambda e, f_=fo, tg_=tg: e.tensor_tensor(
                    out=self.AF[:, f_:f_ + 512], in0=self.AF[:, f_:f_ + 512],
                    in1=self.AF[:, AF_RS2B + tg_ * 512:AF_RS2B + (tg_ + 1) * 512], op=ALU.mult), inc=VE)
                xnlast = P.op("vector", lambda e, f_=fo, kc_=kc: e.tensor_scalar(
                    out=self.AB[:, AB_XN2 + kc_ * 512:AB_XN2 + (kc_ + 1) * 512], in0=self.AF[:, f_:f_ + 512],
                    scalar1=self.vec_sb["ln2s_v"][:, kc_:kc_ + 1], scalar2=self.vec_sb["ln2b_v"][:, kc_:kc_ + 1],
                    op0=ALU.mult, op1=ALU.add), inc=VE)
                self.done(("xn", p), VE, xnlast)
            gelast = None
            for ffq in range(16):
                mlast = None
                for half in range(2):
                    sl = half
                    ds = self.dma("sync", self.AB[:, AB_FSLAB + sl * 4096:AB_FSLAB + (sl + 1) * 4096],
                                  self.wbf["ff1T"][ffq * 128:(ffq + 1) * 128,
                                                   half * 4096:(half + 1) * 4096],
                                  waits=self.need(("fslab", sl)) + self.need(("wrdy", "ff1T")),
                                  stream="fsl%d" % sl)
                    for kc8 in range(8):
                        kc = half * 8 + kc8
                        for f4 in range(4):
                            waits = [ds, (VE, xnlast)]
                            if kc == 0:
                                waits += self.need(("fps", f4))
                            mlast = P.op("tensor", lambda t, kc_=kc, kc8_=kc8, f4_=f4, sl_=sl: t.matmul(
                                self.psum[:, f4_ * 512:f4_ * 512 + 512],
                                self.AB[:, AB_FSLAB + sl_ * 4096 + kc8_ * 512 + f4_ * 128:
                                        AB_FSLAB + sl_ * 4096 + kc8_ * 512 + f4_ * 128 + 128],
                                self.AB[:, AB_XN2 + kc_ * 512:AB_XN2 + (kc_ + 1) * 512],
                                start=(kc_ == 0), stop=(kc_ == DC - 1)), waits=waits, inc=PE)
                    self.done(("fslab", sl), PE, mlast)
                for f4 in range(4):
                    ffc = ffq * 4 + f4
                    gelast = P.op("scalar", lambda e, f4_=f4, ffc_=ffc: e.activation(
                        out=self.AB[:, AB_G + ffc_ * 512:AB_G + (ffc_ + 1) * 512],
                        in_=self.psum[:, f4_ * 512:f4_ * 512 + 512], func=AF_.Gelu_apprx_tanh),
                        waits=[(PE, mlast)], inc=SE)
                    self.done(("fps", f4), SE, gelast)
            for doq in range(4):
                mlast = None
                for run in range(8):
                    sl = run % 2
                    ds = self.dma("sync", self.AB[:, AB_FSLAB + sl * 4096:AB_FSLAB + (sl + 1) * 4096],
                                  self.wbf["ff2T"][doq * 128:(doq + 1) * 128,
                                                   run * 4096:(run + 1) * 4096],
                                  waits=self.need(("fslab", sl)) + self.need(("wrdy", "ff2T")),
                                  stream="fsl%d" % sl)
                    for f8 in range(8):
                        ffc = run * 8 + f8
                        for d4 in range(4):
                            waits = [ds, (SE, gelast)]
                            if ffc == 0:
                                waits += self.need(("fps", d4))
                            mlast = P.op("tensor", lambda t, ffc_=ffc, f8_=f8, d4_=d4, sl_=sl: t.matmul(
                                self.psum[:, d4_ * 512:d4_ * 512 + 512],
                                self.AB[:, AB_FSLAB + sl_ * 4096 + f8_ * 512 + d4_ * 128:
                                        AB_FSLAB + sl_ * 4096 + f8_ * 512 + d4_ * 128 + 128],
                                self.AB[:, AB_G + ffc_ * 512:AB_G + (ffc_ + 1) * 512],
                                start=(ffc_ == 0), stop=(ffc_ == FFD // 128 - 1)), waits=waits, inc=PE)
                    self.done(("fslab", sl), PE, mlast)
                for d4 in range(4):
                    oc = doq * 4 + d4
                    self.select_tile(oc, d4, tg, mlast, xsrc, xdst, final, sbc_ve)
            if not final:
                v = P.op("vector", lambda e, tg_=tg: e.tensor_copy(
                    out=self.statn[:, tg_ * 512:tg_ * 512 + 512], in_=self.psum[0:1, 2048:2560]),
                    waits=[(PE, P.counts[PE])], inc=VE)
                v = P.op("vector", lambda e, tg_=tg: e.tensor_copy(
                    out=self.statn[:, TPC + tg_ * 512:TPC + tg_ * 512 + 512],
                    in_=self.psum[0:1, 2560:3072]),
                    waits=[(PE, P.counts[PE])], inc=VE)
                self.done("sst", VE, v)

    def select_tile(self, oc, d4, tg, mlast, xsrc, xdst, final, sbc_ve):
        P = self.P
        slot = oc % 2
        fXT = AF_IO + 1024 + slot * 512
        fXO = AF_IO + 2048 + slot * 512
        fO2 = AF_IO + 3072 + (slot % 1) * 512
        fE1 = AF_MISC + 1024
        fA = AF_MISC + 1536
        lr = AB_LR2T + slot * 512
        dxt = self.dma("sync", self.AF[:, fXT:fXT + 512],
                       self.xt_scr[oc * 128:(oc + 1) * 128, tg * 512:(tg + 1) * 512],
                       waits=self.need(("se_io", slot)) + self.need(("se_io2", slot)),
                       stream="sio%d" % slot)
        dxo = self.dma("sync", self.AF[:, fXO:fXO + 512],
                       xsrc[oc * 128:(oc + 1) * 128, tg * 512:(tg + 1) * 512], stream="sio%d" % slot)
        dlr = self.dma("sync", self.AB[:, lr:lr + 512],
                       self.lr2_scr[oc * 128:(oc + 1) * 128, tg * 512:(tg + 1) * 512],
                       stream="sio%d" % slot)
        v = P.op("vector", lambda e: e.tensor_tensor(
            out=self.AF[:, fO2:fO2 + 512], in0=self.AF[:, fXT:fXT + 512],
            in1=self.psum[:, d4 * 512:d4 * 512 + 512], op=ALU.add),
            waits=[dlr, (PE, mlast), (VE, sbc_ve)] + self.need("sel_o2"), inc=VE)
        v = P.op("vector", lambda e, tg_=tg: e.tensor_tensor(
            out=self.AF[:, fXO:fXO + 512], in0=self.AF[:, fXO:fXO + 512],
            in1=self.AF[:, AF_MUB + tg_ * 512:AF_MUB + (tg_ + 1) * 512], op=ALU.subtract), inc=VE)
        v = P.op("vector", lambda e, tg_=tg: e.tensor_tensor(
            out=self.AF[:, fXO:fXO + 512], in0=self.AF[:, fXO:fXO + 512],
            in1=self.AF[:, AF_RSB + tg_ * 512:AF_RSB + (tg_ + 1) * 512], op=ALU.mult), inc=VE)
        v = P.op("vector", lambda e: e.tensor_tensor(
            out=self.AF[:, fE1:fE1 + 512], in0=self.AF[:, fXO:fXO + 512],
            in1=self.AF[:, fO2:fO2 + 512], op=ALU.subtract),
            waits=self.need("st_sq2"), inc=VE)
        v = P.op("vector", lambda e, oc_=oc: e.tensor_scalar(
            out=self.AF[:, fXO:fXO + 512], in0=self.AF[:, fE1:fE1 + 512],
            scalar1=self.vec_sb["bias_v"][:, oc_:oc_ + 1], scalar2=None, op0=ALU.add), inc=VE)
        v = P.op("vector", lambda e: e.tensor_tensor(
            out=self.AF[:, fXO:fXO + 512], in0=self.AF[:, fXO:fXO + 512],
            in1=self.AB[:, AB_SBC:AB_SBC + 512], op=ALU.mult), inc=VE)
        v = P.op("vector", lambda e, lr_=lr: e.tensor_tensor(
            out=self.AF[:, fE1:fE1 + 512], in0=self.AF[:, fE1:fE1 + 512],
            in1=self.AB[:, lr_:lr_ + 512], op=ALU.add), inc=VE)
        v = P.op("vector", lambda e: e.tensor_tensor(
            out=self.AF[:, fE1:fE1 + 512], in0=self.AF[:, fE1:fE1 + 512],
            in1=self.AB[:, AB_SBC + 512:AB_SBC + 1024], op=ALU.mult), inc=VE)
        v = P.op("vector", lambda e: e.tensor_tensor(
            out=self.AF[:, fA:fA + 512], in0=self.AF[:, fXO:fXO + 512],
            in1=self.AF[:, fE1:fE1 + 512], op=ALU.add), inc=VE)
        dxo2 = self.dma("sync", self.AF[:, fXO:fXO + 512],
                        xsrc[oc * 128:(oc + 1) * 128, tg * 512:(tg + 1) * 512], waits=[(VE, v)],
                        stream="sio%d" % slot)
        v = P.op("vector", lambda e: e.tensor_tensor(
            out=self.AF[:, fA:fA + 512], in0=self.AF[:, fA:fA + 512],
            in1=self.AF[:, fXO:fXO + 512], op=ALU.add), waits=[dxo2], inc=VE)
        v = P.op("vector", lambda e: e.scalar_tensor_tensor(
            out=self.AF[:, fXT:fXT + 512], in0=self.AF[:, fO2:fO2 + 512], scalar=0.5,
            in1=self.AF[:, fA:fA + 512], op0=ALU.mult, op1=ALU.add), inc=VE)
        self.done("sel_o2", VE, v)
        self.done(("fps", d4), VE, v)
        d = self.dma("gpsimd", xdst[oc * 128:(oc + 1) * 128, tg * 512:(tg + 1) * 512],
                     self.AF[:, fXT:fXT + 512], waits=[(VE, v)], stream="sxn%d" % slot)
        self.done(("se_io", slot), *d)
        if not final:
            sq = P.op("scalar", lambda e: e.activation(
                out=self.AF[:, fE1:fE1 + 512], in_=self.AF[:, fXT:fXT + 512], func=AF_.Square),
                waits=[(VE, v)] + self.need("st_sq"), inc=SE)
            P.op("tensor", lambda t, first=(oc == 0), last=(oc == DC - 1): t.matmul(
                self.psum[0:1, 2048:2560], self.ones_f128[:, :], self.AF[:, fXT:fXT + 512],
                start=first, stop=last), waits=[(VE, v)] + (self.need("sst") if oc == 0 else []), inc=PE)
            m2 = P.op("tensor", lambda t, first=(oc == 0), last=(oc == DC - 1): t.matmul(
                self.psum[0:1, 2560:3072], self.ones_f128[:, :], self.AF[:, fE1:fE1 + 512],
                start=first, stop=last), waits=[(SE, sq)], inc=PE)
            self.done("st_sq", PE, m2)
            self.done("st_sq2", PE, m2)
            self.done(("se_io2", slot), PE, m2)

    def mla_sublayer(self, xsrc, xdst):
        P = self.P
        self.h_pass(xsrc, router=False)
        P.barrier()
        self.ck()
        self.mla_latents()
        P.barrier()
        self.ck()
        self.mla_q()
        P.barrier()
        self.ck()
        self.tap("qm", self.qm_scr)
        self.tap("contribm", self.contribM)
        cc = self.ccs.pop(0)
        P.barrier()
        P.op("gpsimd", lambda g: g.collective_compute(
            "AllGather", ALU.bypass, replica_groups=[list(range(NCORE))],
            ins=[self.contribM.opt()], outs=[self.gathM.opt()]), inc=cc, amt=1)
        self.cc_cur = cc
        P.barrier()
        self.ck()
        self.mla_attn()
        P.barrier()
        self.tap("ctxm", self.ctx_scr)
        self.ck()
        self.dense_to_x(self.wbf["attn_proj"], xsrc, xdst, kind="proj", fuse_stats=True,
                        wname_="attn_proj")

    def mla_latents(self):
        P = self.P
        sub = int(os.environ.get("KSUB", "9"))
        dp = None
        for kc in range(DC):
            dp = self.dma("sync", self.AB[:, AB_KVPAN + kc * 128:AB_KVPAN + (kc + 1) * 128],
                          self.wbf["wkvlat"][kc * 128:(kc + 1) * 128, :],
                          waits=self.need(("wrdy", "wkvlat")), stream="kvp")
        for t128 in range(DC):
            if int(os.environ.get("KSUB", "9")) < 1:
                continue
            par = t128 % 2
            pc = par * 512
            ring = AB_RING + par * 512
            m = None
            for kc in range(DC):
                waits = [dp, (SE, self.h1_se)]
                if kc == 0:
                    waits += self.need(("lat_ps", par))
                m = P.op("tensor", lambda t, kc_=kc, t_=t128, pc_=pc: t.matmul(
                    self.psum[:, pc_:pc_ + 128],
                    self.AB[:, AB_H1 + kc_ * TPC + t_ * 128:AB_H1 + kc_ * TPC + t_ * 128 + 128],
                    self.AB[:, AB_KVPAN + kc_ * 128:AB_KVPAN + (kc_ + 1) * 128],
                    start=(kc_ == 0), stop=(kc_ == DC - 1)), waits=waits, inc=PE)
            s = P.op("scalar", lambda e, pc_=pc, r_=ring: e.copy(
                out=self.AB[:, r_:r_ + 128], in_=self.psum[:, pc_:pc_ + 128]),
                waits=[(PE, m)] + self.need(("lat_ev", par)), inc=SE)
            self.done(("lat_ps", par), SE, s)
            if sub < 2:
                self.done(("lat_cs", par), SE, s)
                self.done(("lat_ev", par), SE, s)
                self.done(("lat_ps", par), SE, s)
                continue
            fc = AF_MISC + par * 1024
            dc_ = self.dma("sync", self.AF[:, fc:fc + 32],
                           self.rk_cos_in[t128 * 128:(t128 + 1) * 128, :],
                           waits=self.need(("lat_cs", par)), stream="lcs%d" % par)
            ds_ = self.dma("sync", self.AF[:, fc + 32:fc + 64],
                           self.rk_sin_in[t128 * 128:(t128 + 1) * 128, :], stream="lcs%d" % par)
            ta, tb, tc, td = fc + 64, fc + 96, fc + 128, fc + 160
            v = P.op("vector", lambda e, r_=ring, a=ta, fc_=fc: e.tensor_tensor(
                out=self.AF[:, a:a + 32], in0=self.AB[:, r_:r_ + 64:2], in1=self.AF[:, fc_:fc_ + 32],
                op=ALU.mult), waits=[(SE, s), ds_], inc=VE)
            v = P.op("vector", lambda e, r_=ring, b_=tb, fc_=fc: e.tensor_tensor(
                out=self.AF[:, b_:b_ + 32], in0=self.AB[:, r_ + 1:r_ + 64:2],
                in1=self.AF[:, fc_ + 32:fc_ + 64], op=ALU.mult), inc=VE)
            v = P.op("vector", lambda e, r_=ring, c_=tc, fc_=fc: e.tensor_tensor(
                out=self.AF[:, c_:c_ + 32], in0=self.AB[:, r_:r_ + 64:2],
                in1=self.AF[:, fc_ + 32:fc_ + 64], op=ALU.mult), inc=VE)
            v = P.op("vector", lambda e, r_=ring, d_=td, fc_=fc: e.tensor_tensor(
                out=self.AF[:, d_:d_ + 32], in0=self.AB[:, r_ + 1:r_ + 64:2],
                in1=self.AF[:, fc_:fc_ + 32], op=ALU.mult), inc=VE)
            v = P.op("vector", lambda e, r_=ring, a=ta, b_=tb: e.tensor_tensor(
                out=self.AB[:, r_:r_ + 64:2], in0=self.AF[:, a:a + 32], in1=self.AF[:, b_:b_ + 32],
                op=ALU.subtract), inc=VE)
            v = P.op("vector", lambda e, r_=ring, c_=tc, d_=td: e.tensor_tensor(
                out=self.AB[:, r_ + 1:r_ + 64:2], in0=self.AF[:, c_:c_ + 32], in1=self.AF[:, d_:d_ + 32],
                op=ALU.add), inc=VE)
            self.done(("lat_cs", par), VE, v)
            if sub < 3:
                self.done(("lat_ev", par), VE, v)
                self.done(("lat_ps", par), VE, v)
                continue
            trp = P.op("tensor", lambda t, r_=ring, par_=par: t.transpose(
                self.psb[:, par_ * 128:par_ * 128 + 128], self.AB[:, r_:r_ + 128], self.idb[:, :]),
                waits=[(VE, v)] + self.need(("psb", par)) + self.need("psb_bank"), inc=PE)
            v2 = P.op("vector", lambda e, par_=par: e.tensor_copy(
                out=self.AB[:, AB_QME + par_ * 512:AB_QME + par_ * 512 + 128],
                in_=self.psb[:, par_ * 128:par_ * 128 + 128]),
                waits=[(PE, trp)] + self.need(("lat_e2", par)), inc=VE)
            self.done(("psb", par), VE, v2)
            self.done("psb_bank", VE, v2)
            self.done(("lat_ev", par), VE, v2)
            d = self.dma("gpsimd", self.contribM[:, t128 * 128:(t128 + 1) * 128],
                         self.AB[:, AB_QME + par * 512:AB_QME + par * 512 + 128], waits=[(VE, v2)],
                         stream="lco%d" % par)
            self.done(("lat_e2", par), *d)

    def mla_q(self):
        P = self.P
        dke = None
        for hh in range(H):
            dke = self.dma("sync", self.AB[:, AB_KEXP + hh * 64:AB_KEXP + (hh + 1) * 64],
                           self.wbf["kexpT"][hh * 128:(hh + 1) * 128, :],
                           waits=self.need(("wrdy", "kexpT")), stream="kexp")
        for vg in range(4):
            pb = AB_PANEL
            dp = self.dma("sync", self.AB[:, AB_PANEL:AB_PANEL + 8192],
                          self.wbf["attn_wq"][vg * 128:(vg + 1) * 128, :],
                          waits=self.need(("pan", 0)) + self.need(("pan", 1)), stream="pan")
            for t128 in range(DC):
                par = t128 % 2
                pc = par * 512
                ring = AB_RING + par * 1024
                m = None
                for kc in range(DC):
                    waits = [dp, (SE, self.h1_se)]
                    if kc == 0:
                        waits += self.need(("q_ps", par))
                    m = P.op("tensor", lambda t, kc_=kc, t_=t128, pc_=pc, pb_=pb: t.matmul(
                        self.psum[:, pc_:pc_ + 512],
                        self.AB[:, AB_H1 + kc_ * TPC + t_ * 128:AB_H1 + kc_ * TPC + t_ * 128 + 128],
                        self.AB[:, pb_ + kc_ * 512:pb_ + (kc_ + 1) * 512],
                        start=(kc_ == 0), stop=(kc_ == DC - 1)), waits=waits, inc=PE)
                s = P.op("scalar", lambda e, pc_=pc, r_=ring: e.copy(
                    out=self.AB[:, r_:r_ + 512], in_=self.psum[:, pc_:pc_ + 512]),
                    waits=[(PE, m)] + self.need(("q_ev", par)), inc=SE)
                self.done(("q_ps", par), SE, s)
                if int(os.environ.get("KSUB2", "9")) < 2:
                    self.done(("q_ev", par), SE, s)
                    self.done(("q_cs", par), SE, s)
                    self.done(("q_t", par), SE, s)
                    continue
                fc = AF_MISC + par * 1024
                dc_ = self.dma("sync", self.AF[:, fc:fc + 256],
                               self.rq_cos_in[t128 * 128:(t128 + 1) * 128, :],
                               waits=self.need(("q_cs", par)), stream="qcs%d" % par)
                ds_ = self.dma("sync", self.AF[:, fc + 256:fc + 512],
                               self.rq_sin_in[t128 * 128:(t128 + 1) * 128, :], stream="qcs%d" % par)
                ta, tb = AF_IO + 2048 + par * 1024, AF_IO + 2048 + par * 1024 + 256
                tc, td = AF_IO + 2048 + par * 1024 + 512, AF_IO + 2048 + par * 1024 + 768
                v = P.op("vector", lambda e, r_=ring, a=ta, fc_=fc: e.tensor_tensor(
                    out=self.AF[:, a:a + 256], in0=self.AB[:, r_:r_ + 512:2], in1=self.AF[:, fc_:fc_ + 256],
                    op=ALU.mult), waits=[(SE, s), ds_] + self.need(("q_t", par)), inc=VE)
                v = P.op("vector", lambda e, r_=ring, b_=tb, fc_=fc: e.tensor_tensor(
                    out=self.AF[:, b_:b_ + 256], in0=self.AB[:, r_ + 1:r_ + 512:2],
                    in1=self.AF[:, fc_ + 256:fc_ + 512], op=ALU.mult), inc=VE)
                v = P.op("vector", lambda e, r_=ring, c_=tc, fc_=fc: e.tensor_tensor(
                    out=self.AF[:, c_:c_ + 256], in0=self.AB[:, r_:r_ + 512:2],
                    in1=self.AF[:, fc_ + 256:fc_ + 512], op=ALU.mult), inc=VE)
                v = P.op("vector", lambda e, r_=ring, d_=td, fc_=fc: e.tensor_tensor(
                    out=self.AF[:, d_:d_ + 256], in0=self.AB[:, r_ + 1:r_ + 512:2],
                    in1=self.AF[:, fc_:fc_ + 256], op=ALU.mult), inc=VE)
                v = P.op("vector", lambda e, r_=ring, a=ta, b_=tb: e.tensor_tensor(
                    out=self.AB[:, r_:r_ + 512:2], in0=self.AF[:, a:a + 256], in1=self.AF[:, b_:b_ + 256],
                    op=ALU.subtract), inc=VE)
                v = P.op("vector", lambda e, r_=ring, c_=tc, d_=td: e.tensor_tensor(
                    out=self.AB[:, r_ + 1:r_ + 512:2], in0=self.AF[:, c_:c_ + 256], in1=self.AF[:, d_:d_ + 256],
                    op=ALU.add), inc=VE)
                self.done(("q_cs", par), VE, v)
                self.done(("q_t", par), VE, v)
                if int(os.environ.get("KSUB2", "9")) < 3:
                    self.done(("q_ev", par), VE, v)
                    continue
                for hl in range(4):
                    slot = (t128 * 4 + hl) % 8
                    trp = P.op("tensor", lambda t, r_=ring, hl_=hl, sl=slot: t.transpose(
                        self.psb[:, sl * 128:(sl + 1) * 128], self.AB[:, r_ + hl_ * 128:r_ + hl_ * 128 + 128],
                        self.idb[:, :]),
                        waits=[(VE, v)] + self.need(("psb", slot)) + self.need("psb_bank"), inc=PE)
                    v2 = P.op("vector", lambda e, hl_=hl, t_=t128, sl=slot: e.tensor_copy(
                        out=self.AB[:, AB_RQT + hl_ * 512 + (t_ % 4) * 128:
                                    AB_RQT + hl_ * 512 + (t_ % 4) * 128 + 128],
                        in_=self.psb[:, sl * 128:(sl + 1) * 128]),
                        waits=[(PE, trp)] + self.need(("rqt", hl, t128 % 4)), inc=VE)
                    self.done(("psb", slot), VE, v2)
                    self.done("psb_bank", VE, v2)
                    self.done(("q_t", par), VE, v2)
                    self.done(("rqt_v", hl, t128 % 4), VE, v2)
                if t128 % 4 == 3 and int(os.environ.get("KSUB2", "9")) >= 4:
                    tq = t128 // 4
                    bb = tq // 2
                    for hl in range(4):
                        h = vg * 4 + hl
                        wv = self.need(("rqt_v", hl, 0)) + self.need(("rqt_v", hl, 1)) + \
                            self.need(("rqt_v", hl, 2)) + self.need(("rqt_v", hl, 3)) + [dke]
                        mq = P.op("tensor", lambda t, h_=h, hl_=hl: t.matmul(
                            self.psum[0:DL, 1024:1536],
                            self.AB[:, AB_KEXP + h_ * 64:AB_KEXP + (h_ + 1) * 64],
                            self.AB[:, AB_RQT + hl_ * 512:AB_RQT + (hl_ + 1) * 512],
                            start=True, stop=True), waits=wv + self.need("qm_ps") + self.need("qn_rd"), inc=PE)
                        s2 = P.op("scalar", lambda e: e.copy(
                            out=self.AB[0:DL, AB_QME:AB_QME + 512], in_=self.psum[0:DL, 1024:1536]),
                            waits=[(PE, mq)] + self.need("qme"), inc=SE)
                        sq = P.op("scalar", lambda e: e.activation(
                            out=self.AB[0:DL, AB_SQ2:AB_SQ2 + 512], in_=self.psum[0:DL, 1024:1536],
                            func=AF_.Square), waits=[(PE, mq)], inc=SE)
                        self.done("qm_ps", SE, sq)
                        mn = P.op("tensor", lambda t: t.matmul(
                            self.psum[0:1, 1536:2048], self.ones_b[0:DL, :], self.AB[0:DL, AB_SQ2:AB_SQ2 + 512],
                            start=True, stop=True), waits=[(SE, sq)] + self.need("qn_ps"), inc=PE)
                        if tq % 2 == 0:
                            nv = P.op("vector", lambda e, b_=bb, h_=h: e.reduce_max(
                                out=self.rq2m[0:1, b_ * 16 + h_:b_ * 16 + h_ + 1],
                                in_=self.psum[0:1, 1536:2048], axis=AXX), waits=[(PE, mn)], inc=VE)
                            self.done("qn_rd", VE, nv)
                        else:
                            nv = P.op("vector", lambda e: e.reduce_max(
                                out=self.tiny[0:1, 48:49], in_=self.psum[0:1, 1536:2048], axis=AXX),
                                waits=[(PE, mn)], inc=VE)
                            nv = P.op("vector", lambda e, b_=bb, h_=h: e.tensor_tensor(
                                out=self.rq2m[0:1, b_ * 16 + h_:b_ * 16 + h_ + 1],
                                in0=self.rq2m[0:1, b_ * 16 + h_:b_ * 16 + h_ + 1],
                                in1=self.tiny[0:1, 48:49], op=ALU.max), inc=VE)
                            self.done("qn_rd", VE, nv)
                        self.done("qn_ps", VE, nv)
                        d = self.dma("gpsimd",
                                     self.qm_scr[(bb * H + h) * DL:(bb * H + h) * DL + DL,
                                                 (tq % 2) * 512:(tq % 2) * 512 + 512],
                                     self.AB[0:DL, AB_QME:AB_QME + 512], waits=[(SE, s2)],
                                     stream="qme")
                        self.done("qme", *d)
                    for hl in range(4):
                        for qq in range(4):
                            self.done(("rqt", hl, qq), PE, P.counts[PE])
            self.done(("pan", 0), PE, P.counts[PE])
        self.done(("q_ps", 0), SE, self.P.counts[SE])
        self.done(("q_ps", 1), SE, self.P.counts[SE])

    def mla_attn(self):
        P = self.P
        cc = self.cc_cur
        exp_at_slot = {}
        pv_at_probs = {}
        for bb in range(B):
            dk = None
            for rr in range(NCORE):
                self.dma("sync", self.AB[0:DL, AB_KLAT + rr * 1024:AB_KLAT + (rr + 1) * 1024],
                         self.gathM[rr * 128:rr * 128 + DL, bb * NQB:(bb + 1) * NQB],
                         waits=[(cc, 1)] + self.need("mlakv"), stream="mkv")
                dk = self.dma("sync", self.AB[0:DL, AB_VLT + rr * 1024:AB_VLT + (rr + 1) * 1024],
                              self.gathM[rr * 128 + DL:(rr + 1) * 128, bb * NQB:(bb + 1) * NQB],
                              stream="mkv")
            sqk = P.op("scalar", lambda e: e.activation(out=self.AB[0:DL, AB_SQK:AB_SQK + 8192],
                                                        in_=self.AB[0:DL, AB_KLAT:AB_KLAT + 8192],
                                                        func=AF_.Square), waits=[dk], inc=SE)
            nv = None
            for i in range(16):
                mn = P.op("tensor", lambda t, i_=i: t.matmul(
                    self.psum[0:1, 1536:2048], self.ones_b[0:DL, :],
                    self.AB[0:DL, AB_SQK + i_ * 512:AB_SQK + (i_ + 1) * 512],
                    start=True, stop=True),
                    waits=[(SE, sqk)] + ([(VE, nv)] if nv else []) + self.need("mh_m2"), inc=PE)
                if i == 0:
                    nv = P.op("vector", lambda e, b_=bb: e.reduce_max(
                        out=self.rk2m[0:1, b_:b_ + 1], in_=self.psum[0:1, 1536:2048], axis=AXX),
                        waits=[(PE, mn)], inc=VE)
                else:
                    nv = P.op("vector", lambda e: e.reduce_max(
                        out=self.tiny[0:1, 50:51], in_=self.psum[0:1, 1536:2048], axis=AXX),
                        waits=[(PE, mn)], inc=VE)
                    nv = P.op("vector", lambda e, b_=bb: e.tensor_tensor(
                        out=self.rk2m[0:1, b_:b_ + 1], in0=self.rk2m[0:1, b_:b_ + 1],
                        in1=self.tiny[0:1, 50:51], op=ALU.max), inc=VE)
            vm = P.op("vector", lambda e: e.memset(self.AB[:, AB_VT:AB_VT + NKT * 65], 1.0),
                      waits=self.need("vt_ws"), inc=VE)
            vtv = None
            for j in range(NKT):
                slot = j % 8
                trp = P.op("tensor", lambda t, j_=j, sl=slot: t.transpose(
                    self.psb[:, sl * 128:sl * 128 + 64],
                    self.AB[0:DL, AB_VLT + (j_ % 8) * 1024 + (j_ // 8) * 128:
                            AB_VLT + (j_ % 8) * 1024 + (j_ // 8) * 128 + 128],
                    self.idb[0:DL, 0:DL]),
                    waits=[dk, (VE, vm)] + self.need(("psb", slot)) + self.need("psb_bank"), inc=PE)
                vtv = P.op("vector", lambda e, j_=j, sl=slot: e.tensor_copy(
                    out=self.AB[:, AB_VT + j_ * 65:AB_VT + j_ * 65 + 64],
                    in_=self.psb[:, sl * 128:sl * 128 + 64]), waits=[(PE, trp)], inc=VE)
                self.done(("psb", slot), VE, vtv)
                self.done("psb_bank", VE, vtv)
            for h in range(H):
                buf = h % 2
                dq = self.dma("sync", self.AB[0:DL, AB_QM + buf * 1024:AB_QM + (buf + 1) * 1024],
                              self.qm_scr[(bb * H + h) * DL:(bb * H + h) * DL + DL, :],
                              waits=self.need(("qm_b", buf)), stream="qmb%d" % buf)
                v = P.op("vector", lambda e, b_=bb, h_=h: e.tensor_tensor(
                    out=self.tiny[0:1, 52:53], in0=self.rq2m[0:1, b_ * 16 + h_:b_ * 16 + h_ + 1],
                    in1=self.rk2m[0:1, b_:b_ + 1], op=ALU.add), waits=[(VE, nv)], inc=VE)
                v = P.op("vector", lambda e: e.tensor_scalar(
                    out=self.tiny[0:1, 53:54], in0=self.tiny[0:1, 52:53],
                    scalar1=float(-0.5 * SCALE), scalar2=None, op0=ALU.mult), inc=VE)
                mb = P.op("tensor", lambda t: t.matmul(
                    self.psum[:, 1536:1537], self.ones_f1[:, :], self.tiny[0:1, 53:54],
                    start=True, stop=True), waits=[(VE, v)] + self.need("mh_m") + self.need("mh_m2"),
                    inc=PE)
                mhv = P.op("vector", lambda e, c_=buf: e.tensor_copy(
                    out=self.mh[:, c_:c_ + 1], in_=self.psum[:, 1536:1537]), waits=[(PE, mb)], inc=VE)
                self.done("mh_m", VE, mhv)
                probs_info = {}
                acc_last = None

                def emit_pv_m(jj):
                    nonlocal acc_last
                    jo2 = jj % 8
                    se_c2, ve_c2, pcol2, qs0m = probs_info[jj]
                    for si2, (qs2, n2) in enumerate(slabs_for(jj)):
                        waits2 = [(SE, se_c2), (VE, vtv)]
                        if ve_c2:
                            waits2.append((VE, ve_c2))
                        if jj == 0:
                            waits2 += self.need("macc")
                        if jo2 == 7 and qs2 == 128 * (jj // 8):
                            pieces = [(qs2, 128, True)]
                            if n2 > 128:
                                pieces.append((qs2 + 128, n2 - 128, jj == NKT - 1))
                        else:
                            pieces = [(qs2, n2, jj == NKT - 1)]
                        for (pq, pn, pstop) in pieces:
                            acc_last = P.op("tensor", lambda t, jx=jj, pc=pcol2, q0=pq, nn=pn, st=pstop:
                                            t.matmul(self.psum[0:65, q0:q0 + nn],
                                                     self.AB[:, AB_VT + jx * 65:AB_VT + (jx + 1) * 65],
                                                     self.AB[:, pc + q0:pc + q0 + nn],
                                                     start=(jx == 0), stop=st, skip_group_check=True),
                                            waits=waits2, inc=PE)
                    pv_at_probs[pcol2] = acc_last

                for j in range(NKT):
                    jo = j % 8
                    qs0 = 128 * (j // 8)
                    nact = 1024 - qs0
                    sbase = 1024 + (j % 2) * 1024
                    pcol = AB_PROBS + (j % 3) * 1024
                    mlast = None
                    for si, (qs, n) in enumerate(slabs_for(j)):
                        waits = [dq, (VE, mhv), (SE, exp_at_slot.get(j % 2, 0))] + \
                            self.need(("msc", j % 2))
                        mlast = P.op("tensor", lambda t, sb=sbase, jj=j, q0=qs, nn=n, b2=buf:
                                     t.matmul(self.psum[:, sb + q0:sb + q0 + nn],
                                              self.AB[0:DL, AB_KLAT + (jj % 8) * 1024 + (jj // 8) * 128:
                                                      AB_KLAT + (jj % 8) * 1024 + (jj // 8) * 128 + 128],
                                              self.AB[0:DL, AB_QM + b2 * 1024 + q0:AB_QM + b2 * 1024 + q0 + nn],
                                              start=True, stop=True), waits=waits, inc=PE)
                    waits = [(PE, mlast), (PE, pv_at_probs.get(pcol, 0))]
                    se_c = P.op("scalar", lambda s2, sb=sbase, pc=pcol, c_=buf, q0=qs0, nn=nact:
                                s2.activation(out=self.AB[:, pc + q0:pc + q0 + nn],
                                              in_=self.psum[:, sb + q0:sb + q0 + nn],
                                              func=AF_.Exp, bias=self.mh[:, c_:c_ + 1], scale=SCALE),
                                waits=waits, inc=SE)
                    exp_at_slot[j % 2] = se_c
                    ve_c = P.op("vector", lambda v2, pc=pcol, q0=qs0, jo_=jo: v2.tensor_tensor(
                        out=self.AB[:, pc + q0:pc + q0 + 128], in0=self.AB[:, pc + q0:pc + q0 + 128],
                        in1=self.masks[:, jo_ * 128:(jo_ + 1) * 128], op=ALU.mult),
                        waits=[(SE, se_c)], inc=VE)
                    probs_info[j] = (se_c, ve_c, pcol, qs0)
                    if j >= 1:
                        emit_pv_m(j - 1)
                    if j == NKT - 1:
                        emit_pv_m(j)
                vrl = P.op("vector", lambda e: e.reciprocal(out=self.AF[0:1, AF_MISC:AF_MISC + 1024],
                                                            in_=self.psum[64:65, 0:1024]),
                           waits=[(PE, acc_last)], inc=VE)
                bc = None
                for i2 in range(2):
                    bc = P.op("tensor", lambda t, i_=i2: t.matmul(
                        self.psum[0:DL, 1024 + i_ * 512:1024 + i_ * 512 + 512], self.ones_f1[:, 0:DL],
                        self.AF[0:1, AF_MISC + i_ * 512:AF_MISC + (i_ + 1) * 512],
                        start=True, stop=True), waits=[(VE, vrl)] + (self.need("mrb") if i2 == 0 else []),
                        inc=PE)
                vb = P.op("vector", lambda e: e.tensor_copy(out=self.AF[0:DL, AF_MISC:AF_MISC + 1024],
                                                            in_=self.psum[0:DL, 1024:2048]),
                          waits=[(PE, bc)], inc=VE)
                vy = P.op("vector", lambda e: e.tensor_tensor(
                    out=self.AB[0:DL, AB_YN:AB_YN + 1024], in0=self.psum[0:DL, 0:1024],
                    in1=self.AF[0:DL, AF_MISC:AF_MISC + 1024], op=ALU.mult),
                    waits=[(VE, vb)] + self.need("yn"), inc=VE)
                self.done("macc", VE, vy)
                dv = self.dma("sync", self.AB[0:DL, AB_VEXP:AB_VEXP + 128],
                              self.wbf["vexp"][h * DL:(h + 1) * DL, :],
                              waits=self.need("vexp") + self.need(("wrdy", "vexp")), stream="vex")
                my = P.op("tensor", lambda t: t.matmul(
                    self.psum[:, 1024:1536], self.AB[0:DL, AB_VEXP:AB_VEXP + 128],
                    self.AB[0:DL, AB_YN:AB_YN + 512], start=True, stop=True),
                    waits=[(VE, vy), dv, (VE, vb)], inc=PE)
                my2 = P.op("tensor", lambda t: t.matmul(
                    self.psum[:, 1536:2048], self.AB[0:DL, AB_VEXP:AB_VEXP + 128],
                    self.AB[0:DL, AB_YN + 512:AB_YN + 1024], start=True, stop=True),
                    waits=self.need("mh_m2"), inc=PE)
                self.done("mrb", PE, my2)
                self.done("vexp", PE, my2)
                sy = P.op("scalar", lambda e: e.copy(out=self.AB[:, AB_MYE:AB_MYE + 1024],
                                                     in_=self.psum[:, 1024:2048]),
                          waits=[(PE, my2)] + self.need("ydma"), inc=SE)
                self.done("yn", PE, my2)
                self.done("mh_m2", SE, sy)
                self.done(("msc", 0), SE, sy)
                self.done(("msc", 1), SE, sy)
                dy = self.dma("gpsimd", self.ctx_scr[h * 128:(h + 1) * 128, bb * NQB:(bb + 1) * NQB],
                              self.AB[:, AB_MYE:AB_MYE + 1024], waits=[(SE, sy)], stream="ymy")
                self.done("ydma", *dy)
                self.done(("qm_b", buf), PE, acc_last)
            self.done("mlakv", PE, self.P.counts[PE])
            self.done("vt_ws", PE, self.P.counts[PE])


def build_nc():
    nc = bass.Bass("TRN2", target_bir_lowering=False, debug=False)
    Builder(nc).build()
    return nc


def _host_inputs(inputs, r):
    asc = np.ascontiguousarray
    f32 = np.float32
    d = {}
    x = inputs["L0_attn_x"]
    d["x0"] = asc(x.reshape(2, 64, 128, 2048)[:, r::8].transpose(3, 0, 1, 2).reshape(2048, 2048))
    return d


_SHARED_CACHE = {}


def _host_shared(inputs):
    if _SHARED_CACHE.get("done"):
        return _SHARED_CACHE
    asc = np.ascontiguousarray
    f32 = np.float32
    s = _SHARED_CACHE
    s["ffn_wqT"] = asc(inputs["L0_ffn_wq"].T)
    s["ffn_wkT"] = asc(inputs["L0_ffn_wk"].T)
    s["ffn_wvT"] = asc(inputs["L0_ffn_wv"].T)
    s["ffn_woT"] = asc(inputs["L0_ffn_wo"].T)
    s["ff1T"] = asc(inputs["L0_ffn_w_ff1"].T)
    s["ff2T"] = asc(inputs["L0_ffn_w_ff2"].T)
    s["wdownT"] = asc(inputs["L0_ffn_w_down"].T)
    s["wupT"] = asc(inputs["L0_ffn_w_up"].T)
    s["attn_wq"] = asc(inputs["L0_attn_wq"])
    s["attn_proj"] = asc(inputs["L0_attn_proj_w"])
    s["wkvlat"] = asc(np.concatenate([inputs["L0_attn_wk_lat"], inputs["L0_attn_wv_lat"]], axis=1))
    s["kexpT"] = asc(inputs["L0_attn_k_expand"].transpose(0, 2, 1).reshape(H * DH, DL))
    s["vexp"] = asc(inputs["L0_attn_v_expand"].reshape(H * DL, DH))
    wrT = inputs["L0_ffn_w_router"].T
    s["wrouter_sb"] = asc(wrT.reshape(16, 128, 3).transpose(1, 0, 2).reshape(128, 48))
    for name, key in [("bias_v", "L0_ffn_simple_bias"), ("ln1s_v", "L0_ffn_ln1_scale"),
                      ("ln1b_v", "L0_ffn_ln1_bias"), ("ln2s_v", "L0_ffn_ln2_scale"),
                      ("ln2b_v", "L0_ffn_ln2_bias")]:
        s[name] = asc(inputs[key].reshape(16, 128).T)
    s["ident_f"] = np.eye(128, dtype=f32)
    s["ident_b"] = np.eye(128, dtype=f32).astype(BFNP)
    inv_q = (1.0 / 10000.0 ** (np.arange(0, DH, 2, dtype=f32) / DH)).astype(f32)
    inv_k = (1.0 / 10000.0 ** (np.arange(0, DL, 2, dtype=f32) / DL)).astype(f32)
    tpos = np.arange(T, dtype=f32)
    angq = tpos[:, None] * inv_q[None, :]
    angk = tpos[:, None] * inv_k[None, :]
    s["cos_q"] = np.cos(angq, dtype=f32)
    s["sin_q"] = np.sin(angq, dtype=f32)
    s["cos_k"] = np.cos(angk, dtype=f32)
    s["sin_k"] = np.sin(angk, dtype=f32)
    s["done"] = True
    return s


def _core_inputs(inputs, s, r):
    asc = np.ascontiguousarray
    f32 = np.float32
    d = {k: s[k] for k in ("ffn_wqT", "ffn_wkT", "ffn_wvT", "ffn_woT", "ff1T", "ff2T",
                           "wdownT", "wupT", "attn_wq", "attn_proj", "wkvlat", "kexpT",
                           "vexp", "wrouter_sb", "bias_v", "ln1s_v", "ln1b_v", "ln2s_v",
                           "ln2b_v", "ident_f", "ident_b")}
    x = inputs["L0_attn_x"]
    d["x0"] = asc(x.reshape(2, 64, 128, 2048)[:, r::8].transpose(3, 0, 1, 2).reshape(2048, 2048))
    kk = np.arange(128)[:, None]
    qq = np.arange(128)[None, :]
    m = np.zeros((8, 128, 128), f32)
    for jo in range(8):
        m[jo] = (qq + (r - jo) * 128 >= kk)
    d["masks_i"] = m.astype(BFNP)
    mm, ii = np.meshgrid(np.arange(8), np.arange(128), indexing="ij")
    pos1 = ((8 * mm + r) * 128 + ii).reshape(-1)
    pos = np.concatenate([pos1, pos1])
    d["rq_cos"] = asc(np.tile(s["cos_q"][pos], (1, 4)))
    d["rq_sin"] = asc(np.tile(s["sin_q"][pos], (1, 4)))
    d["rk_cos"] = asc(s["cos_k"][pos])
    d["rk_sin"] = asc(s["sin_k"][pos])
    return d


def kernel(**inputs):
    s = dict()
    _SHARED_CACHE.clear()
    sh = _host_shared(inputs)
    in_maps = [_core_inputs(inputs, sh, r) for r in range(NCORE)]
    nc = build_nc()
    res = run_bass_kernel_spmd(nc, in_maps, core_ids=list(range(NCORE)))
    y = np.empty((2, 8192, 2048), np.float32)
    yv = y.reshape(2, 64, 128, 2048)
    for r in range(NCORE):
        o = res.results[r]["out"]
        yv[:, r::8] = o.reshape(2048, 2, 8, 128).transpose(1, 2, 3, 0)
    if DEBUG:
        kernel.last_results = res
    return y
